# version 04
# baseline (speedup 1.0000x reference)
import contextlib
import numpy as np
import concourse.bass as bass
import concourse.mybir as mybir
from concourse.bass_utils import run_bass_kernel_spmd

F32 = mybir.dt.float32
AF = mybir.ActivationFunctionType
OP = mybir.AluOpType
AXL = mybir.AxisListType

NCORES = 8
BATCH, SEQ, D = 32, 2048, 1024
BPC = BATCH // NCORES
RTOK = BPC * SEQ
TT = 256
NTILE = RTOK // TT
Q = 128
CPT = TT // Q
NH, HD = 8, 64
DI = NH * HD
DZ = 2 * D
EPS = 1e-5
MASKNEG = -30000.0
CHPS = SEQ // Q

C1024, CEPS, CNEG1, CZERO, CONE, C1E6, CTH, CNTH, C30, CNEGH = 0, 1, 2, 3, 4, 5, 6, 7, 8, 9


class Prog:
    ENGINES = ("tensor", "vector", "scalar", "gpsimd", "sync")

    def __init__(self):
        self.ops = {e: [] for e in self.ENGINES}
        self.count = {}
        self.hw = {e: {} for e in self.ENGINES}

    def sem(self, name):
        self.count.setdefault(name, 0)

    def op(self, eng, fn, waits=(), inc=None, inc_n=1):
        ws = []
        for s, c in waits:
            if c is None or c <= 0:
                continue
            assert c <= self.count[s], f"wait {s}>={c}, recorded {self.count[s]}"
            if self.hw[eng].get(s, -1) < c:
                ws.append((s, c))
                self.hw[eng][s] = c
        if inc is not None:
            self.count[inc] += inc_n
        self.ops[eng].append((fn, ws, inc, inc_n))
        return self.count[inc] if inc is not None else None

    def emit(self, eng_name, eng, sems):
        for fn, ws, inc, inc_n in self.ops[eng_name]:
            for s, c in ws:
                eng.wait_ge(sems[s], c)
            instr = fn(eng)
            if inc is not None:
                assert instr is not None, f"inc={inc} op returned None on {eng_name}"
                instr.then_inc(sems[inc], inc_n)


class Banks:
    def __init__(self):
        self.pending = {b: [] for b in range(8)}

    def writer_waits(self, banks):
        ws = []
        for b in banks:
            ws += self.pending[b]
            self.pending[b] = []
        return ws

    def consumed(self, banks, sem, count):
        for b in banks:
            self.pending[b].append((sem, count))


def rep(ap, nrep, inner):
    return bass.AP(ap.tensor, ap.offset, [ap.ap[0], [0, nrep], [ap.ap[-1][0], inner]])


def col_rep(ap, ncols, inner):
    return bass.AP(ap.tensor, ap.offset, [ap.ap[0], [ap.ap[-1][0], ncols], [0, inner]])


def build_program(n_tiles=NTILE, tap_names=()):
    nc = bass.Bass("TRN2", target_bir_lowering=False, detect_race_conditions=False)

    def dram_in(name, shape, dt=F32):
        return nc.dram_tensor(name, list(shape), dt, kind="ExternalInput")

    d_x = dram_in("x", [D, RTOK])
    d_wzs = dram_in("wzs", [16 * 8 * 128, 128])
    d_win = dram_in("win", [D, 648])
    d_wzg = dram_in("wzg", [DZ, DI])
    d_wout = dram_in("wout", [DI, D])
    d_cw1 = dram_in("cw1", [D, 256])
    d_cw2 = dram_in("cw2", [256, 2])
    d_w1a = dram_in("w1a", [64, 16])
    d_w1b = dram_in("w1b", [64, 16])
    d_b1t = dram_in("b1t", [64, 16])
    d_w2t = dram_in("w2t", [64, 16])
    d_b2t = dram_in("b2t", [64, 1])
    d_dirs = dram_in("dirs", [64, D])
    d_sigs = dram_in("sigs", [64, D])
    d_csel = dram_in("csel", [64, 8])
    d_dtb = dram_in("dtb", [8, 1])
    d_alog = dram_in("alog", [8, 1])
    d_drep = dram_in("drep", [128, 4])
    d_gamc = dram_in("gamc", [128, 8])
    d_betc = dram_in("betc", [128, 8])
    d_cb1 = dram_in("cb1", [128, 2])
    d_cb2 = dram_in("cb2", [2, 1])
    d_temp = dram_in("temp", [1, 1])
    d_oscale = dram_in("oscale", [1, 1])
    d_ident = dram_in("ident", [128, 128])
    d_rev8 = dram_in("rev8", [128, 8])
    d_u01 = dram_in("u01", [128, 128])
    d_mbias = dram_in("mbias", [128, 128])
    d_onesr = dram_in("onesr", [1, 256])
    d_ones8r = dram_in("ones8r", [8, 128])
    d_onespad = dram_in("onespad", [128, 128])
    d_ones16 = dram_in("ones16", [16, 1])
    d_cc = dram_in("cc", [128, 10])
    d_hmE = dram_in("hmE", [128, 8])
    d_hmO = dram_in("hmO", [128, 8])
    d_hsel = dram_in("hsel", [8, 1024])
    d_out = nc.dram_tensor("out", [D, RTOK], F32, kind="ExternalOutput")
    d_tap = {t: nc.dram_tensor(f"tap_{t}", [128, 2048], F32, kind="ExternalOutput")
             for t in tap_names}

    P = Prog()
    for s in ("ldw", "ldx", "lz", "mm", "vv", "ss", "gg", "st", "xf"):
        P.sem(s)
    BK = Banks()

    es = contextlib.ExitStack()

    def sbuf(name, shape, dt=F32):
        return es.enter_context(nc.sbuf_tensor(name, shape, dt))

    s_wzb = sbuf("s_wzb", [128, 3072])
    s_win = sbuf("s_win", [128, 8 * 648])
    s_wzg = sbuf("s_wzg", [128, 16 * DI])
    s_wout = sbuf("s_wout", [128, 4 * D])
    s_cw1 = sbuf("s_cw1", [128, 8 * 256])
    s_cw2 = sbuf("s_cw2", [128, 4])
    s_w1a = sbuf("s_w1a", [64, 16])
    s_w1b = sbuf("s_w1b", [64, 16])
    s_b1t = sbuf("s_b1t", [64, 16])
    s_w2t = sbuf("s_w2t", [64, 16])
    s_b2t = sbuf("s_b2t", [64, 1])
    s_dirss = sbuf("s_dirss", [64, D])
    s_csel = sbuf("s_csel", [64, 8])
    s_qT = sbuf("s_qT", [128, 8 * 64])
    s_csT = sbuf("s_csT", [128, 8 * 8])
    s_dtb = sbuf("s_dtb", [8, 1])
    s_alog = sbuf("s_alog", [8, 1])
    s_negA = sbuf("s_negA", [8, 1])
    s_eA = sbuf("s_eA", [8, 1])
    s_drep = sbuf("s_drep", [128, 4])
    s_gamc = sbuf("s_gamc", [128, 8])
    s_betc = sbuf("s_betc", [128, 8])
    s_cb1 = sbuf("s_cb1", [128, 2])
    s_cb2 = sbuf("s_cb2", [2, 1])
    s_temp = sbuf("s_temp", [1, 1])
    s_it1 = sbuf("s_it1", [1, 1])
    s_it2 = sbuf("s_it2", [1, 1])
    s_itcn = sbuf("s_itcn", [128, 1])
    s_osc = sbuf("s_osc", [64, 1])
    s_ident = sbuf("s_ident", [128, 128])
    s_rev8 = sbuf("s_rev8", [128, 8])
    s_u01 = sbuf("s_u01", [128, 128])
    s_mbias = sbuf("s_mbias", [128, 128])
    s_onesr = sbuf("s_onesr", [1, 256])
    s_ones8r = sbuf("s_ones8r", [8, 128])
    s_onespad = sbuf("s_onespad", [128, 128])
    s_ones16 = sbuf("s_ones16", [16, 1])
    s_cc = sbuf("s_cc", [128, 10])
    s_hmE = sbuf("s_hmE", [128, 8])
    s_hmO = sbuf("s_hmO", [128, 8])
    s_hsel = sbuf("s_hsel", [8, 1024])

    s_xa = sbuf("s_xa", [128, 2048])
    s_x2 = sbuf("s_x2", [128, 2048])
    s_h = sbuf("s_h", [128, 2048])
    s_big = sbuf("s_big", [128, 4096])
    s_XT = sbuf("s_XT", [128, 1024])
    s_Bf = sbuf("s_Bf", [64, 256])
    s_Cf = sbuf("s_Cf", [64, 256])
    s_dtf = sbuf("s_dtf", [8, 256])
    s_sp1 = sbuf("s_sp1", [8, 256])
    s_sp2 = sbuf("s_sp2", [8, 256])
    s_sp3 = sbuf("s_sp3", [8, 256])
    s_del = sbuf("s_del", [8, 256])
    s_acl = sbuf("s_acl", [8, 256])
    s_clc = sbuf("s_clc", [8, 256])
    s_clT = sbuf("s_clT", [128, 8])
    s_dT = sbuf("s_dT", [128, 8])
    s_dTE = sbuf("s_dTE", [128, 8])
    s_dTO = sbuf("s_dTO", [128, 8])
    s_wd = sbuf("s_wd", [128, 8])
    s_w8 = sbuf("s_w8", [8, 128])
    s_w8b = sbuf("s_w8b", [8, 128])
    s_w8T = sbuf("s_w8T", [128, 8])
    s_cer = sbuf("s_cer", [1, 8])
    s_cer2 = sbuf("s_cer2", [1, 8])
    s_Btm = sbuf("s_Btm", [128, 64])
    s_XdE = sbuf("s_XdE", [128, 512])
    s_XdO = sbuf("s_XdO", [128, 512])
    s_Wx = sbuf("s_Wx", [128, 512])
    s_Epre = sbuf("s_Epre", [128, 512])
    s_Ehat = sbuf("s_Ehat", [128, 512])
    s_G = sbuf("s_G", [128, 1024])
    s_SU = sbuf("s_SU", [128, 128])
    s_Cexp = sbuf("s_Cexp", [64, 1024])
    s_SP = sbuf("s_SP", [64, 512])
    s_SP2 = sbuf("s_SP2", [64, 512])
    s_HPE = sbuf("s_HPE", [64, 512])
    s_HPO = sbuf("s_HPO", [64, 512])
    s_yT = sbuf("s_yT", [128, 1024])
    s_sq = sbuf("s_sq", [128, 512])
    s_srow = sbuf("s_srow", [1, 512])
    s_mb = sbuf("s_mb", [128, 256])
    s_rb = sbuf("s_rb", [128, 256])
    s_vb = sbuf("s_vb", [128, 256])
    s_mrb = sbuf("s_mrb", [128, 256])
    s_wfin = sbuf("s_wfin", [128, 256])

    A_sigs = lambda: s_big[0:64, 0:1024]
    A_q = lambda: s_big[0:64, 1024:2048]
    A_cs = lambda: s_big[0:8, 2048:3072]
    A_dirs = lambda: s_big[0:64, 3072:4096]
    A_u = lambda: s_big[:, 0:1024]
    A_zc = lambda: s_big[:, 1024:1536]
    A_hc = lambda: s_big[:, 1536:2048]
    A_M0 = lambda: s_big[0:64, 2048:2304]
    A_M1 = lambda: s_big[0:64, 2304:2560]
    A_magr = lambda: s_big[0:1, 2048:2304]
    A_sct = lambda: s_big[:, 2560:2720]
    A_gn = lambda: s_big[:, 2720:2880]
    A_ga = lambda: s_big[:, 2880:3008]
    A_gb = lambda: s_big[:, 3008:3072]
    A_gm = lambda: s_big[:, 3072:3080]
    A_selg = lambda: s_big[:, 3080:3088]
    A_ohc = lambda: s_big[:, 3088:3096]
    A_ohl = lambda: s_big[:, 3096:3104]
    A_gt = lambda: s_big[:, 3104:3112]
    A_oht = lambda: s_big[:, 3112:3176]
    A_ohf = lambda: s_big[0:64, 3176:3432]
    A_soh = lambda: s_big[0:64, 3432:3688]
    A_comp = lambda: s_big[0:2, 3688:3944]
    A_hh = lambda: s_big[0:16, 2560:2816]
    A_t1 = lambda: s_big[0:16, 2816:3072]
    A_tmp3 = lambda: s_big
    A_zga = lambda: s_big[:, 0:1024]
    A_zgb = lambda: s_big[0:128, 2048:3072]
    A_u2 = lambda: s_big[:, 1024:2048]

    ps = es.enter_context(nc.psum_tensor("ps", [128, 4096], mybir.dt.float32))

    def PB(bank, off=0, w=512, rows=128, r0=0):
        return ps[r0:r0 + rows, bank * 512 + off: bank * 512 + off + w]

    cc = lambda i, p=128: s_cc[0:p, i:i + 1]

    nw = [0]

    def wload(fn):
        nw[0] += 1
        P.op("sync", fn, inc="ldw", inc_n=16)

    for k in range(8):
        wload(lambda sy, k=k: sy.dma_start(out=s_win[:, k * 648:(k + 1) * 648],
                                           in_=d_win[k * 128:(k + 1) * 128, :]))
    for k in range(16):
        wload(lambda sy, k=k: sy.dma_start(out=s_wzg[:, k * DI:(k + 1) * DI],
                                           in_=d_wzg[k * 128:(k + 1) * 128, :]))
    for k in range(4):
        wload(lambda sy, k=k: sy.dma_start(out=s_wout[:, k * D:(k + 1) * D],
                                           in_=d_wout[k * 128:(k + 1) * 128, :]))
    for k in range(8):
        wload(lambda sy, k=k: sy.dma_start(out=s_cw1[:, k * 256:(k + 1) * 256],
                                           in_=d_cw1[k * 128:(k + 1) * 128, :]))
    for k in range(2):
        wload(lambda sy, k=k: sy.dma_start(out=s_cw2[:, k * 2:(k + 1) * 2],
                                           in_=d_cw2[k * 128:(k + 1) * 128, :]))
    wload(lambda sy: sy.dma_start(out=A_sigs(), in_=d_sigs[:, :]))
    wload(lambda sy: sy.dma_start(out=A_dirs(), in_=d_dirs[:, :]))
    for a, b in [(s_w1a, d_w1a), (s_w1b, d_w1b), (s_b1t, d_b1t), (s_w2t, d_w2t),
                 (s_b2t, d_b2t), (s_csel, d_csel), (s_dtb, d_dtb), (s_alog, d_alog),
                 (s_drep, d_drep), (s_gamc, d_gamc), (s_betc, d_betc), (s_cb1, d_cb1),
                 (s_cb2, d_cb2), (s_temp, d_temp), (s_ident, d_ident), (s_rev8, d_rev8),
                 (s_u01, d_u01), (s_mbias, d_mbias), (s_onesr, d_onesr),
                 (s_ones8r, d_ones8r), (s_onespad, d_onespad), (s_ones16, d_ones16),
                 (s_cc, d_cc), (s_hmE, d_hmE), (s_hmO, d_hmO), (s_hsel, d_hsel)]:
        wload(lambda sy, a=a, b=b: sy.dma_start(out=a[:], in_=b[:]))
    wload(lambda sy: sy.dma_start(out=s_osc[:, :],
                                  in_=bass.AP(d_oscale, 0, [[0, 64], [1, 1]])))
    LDW = nw[0] * 16
    WL = [("ldw", LDW)]

    P.op("vector", lambda v: v.tensor_scalar(s_G[0:64, 0:1024], A_sigs(), cc(CTH, 64),
                                             None, OP.is_gt), waits=WL, inc="vv")
    P.op("vector", lambda v: v.tensor_scalar(s_yT[0:64, 0:1024], A_sigs(), cc(CNTH, 64),
                                             None, OP.is_lt), inc="vv")
    vq = P.op("vector", lambda v: v.tensor_tensor(A_q(), s_G[0:64, 0:1024],
                                                  s_yT[0:64, 0:1024], OP.subtract), inc="vv")
    for half in range(2):
        m = P.op("tensor", lambda t, h=half: t.matmul(PB(0, 0, 512, 8), s_csel[:, :],
                                                      A_q()[:, h * 512:(h + 1) * 512],
                                                      start=True, stop=True),
                 waits=[("vv", vq)] + BK.writer_waits([0]), inc="mm")
        P.op("vector", lambda v: v.tensor_scalar(s_sq[0:8, 0:512], PB(0, 0, 512, 8),
                                                 cc(CZERO, 8), None, OP.is_gt),
             waits=[("mm", m)], inc="vv")
        P.op("vector", lambda v: v.tensor_scalar(s_yT[0:8, 0:512], PB(0, 0, 512, 8),
                                                 cc(CZERO, 8), None, OP.is_lt),
             inc="vv")
        e = P.op("vector", lambda v, h=half: v.tensor_tensor(
            A_cs()[:, h * 512:(h + 1) * 512], s_sq[0:8, 0:512], s_yT[0:8, 0:512],
            OP.subtract), inc="vv")
        BK.consumed([0], "vv", e)
    for c in range(8):
        m = P.op("tensor", lambda t, c=c: t.transpose(PB(1, 0, 64),
                                                      A_q()[:, c * 128:(c + 1) * 128],
                                                      s_ident[0:64, 0:64]),
                 waits=BK.writer_waits([1]), inc="mm")
        v = P.op("vector", lambda v, c=c: v.tensor_copy(s_qT[:, c * 64:(c + 1) * 64],
                                                        PB(1, 0, 64)),
                 waits=[("mm", m)], inc="vv")
        BK.consumed([1], "vv", v)
        m = P.op("tensor", lambda t, c=c: t.transpose(PB(1, 64, 8),
                                                      A_cs()[:, c * 128:(c + 1) * 128],
                                                      s_ident[0:8, 0:8]),
                 waits=BK.writer_waits([1]), inc="mm")
        v = P.op("vector", lambda v, c=c: v.tensor_copy(s_csT[:, c * 8:(c + 1) * 8],
                                                        PB(1, 64, 8)),
                 waits=[("mm", m)], inc="vv")
        BK.consumed([1], "vv", v)
    e = P.op("scalar", lambda s: s.activation(s_eA[:, :], s_alog[:, :], AF.Exp),
             waits=WL, inc="ss")
    P.op("vector", lambda v: v.tensor_scalar(s_negA[:, :], s_eA[:, :], cc(CNEG1, 8), None,
                                             OP.mult), waits=[("ss", e)], inc="vv")
    P.op("vector", lambda v: v.tensor_scalar(s_it1[:, :], s_temp[:, :], cc(C1E6, 1), None,
                                             OP.add), waits=WL, inc="vv")
    vi = P.op("vector", lambda v: v.reciprocal(s_it2[:, :], s_it1[:, :]), inc="vv")
    m = P.op("tensor", lambda t: t.matmul(PB(1, 128, 1), s_onesr[0:1, 0:128], s_it2[:, :],
                                          start=True, stop=True),
             waits=[("vv", vi)] + BK.writer_waits([1]), inc="mm")
    v = P.op("vector", lambda v: v.tensor_scalar(s_itcn[:, :], PB(1, 128, 1), cc(CNEG1),
                                                 None, OP.mult), waits=[("mm", m)], inc="vv")
    BK.consumed([1], "vv", v)
    P.op("vector", lambda v: v.tensor_scalar(s_dirss[:, :], A_dirs(), s_osc[:, :],
                                             None, OP.mult), waits=WL, inc="vv")

    def ln_stats(src, wv=None):
        w01 = BK.writer_waits([0, 1])
        last = None
        f32r = mybir.dt.float32r
        for c in range(8):
            sqw = [("mm", sq_slot[c % 2])] if sq_slot[c % 2] else []
            if c < 2 and wv is not None:
                sqw.append(("vv", wv))
            sc = P.op("scalar", lambda s, c=c: s.activation(
                s_sq[:, (c % 2) * 256:(c % 2) * 256 + 256], src[:, c * 256:(c + 1) * 256],
                AF.Square), waits=sqw, inc="ss")
            P.op("tensor", lambda t, c=c: t.matmul(PB(0, 0, 256), s_onespad[:, :],
                                                   src[:, c * 256:(c + 1) * 256],
                                                   start=(c == 0), stop=(c == 7)),
                 waits=(w01 + ([("vv", wv)] if wv is not None else [])) if c == 0 else [],
                 inc=None)
            last = P.op("tensor", lambda t, c=c: t.matmul(
                PB(1, 0, 256), s_onespad[:, :],
                s_sq[:, (c % 2) * 256:(c % 2) * 256 + 256],
                start=(c == 0), stop=(c == 7)), waits=[("ss", sc)], inc="mm")
            sq_slot[c % 2] = last
        vr = P.op("vector", lambda v: v.tensor_copy(s_srow[:, 0:256], PB(0, 0, 256, 1)),
                  waits=[("mm", last)], inc="vv")
        vr2 = P.op("vector", lambda v: v.tensor_copy(s_srow[:, 256:512], PB(1, 0, 256, 1)),
                   inc="vv")
        BK.consumed([0], "vv", vr)
        BK.consumed([1], "vv", vr2)
        mb = P.op("tensor", lambda t: t.matmul(PB(0, 0, 512), s_onesr[0:1, 0:128],
                                               s_srow[:, :], start=True, stop=True),
                  waits=[("vv", vr2)] + BK.writer_waits([0]), inc="mm")
        P.op("vector", lambda v: v.tensor_scalar(s_mb[:, :], PB(0, 0, 256), cc(C1024),
                                                 None, OP.mult), waits=[("mm", mb)], inc="vv")
        P.op("vector", lambda v: v.tensor_tensor(s_vb[:, :], s_mb[:, :], s_mb[:, :],
                                                 OP.mult), inc="vv")
        vv = P.op("vector", lambda v: v.scalar_tensor_tensor(s_rb[:, :], PB(0, 256, 256),
                                                             cc(C1024), s_vb[:, :],
                                                             OP.mult, OP.subtract), inc="vv")
        BK.consumed([0], "vv", vv)
        sv = P.op("scalar", lambda s: s.activation(s_vb[:, :], s_rb[:, :], AF.Ln,
                                                   bias=cc(CEPS)), waits=[("vv", vv)],
                  inc="ss")
        sv2 = P.op("scalar", lambda s: s.activation(s_rb[:, :], s_vb[:, :], AF.Exp,
                                                    scale=cc(CNEGH)), inc="ss")
        return P.op("vector", lambda v: v.nop(), waits=[("ss", sv2)], inc="vv")

    def ln_apply(src, dst, tmp, extra=()):
        P.op("vector", lambda v: v.tensor_tensor(
            tmp[:, 0:2048].rearrange("p (c f) -> p c f", c=8),
            src[:, :].rearrange("p (c f) -> p c f", c=8),
            rep(s_mb[:, :], 8, 256), OP.subtract),
            waits=list(extra), inc="vv")
        return P.op("vector", lambda v: v.tensor_tensor(
            dst[:, :].rearrange("p (c f) -> p c f", c=8),
            tmp[:, 0:2048].rearrange("p (c f) -> p c f", c=8),
            rep(s_rb[:, :], 8, 256), OP.mult), inc="vv")

    taps_set = set(tap_names)
    TAP_TILE = 0

    def tap(name, ap_fn, tile):
        if name not in taps_set or tile != TAP_TILE:
            return
        src = ap_fn()
        rr, cc_ = src.shape[0], int(np.prod(src.shape[1:]))
        P.op("gpsimd", lambda gp, name=name, src=src, rr=rr, cc_=cc_: gp.dma_start(
            out=d_tap[name][0:rr, 0:cc_], in_=src),
            waits=[("vv", P.count["vv"]), ("ss", P.count["ss"]), ("mm", P.count["mm"])],
            inc="gg", inc_n=16)
        P.op("vector", lambda v: None, waits=[("gg", P.count["gg"])])
        P.op("scalar", lambda s: None, waits=[("gg", P.count["gg"])])
        P.op("tensor", lambda t_: None, waits=[("gg", P.count["gg"])])

    sq_slot = [None, None]
    st_count = 0
    last_sce = [0]
    z_ring = {}
    for t in range(n_tiles):
        P.op("sync", lambda sy, t=t: sy.dma_start(
            out=s_xa[:, :].rearrange("p (c f) -> p c f", c=8),
            in_=bass.AP(d_x, t * TT, [[RTOK, 128], [128 * RTOK, 8], [1, TT]])),
            waits=([("xf", t)] if t >= 1 else []), inc="ldx", inc_n=16)
        XL = [("ldx", 16 * (t + 1))]
        P.op("vector", lambda v: None, waits=XL)
        P.op("scalar", lambda s: None, waits=XL)
        P.op("tensor", lambda t_: None, waits=XL)

        ln_stats(s_xa)
        vh1 = ln_apply(s_xa, s_h, s_x2, extra=[("st", st_count)])
        tap("h1", lambda: s_h[:, :], t)

        xe = []
        for fc in range(4):
            bank = 2 + (fc % 2)
            for k in range(8):
                P.op("tensor", lambda t_, fc=fc, k=k, bank=bank: t_.matmul(
                    PB(bank, 0, 256), s_win[:, k * 648 + fc * 128: k * 648 + fc * 128 + 128],
                    s_h[:, k * 256:(k + 1) * 256], start=(k == 0), stop=(k == 7)),
                    waits=([("vv", vh1)] + BK.writer_waits([bank])) if k == 0 else [],
                    inc="mm" if k == 7 else None)
            mmx = P.count["mm"]
            v = P.op("vector", lambda v, fc=fc, bank=bank: v.tensor_copy(
                s_XT[:, fc * 256:(fc + 1) * 256], PB(bank, 0, 256)),
                waits=[("mm", mmx)], inc="vv")
            BK.consumed([bank], "vv", v)
            xe.append(v)
        for c0, w, dst, r in [(512, 64, s_Bf, 64), (576, 64, s_Cf, 64), (640, 8, s_dtf, 8)]:
            for k in range(8):
                P.op("tensor", lambda t_, k=k, c0=c0, w=w, r=r: t_.matmul(
                    PB(4, 0, 256, r), s_win[:, k * 648 + c0: k * 648 + c0 + w],
                    s_h[:, k * 256:(k + 1) * 256], start=(k == 0), stop=(k == 7)),
                    waits=BK.writer_waits([4]) if k == 0 else [], inc="mm" if k == 7 else None)
            mmb = P.count["mm"]
            v = P.op("vector", lambda v, dst=dst, r=r: v.tensor_copy(dst[:, :],
                                                                     PB(4, 0, 256, r)),
                     waits=[("mm", mmb)], inc="vv")
            BK.consumed([4], "vv", v)
        vbc = P.count["vv"]

        P.op("scalar", lambda s: s.activation(s_sp1[:, :], s_dtf[:, :], AF.Relu,
                                              bias=s_dtb[:, :]), waits=[("vv", vbc)],
             inc="ss")
        P.op("scalar", lambda s: s.activation(s_sp2[:, :], s_dtf[:, :], AF.Abs,
                                              bias=s_dtb[:, :]), inc="ss")
        s3 = P.op("scalar", lambda s: s.activation(s_sp3[:, :], s_sp2[:, :], AF.Exp,
                                                   scale=cc(CNEG1, 8)), inc="ss")
        v1 = P.op("vector", lambda v: v.tensor_scalar(s_sp2[:, :], s_sp3[:, :], cc(CONE, 8),
                                                      None, OP.add), waits=[("ss", s3)],
                  inc="vv")
        s4 = P.op("scalar", lambda s: s.activation(s_sp3[:, :], s_sp2[:, :], AF.Ln),
                  waits=[("vv", v1)], inc="ss")
        vdl = P.op("vector", lambda v: v.tensor_tensor(s_del[:, :], s_sp1[:, :], s_sp3[:, :],
                                                       OP.add), waits=[("ss", s4)], inc="vv")
        va = P.op("vector", lambda v: v.tensor_scalar(s_acl[:, :], s_del[:, :], s_negA[:, :],
                                                      None, OP.mult), inc="vv")

        def wz_dma(zf, gz):
            P.op("sync", lambda sy, zf=zf, gz=gz: sy.dma_start(
                out=bass.AP(s_wzb, (gz % 3) * 1024, [[3072, 128], [128, 8], [1, 128]]),
                in_=bass.AP(d_wzs, zf * 1024 * 128, [[128, 128], [128 * 128, 8], [1, 128]])),
                waits=[("mm", z_ring.get(gz - 3, 0))], inc="lz", inc_n=16)

        def scan_chain(c):
            g = t * CPT + c
            first = (g % CHPS == 0)
            co = c * 128
            vcl = P.op("vector", lambda v, co=co: v.tensor_tensor_scan(
                s_clc[:, co:co + 128], s_ones8r[:, :], s_acl[:, co:co + 128], 0.0,
                OP.mult, OP.add), waits=[("vv", va)], inc="vv")
            m = P.op("tensor", lambda t_, co=co: t_.transpose(PB(1, 0, 8),
                                                              s_clc[:, co:co + 128],
                                                              s_ident[0:8, 0:8]),
                     waits=[("vv", vcl)] + BK.writer_waits([1]), inc="mm")
            vclT = P.op("vector", lambda v: v.tensor_copy(s_clT[:, :], PB(1, 0, 8)),
                        waits=[("mm", m)], inc="vv")
            BK.consumed([1], "vv", vclT)
            m = P.op("tensor", lambda t_, co=co: t_.transpose(PB(1, 8, 8),
                                                              s_del[:, co:co + 128],
                                                              s_ident[0:8, 0:8]),
                     waits=BK.writer_waits([1]) + [("vv", vdl)], inc="mm")
            vdT = P.op("vector", lambda v: v.tensor_copy(s_dT[:, :], PB(1, 8, 8)),
                       waits=[("mm", m)], inc="vv")
            BK.consumed([1], "vv", vdT)
            vw = P.op("vector", lambda v, co=co: v.tensor_scalar(
                s_w8[:, :], s_clc[:, co:co + 128], s_clc[:, co + 127:co + 128], None,
                OP.subtract), waits=[("vv", vcl)], inc="vv")
            sw = P.op("scalar", lambda s: s.activation(s_w8b[:, :], s_w8[:, :], AF.Exp,
                                                       scale=cc(CNEG1, 8)),
                      waits=[("vv", vw)], inc="ss")
            m = P.op("tensor", lambda t_: t_.transpose(PB(1, 16, 8), s_w8b[:, :],
                                                       s_ident[0:8, 0:8]),
                     waits=[("ss", sw)] + BK.writer_waits([1]), inc="mm")
            vw8 = P.op("vector", lambda v: v.tensor_copy(s_w8T[:, :], PB(1, 16, 8)),
                       waits=[("mm", m)], inc="vv")
            BK.consumed([1], "vv", vw8)
            m = P.op("tensor", lambda t_, co=co: t_.transpose(PB(1, 64, 64),
                                                              s_Bf[:, co:co + 128],
                                                              s_ident[0:64, 0:64]),
                     waits=BK.writer_waits([1]), inc="mm")
            vB = P.op("vector", lambda v: v.tensor_copy(s_Btm[:, :], PB(1, 64, 64)),
                      waits=[("mm", m)], inc="vv")
            BK.consumed([1], "vv", vB)
            mS = P.op("tensor", lambda t_, co=co: t_.matmul(
                PB(1, 128, 128), s_Bf[:, co:co + 128], s_Cf[:, co:co + 128],
                start=True, stop=True), waits=BK.writer_waits([1]), inc="mm")
            vSU = P.op("vector", lambda v: v.tensor_tensor(s_SU[:, :], PB(1, 128, 128),
                                                           s_u01[:, :], OP.mult),
                       waits=[("mm", mS)], inc="vv")
            BK.consumed([1], "vv", vSU)
            vde = P.op("vector", lambda v: v.tensor_tensor(s_dTE[:, :], s_dT[:, :],
                                                           s_hmE[:, :], OP.mult),
                       waits=[("vv", vdT)], inc="vv")
            vdo = P.op("vector", lambda v: v.tensor_tensor(s_dTO[:, :], s_dT[:, :],
                                                           s_hmO[:, :], OP.mult), inc="vv")
            vwd = P.op("vector", lambda v: v.tensor_tensor(s_wd[:, :], s_w8T[:, :],
                                                           s_dT[:, :], OP.mult),
                       waits=[("vv", vw8)], inc="vv")
            for fc in range(4):
                P.op("tensor", lambda t_, fc=fc, co=co: t_.transpose(
                    PB(0, fc * 128, 128), s_XT[:, fc * 256 + co: fc * 256 + co + 128],
                    s_ident[:, :]),
                    waits=(BK.writer_waits([0]) + [("vv", xe[3])]) if fc == 0 else [],
                    inc="mm" if fc == 3 else None)
            mX = P.count["mm"]
            vXE = P.op("vector", lambda v: v.tensor_tensor(
                s_XdE[:, :].rearrange("p (h d) -> p h d", h=8), PB(0, 0, 512),
                col_rep(s_dTE[:, :], 8, 64), OP.mult),
                waits=[("mm", mX), ("vv", vde)], inc="vv")
            vXO = P.op("vector", lambda v: v.tensor_tensor(
                s_XdO[:, :].rearrange("p (h d) -> p h d", h=8), PB(0, 0, 512),
                col_rep(s_dTO[:, :], 8, 64), OP.mult), waits=[("vv", vdo)], inc="vv")
            vWx = P.op("vector", lambda v: v.tensor_tensor(
                s_Wx[:, :].rearrange("p (h d) -> p h d", h=8), PB(0, 0, 512),
                col_rep(s_wd[:, :], 8, 64), OP.mult), waits=[("vv", vwd)], inc="vv")
            BK.consumed([0], "vv", vWx)
            mb8 = None
            for h in range(8):
                mb8 = P.op("tensor", lambda t_, h=h, co=co: t_.matmul(
                    PB(2 + h // 4, (h % 4) * 128, 128), s_hsel[:, h * 128:(h + 1) * 128],
                    s_clc[:, co:co + 128], start=True, stop=True),
                    waits=(BK.writer_waits([2, 3]) + [("vv", vcl)]) if h == 0 else [],
                    inc="mm" if h == 7 else None)
            se1 = P.op("scalar", lambda s: s.activation(s_G[0:64, 0:512], PB(2, 0, 512, 64),
                                                        AF.Exp), waits=[("mm", mb8)], inc="ss")
            se2 = P.op("scalar", lambda s: s.activation(s_G[0:64, 512:1024],
                                                        PB(3, 0, 512, 64), AF.Exp), inc="ss")
            vce = P.op("vector", lambda v, co=co: v.tensor_tensor(
                s_Cexp[:, :].rearrange("p (h d) -> p h d", h=8),
                rep(s_Cf[:, co:co + 128], 8, 128),
                s_G[0:64, :].rearrange("p (h d) -> p h d", h=8), OP.mult),
                waits=[("ss", se2)], inc="vv")
            vGs = []
            for gi in range(2):
                for hh in range(4):
                    h = gi * 4 + hh
                    P.op("vector", lambda v, h=h, hh=hh, gi=gi: v.scalar_tensor_tensor(
                        s_Epre[:, hh * 128:(hh + 1) * 128], PB(2 + gi, hh * 128, 128),
                        s_clT[:, h:h + 1], s_mbias[:, :], OP.subtract, OP.add),
                        waits=[("vv", vclT), ("mm", mb8)] if (gi == 0 and hh == 0) else [],
                        inc="vv")
                sE = P.op("scalar", lambda s: s.activation(s_Ehat[:, :], s_Epre[:, :],
                                                           AF.Exp),
                          waits=[("vv", P.count["vv"])], inc="ss")
                vG = P.op("vector", lambda v, gi=gi: v.tensor_tensor(
                    s_G[:, gi * 512:(gi + 1) * 512].rearrange("p (h d) -> p h d", h=4),
                    s_Ehat[:, :].rearrange("p (h d) -> p h d", h=4),
                    rep(s_SU[:, :], 4, 128), OP.mult),
                    waits=[("ss", sE), ("vv", vSU)], inc="vv")
                vGs.append(vG)
            ve2 = P.count["vv"]
            BK.consumed([2], "vv", ve2)
            BK.consumed([3], "vv", ve2)
            BK.consumed([2], "ss", se1)
            BK.consumed([3], "ss", se2)
            vpo = None
            if not first:
                P.op("vector", lambda v: v.tensor_tensor(
                    s_HPE[:, :].rearrange("p (h d) -> p h d", h=8),
                    s_SP[:, :].rearrange("p (h d) -> p h d", h=8),
                    col_rep(s_hmE[0:64, :], 8, 64), OP.mult), inc="vv")
                vpo = P.op("vector", lambda v: v.tensor_tensor(
                    s_HPO[:, :].rearrange("p (h d) -> p h d", h=8),
                    s_SP[:, :].rearrange("p (h d) -> p h d", h=8),
                    col_rep(s_hmO[0:64, :], 8, 64), OP.mult), inc="vv")
                P.op("gpsimd", lambda gp: gp.dma_start(out=s_cer[:, :],
                                                       in_=s_clT[127:128, :]),
                     waits=[("vv", vclT), ("ss", last_sce[0])], inc="gg", inc_n=16)
                sce = P.op("scalar", lambda s: s.activation(s_cer2[:, :], s_cer[:, :],
                                                            AF.Exp),
                           waits=[("gg", P.count["gg"])], inc="ss")
                last_sce[0] = sce
                mEH = P.op("tensor", lambda t_: t_.matmul(PB(1, 256, 8, 64),
                                                          s_onesr[0:1, 0:64], s_cer2[:, :],
                                                          start=True, stop=True),
                           waits=[("ss", sce)] + BK.writer_waits([1]), inc="mm")
                vsp = P.op("vector", lambda v: v.tensor_tensor(
                    s_SP2[:, :].rearrange("p (h d) -> p h d", h=8),
                    s_SP[:, :].rearrange("p (h d) -> p h d", h=8),
                    col_rep(PB(1, 256, 8, 64), 8, 64), OP.mult),
                    waits=[("mm", mEH), ("vv", vpo)], inc="vv")
                BK.consumed([1], "vv", vsp)
            return dict(first=first, co=co, vXO=vXO, vWx=vWx, vB=vB, vce=vce,
                        vGs=vGs, vpo=vpo)

        def z_half(h0):
            for zf in range(h0, h0 + 8):
                gz = t * 16 + zf
                zb = 6 + zf % 2
                for k in range(8):
                    P.op("tensor", lambda t_, k=k, zb=zb, gz=gz: t_.matmul(
                        PB(zb, 0, 256),
                        s_wzb[:, (gz % 3) * 1024 + k * 128: (gz % 3) * 1024 + k * 128 + 128],
                        s_h[:, k * 256:(k + 1) * 256], start=(k == 0), stop=(k == 7)),
                        waits=(BK.writer_waits([zb]) + [("lz", 16 * (gz + 1))]) if k == 0
                        else [], inc="mm" if k == 7 else None)
                mz = P.count["mm"]
                z_ring[gz] = mz
                if zf + 3 < 16:
                    wz_dma(zf + 3, t * 16 + zf + 3)
                elif t + 1 < n_tiles:
                    nzf = zf + 3 - 16
                    wz_dma(nzf, (t + 1) * 16 + nzf)
                sz = P.op("scalar", lambda s, zf=zf, zb=zb: s.activation(
                    A_zc()[:, (zf % 2) * 256:(zf % 2) * 256 + 256], PB(zb, 0, 256),
                    AF.Silu),
                    waits=[("mm", mz)] + ([("mm", zgp_mm.get(zf - 2, None) or 0)]
                                          if zf >= 2 else []), inc="ss")
                BK.consumed([zb], "ss", sz)
                for j in range(4):
                    bank, off = 4 + j // 2, (j % 2) * 256
                    P.op("tensor", lambda t_, zf=zf, j=j, bank=bank, off=off: t_.matmul(
                        PB(bank, off, 256),
                        s_wzg[:, zf * DI + j * 128: zf * DI + j * 128 + 128],
                        A_zc()[:, (zf % 2) * 256:(zf % 2) * 256 + 256],
                        start=True, stop=True),
                        waits=([("ss", sz)] if j == 0 else []) +
                              (BK.writer_waits([4, 5]) if j == 0 else []),
                        inc="mm" if j == 3 else None)
                zgp = P.count["mm"]
                zgp_mm[zf] = zgp
                src, dst = (A_zga, A_zgb) if zf % 2 == 1 else (A_zgb, A_zga)
                if zf == 0:
                    vz = P.op("vector", lambda v: v.tensor_copy(
                        A_zga(), ps[:, 4 * 512: 4 * 512 + 1024]),
                        waits=[("mm", zgp)], inc="vv")
                else:
                    vz = P.op("vector", lambda v, src=src, dst=dst: v.tensor_tensor(
                        dst(), src(), ps[:, 4 * 512: 4 * 512 + 1024], OP.add),
                        waits=[("mm", zgp)], inc="vv")
                BK.consumed([4, 5], "vv", vz)
            return P.count["vv"]

        def y_state(ch):
            first, co = ch["first"], ch["co"]
            for fc in range(4):
                hE, hO = 2 * fc, 2 * fc + 1
                w0 = (BK.writer_waits([4]) + [("vv", ch["vXO"]), ("vv", ch["vGs"][1]),
                                              ("vv", ch["vce"])]
                      + ([("vv", ch["vpo"])] if not first else [])) if fc == 0 else []
                P.op("tensor", lambda t_, fc=fc, hE=hE: t_.matmul(
                    PB(4, fc * 128, 128), s_XdE[:, fc * 128:(fc + 1) * 128],
                    s_G[:, hE * 128:(hE + 1) * 128], start=True, stop=False),
                    waits=w0, inc=None)
                P.op("tensor", lambda t_, fc=fc, hO=hO, first=first: t_.matmul(
                    PB(4, fc * 128, 128), s_XdO[:, fc * 128:(fc + 1) * 128],
                    s_G[:, hO * 128:(hO + 1) * 128], start=False, stop=first),
                    inc="mm" if (first and fc == 3) else None)
                if not first:
                    P.op("tensor", lambda t_, fc=fc, hE=hE: t_.matmul(
                        PB(4, fc * 128, 128), s_HPE[:, fc * 128:(fc + 1) * 128],
                        s_Cexp[:, hE * 128:(hE + 1) * 128], start=False, stop=False),
                        inc=None)
                    P.op("tensor", lambda t_, fc=fc, hO=hO: t_.matmul(
                        PB(4, fc * 128, 128), s_HPO[:, fc * 128:(fc + 1) * 128],
                        s_Cexp[:, hO * 128:(hO + 1) * 128], start=False, stop=True),
                        inc="mm" if fc == 3 else None)
            ym = P.count["mm"]
            for fc in range(4):
                P.op("tensor", lambda t_, fc=fc: t_.matmul(
                    PB(5, fc * 128, 128, 64), s_Btm[:, :], s_Wx[:, fc * 128:(fc + 1) * 128],
                    start=True, stop=True),
                    waits=(BK.writer_waits([5]) + [("vv", ch["vWx"]), ("vv", ch["vB"])])
                    if fc == 0 else [], inc="mm" if fc == 3 else None)
            mst = P.count["mm"]
            if not first:
                vH = P.op("vector", lambda v: v.tensor_tensor(s_SP[:, :], s_SP2[:, :],
                                                              PB(5, 0, 512, 64), OP.add),
                          waits=[("mm", mst)], inc="vv")
            else:
                vH = P.op("vector", lambda v: v.tensor_copy(s_SP[:, :], PB(5, 0, 512, 64)),
                          waits=[("mm", mst)], inc="vv")
            BK.consumed([5], "vv", vH)
            for fc in range(4):
                v = P.op("vector", lambda v, fc=fc, co=co: v.scalar_tensor_tensor(
                    s_yT[:, fc * 256 + co: fc * 256 + co + 128],
                    s_XT[:, fc * 256 + co: fc * 256 + co + 128],
                    s_drep[:, fc:fc + 1], PB(4, fc * 128, 128), OP.mult, OP.add),
                    waits=[("mm", ym)], inc="vv")
                BK.consumed([4], "vv", v)

        if t == 0:
            for i in range(3):
                wz_dma(i, i)
        zgp_mm = {}
        ch0 = scan_chain(0)
        z_half(0)
        y_state(ch0)
        ch1 = scan_chain(1)
        vzg = z_half(8)
        y_state(ch1)
        tap("yT", lambda: bass.AP(s_yT, 0, [[1024, 128], [1, 1024]]), t)

        vu = P.op("vector", lambda v: v.tensor_tensor(A_u2(), A_zgb(), s_yT[:, :],
                                                      OP.mult), inc="vv")
        vx2 = None
        for cC in range(8):
            bank = 4 + (cC % 2)
            for k in range(4):
                P.op("tensor", lambda t_, cC=cC, k=k, bank=bank: t_.matmul(
                    PB(bank, 0, 256), s_wout[:, k * D + cC * 128: k * D + cC * 128 + 128],
                    A_u2()[:, k * 256:(k + 1) * 256], start=(k == 0), stop=(k == 3)),
                    waits=(BK.writer_waits([bank]) + ([("vv", vu)] if cC == 0 else []))
                    if k == 0 else [], inc="mm" if k == 3 else None)
            ma = P.count["mm"]
            vx2 = P.op("vector", lambda v, cC=cC, bank=bank: v.tensor_tensor(
                s_x2[:, cC * 256:(cC + 1) * 256], PB(bank, 0, 256),
                s_xa[:, cC * 256:(cC + 1) * 256], OP.add), waits=[("mm", ma)], inc="vv")
            BK.consumed([bank], "vv", vx2)
        tap("u", lambda: bass.AP(s_big, 0, [[4096, 128], [1, 1024]]), t)
        tap("x2", lambda: s_x2[:, :], t)

        ln_stats(s_x2, wv=vx2)
        vh2 = ln_apply(s_x2, s_h, s_xa)
        tap("h2", lambda: s_h[:, :], t)

        for k in range(8):
            P.op("tensor", lambda t_, k=k: t_.matmul(
                PB(2, 0, 256, 64), s_qT[:, k * 64:(k + 1) * 64],
                s_h[:, k * 256:(k + 1) * 256], start=(k == 0), stop=(k == 7)),
                waits=(BK.writer_waits([2]) + [("vv", vh2)]) if k == 0 else [], inc=None)
            P.op("tensor", lambda t_, k=k: t_.matmul(
                PB(3, 0, 256, 8), s_csT[:, k * 8:(k + 1) * 8],
                s_h[:, k * 256:(k + 1) * 256], start=(k == 0), stop=(k == 7)),
                waits=BK.writer_waits([3]) if k == 0 else [],
                inc="mm" if k == 7 else None)
        msc = P.count["mm"]
        for hf in range(2):
            ho = hf * 128
            so_ = hf * 80
            vts = P.op("vector", lambda v, ho=ho: v.tensor_copy(A_ga()[0:64, 0:128],
                                                                PB(2, ho, 128, 64)),
                       waits=[("mm", msc)], inc="vv")
            vcs3 = P.op("vector", lambda v, ho=ho: v.tensor_copy(s_sq[0:8, 0:128],
                                                                 PB(3, ho, 128, 8)),
                        inc="vv")
            if hf == 1:
                BK.consumed([2], "vv", vcs3)
                BK.consumed([3], "vv", vcs3)
            mt1 = P.op("tensor", lambda t_: t_.transpose(PB(4, 0, 64), A_ga()[0:64, 0:128],
                                                         s_ident[0:64, 0:64]),
                       waits=[("vv", vts)] + BK.writer_waits([4]), inc="mm")
            mt2 = P.op("tensor", lambda t_: t_.transpose(PB(4, 64, 8), s_sq[0:8, 0:128],
                                                         s_ident[0:8, 0:8]),
                       waits=[("vv", vcs3)], inc="mm")
            vev = P.op("vector", lambda v, so_=so_: v.tensor_copy(A_sct()[:, so_:so_ + 72],
                                                                  PB(4, 0, 72)),
                       waits=[("mm", mt2)], inc="vv")
            BK.consumed([4], "vv", vev)
            vcl2 = P.op("vector", lambda v, so_=so_: v.tensor_scalar(
                A_ga()[:, 0:72], A_sct()[:, so_:so_ + 72], s_itcn[:, :], cc(C30),
                OP.mult, OP.min), waits=[("vv", vev)], inc="vv")
            sx = P.op("scalar", lambda s, so_=so_: s.activation(
                A_gn()[:, so_:so_ + 72], A_ga()[:, 0:72], AF.Exp),
                waits=[("vv", vcl2)], inc="ss")
            P.op("vector", lambda v, so_=so_: v.tensor_scalar(
                A_ga()[:, 0:72], A_gn()[:, so_:so_ + 72], cc(CONE), None, OP.add),
                waits=[("ss", sx)], inc="vv")
            P.op("vector", lambda v, so_=so_: v.reciprocal(A_gn()[:, so_:so_ + 72],
                                                           A_ga()[:, 0:72]), inc="vv")
            P.op("vector", lambda v, so_=so_: v.tensor_reduce(
                A_gm()[:, 0:1], A_gn()[:, so_ + 64:so_ + 72], AXL.X, OP.max), inc="vv")
            P.op("vector", lambda v, so_=so_: v.tensor_scalar(
                A_ohc()[:, :], A_gn()[:, so_ + 64:so_ + 72], A_gm()[:, 0:1], None,
                OP.is_equal), inc="vv")
            P.op("vector", lambda v: v.tensor_tensor(A_ohl()[:, :], A_ohc()[:, :],
                                                     s_rev8[:, :], OP.mult), inc="vv")
            P.op("vector", lambda v: v.tensor_reduce(A_gm()[:, 1:2], A_ohl()[:, :], AXL.X,
                                                     OP.max), inc="vv")
            P.op("vector", lambda v: v.tensor_scalar(A_ohc()[:, :], A_ohl()[:, :],
                                                     A_gm()[:, 1:2], None, OP.is_equal),
                 inc="vv")
            ohc_ap = A_ohc()
            gb_ap = A_gb()
            gt_ap = A_gt()
            P.op("vector", lambda v, so_=so_: v.tensor_tensor(
                A_gb()[:, :].rearrange("p (c j) -> p c j", c=8),
                A_gn()[:, so_:so_ + 64].rearrange("p (c j) -> p c j", c=8),
                bass.AP(ohc_ap.tensor, ohc_ap.offset, [ohc_ap.ap[0], [1, 8], [0, 8]]),
                OP.mult), inc="vv")
            P.op("vector", lambda v: v.tensor_reduce(
                A_selg()[:, :].rearrange("p (j o) -> p j o", o=1),
                bass.AP(gb_ap.tensor, gb_ap.offset, [gb_ap.ap[0], [1, 8], [8, 8]]),
                AXL.X, OP.add), inc="vv")
            P.op("vector", lambda v: v.tensor_reduce(A_gm()[:, 2:3], A_selg()[:, :], AXL.X,
                                                     OP.max), inc="vv")
            P.op("vector", lambda v: v.tensor_scalar(A_gt()[:, :], A_selg()[:, :],
                                                     A_gm()[:, 2:3], None, OP.is_equal),
                 inc="vv")
            P.op("vector", lambda v: v.tensor_tensor(A_ohl()[:, :], A_gt()[:, :],
                                                     s_rev8[:, :], OP.mult), inc="vv")
            P.op("vector", lambda v: v.tensor_reduce(A_gm()[:, 3:4], A_ohl()[:, :], AXL.X,
                                                     OP.max), inc="vv")
            P.op("vector", lambda v: v.tensor_scalar(A_gt()[:, :], A_ohl()[:, :],
                                                     A_gm()[:, 3:4], None, OP.is_equal),
                 inc="vv")
            voh = P.op("vector", lambda v: v.tensor_tensor(
                A_oht()[:, :].rearrange("p (c j) -> p c j", c=8),
                bass.AP(ohc_ap.tensor, ohc_ap.offset, [ohc_ap.ap[0], [1, 8], [0, 8]]),
                bass.AP(gt_ap.tensor, gt_ap.offset, [gt_ap.ap[0], [0, 8], [1, 8]]),
                OP.mult), inc="vv")
            mo = P.op("tensor", lambda t_: t_.transpose(PB(4, 128, 128, 64), A_oht()[:, :],
                                                        s_ident[:, :]),
                      waits=[("vv", voh)] + BK.writer_waits([4]), inc="mm")
            vof = P.op("vector", lambda v, ho=ho: v.tensor_copy(A_ohf()[0:64, ho:ho + 128],
                                                                PB(4, 128, 128, 64)),
                       waits=[("mm", mo)], inc="vv")
            BK.consumed([4], "vv", vof)
        vohf = P.count["vv"]
        tap("sct0", lambda: bass.AP(s_big, 2560, [[4096, 128], [1, 72]]), t)
        tap("sct1", lambda: bass.AP(s_big, 2640, [[4096, 128], [1, 72]]), t)
        tap("gn0", lambda: bass.AP(s_big, 2720, [[4096, 128], [1, 72]]), t)
        tap("gn1", lambda: bass.AP(s_big, 2800, [[4096, 128], [1, 72]]), t)
        tap("ohf", lambda: bass.AP(s_big, 3176, [[4096, 64], [1, 256]]), t)

        hcs = []
        for f in range(2):
            bank = 6 + f
            for k in range(8):
                P.op("tensor", lambda t_, f=f, k=k, bank=bank: t_.matmul(
                    PB(bank, 0, 256), s_cw1[:, k * 256 + f * 128: k * 256 + f * 128 + 128],
                    s_h[:, k * 256:(k + 1) * 256], start=(k == 0), stop=(k == 7)),
                    waits=BK.writer_waits([bank]) if k == 0 else [],
                    inc="mm" if k == 7 else None)
            mc = P.count["mm"]
            sg = P.op("scalar", lambda s, f=f, bank=bank: s.activation(
                A_hc()[:, f * 256:(f + 1) * 256], PB(bank, 0, 256), AF.Gelu,
                bias=s_cb1[:, f:f + 1]), waits=[("mm", mc)], inc="ss")
            hcs.append(sg)
            BK.consumed([bank], "ss", sg)
        for k in range(2):
            P.op("tensor", lambda t_, k=k: t_.matmul(
                PB(6, 0, 256, 2), s_cw2[:, k * 2:(k + 1) * 2],
                A_hc()[:, k * 256:(k + 1) * 256], start=(k == 0), stop=(k == 1)),
                waits=[("ss", hcs[k])] + (BK.writer_waits([6]) if k == 0 else []),
                inc="mm" if k == 1 else None)
        mcp = P.count["mm"]
        scp = P.op("scalar", lambda s: s.activation(A_comp()[:, :], PB(6, 0, 256, 2),
                                                    AF.Tanh, bias=s_cb2[:, :]),
                   waits=[("mm", mcp)], inc="ss")
        BK.consumed([6], "ss", scp)
        vM = []
        for i in range(2):
            bank = 6 + i
            mi = P.op("tensor", lambda t_, i=i, bank=bank: t_.matmul(
                PB(bank, 0, 256, 64), s_hsel[0:2, i * 128:i * 128 + 64],
                A_comp()[0:2, :], start=True, stop=True),
                waits=[("ss", scp)] + BK.writer_waits([bank]), inc="mm")
            v = P.op("vector", lambda v, i=i, bank=bank: v.tensor_tensor(
                (A_M0() if i == 0 else A_M1())[:, :], A_ohf(), PB(bank, 0, 256, 64),
                OP.mult), waits=[("mm", mi), ("vv", vohf)], inc="vv")
            BK.consumed([bank], "vv", v)
            vM.append(v)
        P.op("tensor", lambda t_: t_.matmul(PB(7, 0, 256, 16), s_w1a[:, :], A_M0(),
                                            start=True, stop=False),
             waits=[("vv", vM[1])] + BK.writer_waits([7]), inc=None)
        P.op("tensor", lambda t_: t_.matmul(PB(7, 0, 256, 16), s_w1b[:, :], A_M1(),
                                            start=False, stop=False), inc=None)
        mhh = P.op("tensor", lambda t_: t_.matmul(PB(7, 0, 256, 16), s_b1t[:, :], A_ohf(),
                                                  start=False, stop=True), inc="mm")
        shh = P.op("scalar", lambda s: s.activation(A_hh()[:, :], PB(7, 0, 256, 16),
                                                    AF.Relu), waits=[("mm", mhh)], inc="ss")
        BK.consumed([7], "ss", shh)
        mw2 = P.op("tensor", lambda t_: t_.matmul(PB(6, 0, 256, 16), s_w2t[:, :], A_ohf(),
                                                  start=True, stop=True),
                   waits=BK.writer_waits([6]), inc="mm")
        vt1 = P.op("vector", lambda v: v.tensor_tensor(A_t1()[:, :], A_hh(),
                                                       PB(6, 0, 256, 16), OP.mult),
                   waits=[("mm", mw2), ("ss", shh)], inc="vv")
        BK.consumed([6], "vv", vt1)
        P.op("tensor", lambda t_: t_.matmul(PB(5, 0, 256, 1), s_ones16[:, :], A_t1(),
                                            start=True, stop=False),
             waits=[("vv", vt1)] + BK.writer_waits([5]), inc=None)
        mmg = P.op("tensor", lambda t_: t_.matmul(PB(5, 0, 256, 1), s_b2t[:, :], A_ohf(),
                                                  start=False, stop=True), inc="mm")
        vmg = P.op("vector", lambda v: v.tensor_copy(A_magr()[:, :], PB(5, 0, 256, 1)),
                   waits=[("mm", mmg)], inc="vv")
        BK.consumed([5], "vv", vmg)
        mgb = P.op("tensor", lambda t_: t_.matmul(PB(5, 0, 256, 64), s_onesr[0:1, 0:64],
                                                  A_magr(), start=True, stop=True),
                   waits=[("vv", vmg)] + BK.writer_waits([5]), inc="mm")
        vso = P.op("vector", lambda v: v.tensor_tensor(A_soh()[:, :], A_ohf(),
                                                       PB(5, 0, 256, 64), OP.mult),
                   waits=[("mm", mgb)], inc="vv")
        BK.consumed([5], "vv", vso)
        tap("soh", lambda: bass.AP(s_big, 3432, [[4096, 64], [1, 256]]), t)
        vyp = None
        for cC in range(8):
            bank = 4 + (cC % 2)
            mD = P.op("tensor", lambda t_, cC=cC, bank=bank: t_.matmul(
                PB(bank, 0, 256), s_dirss[:, cC * 128:(cC + 1) * 128], A_soh(),
                start=True, stop=True),
                waits=[("vv", vso)] + BK.writer_waits([bank]), inc="mm")
            vyp = P.op("vector", lambda v, cC=cC, bank=bank: v.tensor_tensor(
                s_xa[:, cC * 256:(cC + 1) * 256], PB(bank, 0, 256),
                s_h[:, cC * 256:(cC + 1) * 256], OP.add), waits=[("mm", mD)], inc="vv")
            BK.consumed([bank], "vv", vyp)
        tap("ypre", lambda: s_xa[:, :], t)

        ln_stats(s_xa, wv=vyp)
        vmr = P.op("vector", lambda v: v.tensor_tensor(s_mrb[:, :], s_mb[:, :], s_rb[:, :],
                                                       OP.mult), inc="vv")
        vt2 = ln_apply(s_xa, s_h, A_tmp3())
        P.op("vector", lambda v: v.nop(), waits=[("vv", vt2)], inc="xf")
        for cC in range(8):
            P.op("vector", lambda v, cC=cC: v.tensor_scalar(
                s_vb[:, :], s_mrb[:, :], s_gamc[:, cC:cC + 1], None, OP.mult),
                waits=[("vv", vmr)], inc="vv")
            P.op("vector", lambda v, cC=cC: v.scalar_tensor_tensor(
                s_wfin[:, :], s_x2[:, cC * 256:(cC + 1) * 256], s_betc[:, cC:cC + 1],
                s_vb[:, :], OP.add, OP.subtract), inc="vv")
            P.op("vector", lambda v, cC=cC: v.scalar_tensor_tensor(
                s_x2[:, cC * 256:(cC + 1) * 256], s_h[:, cC * 256:(cC + 1) * 256],
                s_gamc[:, cC:cC + 1], s_wfin[:, :], OP.mult, OP.add), inc="vv")
        vfin = P.count["vv"]
        tap("out", lambda: s_x2[:, :], t)

        st_count = P.op("gpsimd", lambda g, t=t: g.dma_start(
            out=bass.AP(d_out, t * TT, [[RTOK, 128], [128 * RTOK, 8], [1, TT]]),
            in_=s_x2[:, :].rearrange("p (c f) -> p c f", c=8)),
            waits=[("vv", vfin)], inc="st", inc_n=16)

    P.op("gpsimd", lambda g: None, waits=[("st", P.count["st"])])

    sems = {}
    for s in P.count:
        sems[s] = es.enter_context(nc.semaphore(s))
    with nc.Block() as block:
        @block.tensor
        def _(eng):
            P.emit("tensor", eng, sems)

        @block.vector
        def _(eng):
            P.emit("vector", eng, sems)

        @block.scalar
        def _(eng):
            P.emit("scalar", eng, sems)

        @block.gpsimd
        def _(eng):
            P.emit("gpsimd", eng, sems)

        @block.sync
        def _(eng):
            P.emit("sync", eng, sems)

    nc._keepalive = es
    return nc


def make_inmaps(inputs):
    f32 = np.float32
    x = np.asarray(inputs["L0_attn_x"], f32)
    wz = np.asarray(inputs["L0_attn_Wz"], f32)
    wzs = np.ascontiguousarray(
        wz.reshape(8, 128, 16, 128).transpose(2, 0, 1, 3).reshape(16 * 8 * 128, 128))
    win = np.asarray(inputs["L0_attn_Win"], f32)
    win_r = np.ascontiguousarray(np.concatenate(
        [win[:, 136:648], win[:, 0:64], win[:, 64:128], win[:, 128:136]], axis=1))
    sw1 = np.asarray(inputs["L0_ffn_spline_w1"], f32)
    hm = np.zeros((128, 8), f32)
    hm[:, 0::2] = 1.0
    common = {
        "wzs": wzs,
        "win": win_r,
        "wzg": np.ascontiguousarray(np.asarray(inputs["L0_attn_Wzg"], f32)),
        "wout": np.ascontiguousarray(np.asarray(inputs["L0_attn_Wout"], f32)),
        "cw1": np.ascontiguousarray(np.asarray(inputs["L0_ffn_compress_w1"], f32)),
        "cw2": np.ascontiguousarray(np.asarray(inputs["L0_ffn_compress_w2"], f32)),
        "w1a": np.ascontiguousarray(sw1[:, 0, :]),
        "w1b": np.ascontiguousarray(sw1[:, 1, :]),
        "b1t": np.ascontiguousarray(np.asarray(inputs["L0_ffn_spline_b1"], f32)),
        "w2t": np.ascontiguousarray(np.asarray(inputs["L0_ffn_spline_w2"], f32)[:, :, 0]),
        "b2t": np.ascontiguousarray(np.asarray(inputs["L0_ffn_spline_b2"], f32)),
        "dirs": np.ascontiguousarray(np.asarray(inputs["L0_ffn_directions"], f32)),
        "sigs": np.ascontiguousarray(np.asarray(inputs["L0_ffn_signatures_raw"], f32)),
        "csel": np.ascontiguousarray(
            (np.arange(64)[:, None] // 8 == np.arange(8)[None, :]).astype(f32)),
        "dtb": np.asarray(inputs["L0_attn_dt_bias"], f32).reshape(8, 1),
        "alog": np.asarray(inputs["L0_attn_A_log"], f32).reshape(8, 1),
        "drep": np.ascontiguousarray(
            np.repeat(np.asarray(inputs["L0_attn_D_param"], f32), 64).reshape(4, 128).T),
        "gamc": np.ascontiguousarray(
            np.asarray(inputs["L0_ffn_norm_gamma"], f32).reshape(8, 128).T),
        "betc": np.ascontiguousarray(
            np.asarray(inputs["L0_ffn_norm_beta"], f32).reshape(8, 128).T),
        "cb1": np.ascontiguousarray(
            np.asarray(inputs["L0_ffn_compress_b1"], f32).reshape(2, 128).T),
        "cb2": np.asarray(inputs["L0_ffn_compress_b2"], f32).reshape(2, 1),
        "temp": np.asarray(inputs["L0_ffn_temperature"], f32).reshape(1, 1),
        "oscale": np.asarray(inputs["L0_ffn_output_scale"], f32).reshape(1, 1),
        "ident": np.eye(128, dtype=f32),
        "rev8": np.tile(8.0 - np.arange(8, dtype=f32), (128, 1)),
        "u01": (np.arange(128)[None, :] >= np.arange(128)[:, None]).astype(f32),
        "mbias": np.where(np.arange(128)[None, :] >= np.arange(128)[:, None],
                          0.0, MASKNEG).astype(f32),
        "onesr": np.ones((1, 256), f32),
        "ones8r": np.ones((8, 128), f32),
        "onespad": np.concatenate([np.ones((128, 1), f32), np.zeros((128, 127), f32)], 1),
        "ones16": np.ones((16, 1), f32),
        "cc": np.tile(np.array([[1.0 / 1024, EPS, -1.0, 0.0, 1.0, 1e-6, 0.3, -0.3,
                                 30.0, -0.5]], f32), (128, 1)),
        "hmE": hm,
        "hmO": (1.0 - hm).astype(f32),
        "hsel": np.ascontiguousarray(np.repeat(np.eye(8, dtype=f32), 128, axis=1)),
    }
    in_maps = []
    for i in range(NCORES):
        m = dict(common)
        m["x"] = np.ascontiguousarray(x[BPC * i:BPC * (i + 1)].reshape(RTOK, D).T)
        in_maps.append(m)
    return in_maps


_NC_CACHE = {}


def kernel(**inputs):
    if "full" not in _NC_CACHE:
        _NC_CACHE["full"] = build_program()
    nc = _NC_CACHE["full"]
    in_maps = make_inmaps(inputs)
    res = run_bass_kernel_spmd(nc, in_maps, core_ids=list(range(NCORES)))
    outs = []
    for i in range(NCORES):
        o = np.asarray(res.results[i]["out"])
        outs.append(o.T.reshape(BPC, SEQ, D))
    return np.concatenate(outs, axis=0)


# version 05
# speedup vs baseline: 1.0885x; 1.0885x over previous
import contextlib
import numpy as np
import concourse.bass as bass
import concourse.mybir as mybir
from concourse.bass_utils import run_bass_kernel_spmd

F32 = mybir.dt.float32
AF = mybir.ActivationFunctionType
OP = mybir.AluOpType
AXL = mybir.AxisListType

NCORES = 8
BATCH, SEQ, D = 32, 2048, 1024
BPC = BATCH // NCORES
RTOK = BPC * SEQ
TT = 256
NTILE = RTOK // TT
Q = 128
CPT = TT // Q
NH, HD = 8, 64
DI = NH * HD
DZ = 2 * D
EPS = 1e-5
MASKNEG = -30000.0
CHPS = SEQ // Q

C1024, CEPS, CNEG1, CZERO, CONE, C1E6, CTH, CNTH, C30, CNEGH = 0, 1, 2, 3, 4, 5, 6, 7, 8, 9


class Prog:
    ENGINES = ("tensor", "vector", "scalar", "gpsimd", "sync")

    def __init__(self):
        self.ops = {e: [] for e in self.ENGINES}
        self.count = {}
        self.hw = {e: {} for e in self.ENGINES}

    def sem(self, name):
        self.count.setdefault(name, 0)

    def op(self, eng, fn, waits=(), inc=None, inc_n=1):
        ws = []
        for s, c in waits:
            if c is None or c <= 0:
                continue
            assert c <= self.count[s], f"wait {s}>={c}, recorded {self.count[s]}"
            if self.hw[eng].get(s, -1) < c:
                ws.append((s, c))
                self.hw[eng][s] = c
        if inc is not None:
            self.count[inc] += inc_n
        self.ops[eng].append((fn, ws, inc, inc_n))
        return self.count[inc] if inc is not None else None

    def emit(self, eng_name, eng, sems):
        for fn, ws, inc, inc_n in self.ops[eng_name]:
            for s, c in ws:
                eng.wait_ge(sems[s], c)
            instr = fn(eng)
            if inc is not None:
                assert instr is not None, f"inc={inc} op returned None on {eng_name}"
                instr.then_inc(sems[inc], inc_n)


class Banks:
    def __init__(self):
        self.pending = {b: [] for b in range(8)}

    def writer_waits(self, banks):
        ws = []
        for b in banks:
            ws += self.pending[b]
            self.pending[b] = []
        return ws

    def consumed(self, banks, sem, count):
        for b in banks:
            self.pending[b].append((sem, count))


def rep(ap, nrep, inner):
    return bass.AP(ap.tensor, ap.offset, [ap.ap[0], [0, nrep], [ap.ap[-1][0], inner]])


def col_rep(ap, ncols, inner):
    return bass.AP(ap.tensor, ap.offset, [ap.ap[0], [ap.ap[-1][0], ncols], [0, inner]])


def build_program(n_tiles=NTILE, tap_names=()):
    nc = bass.Bass("TRN2", target_bir_lowering=False, detect_race_conditions=False)

    def dram_in(name, shape, dt=F32):
        return nc.dram_tensor(name, list(shape), dt, kind="ExternalInput")

    d_x = dram_in("x", [D, RTOK])
    d_wzs = dram_in("wzs", [16 * 8 * 128, 128])
    d_win = dram_in("win", [D, 648])
    d_wzg = dram_in("wzg", [DZ, DI])
    d_wout = dram_in("wout", [DI, D])
    d_cw1 = dram_in("cw1", [D, 256])
    d_cw2 = dram_in("cw2", [256, 2])
    d_w1a = dram_in("w1a", [64, 16])
    d_w1b = dram_in("w1b", [64, 16])
    d_b1t = dram_in("b1t", [64, 16])
    d_w2t = dram_in("w2t", [64, 16])
    d_b2t = dram_in("b2t", [64, 1])
    d_dirs = dram_in("dirs", [64, D])
    d_sigs = dram_in("sigs", [64, D])
    d_csel = dram_in("csel", [64, 8])
    d_dtb = dram_in("dtb", [8, 1])
    d_alog = dram_in("alog", [8, 1])
    d_drep = dram_in("drep", [128, 4])
    d_gamc = dram_in("gamc", [128, 8])
    d_betc = dram_in("betc", [128, 8])
    d_cb1 = dram_in("cb1", [128, 2])
    d_cb2 = dram_in("cb2", [2, 1])
    d_temp = dram_in("temp", [1, 1])
    d_oscale = dram_in("oscale", [1, 1])
    d_ident = dram_in("ident", [128, 128])
    d_rev8 = dram_in("rev8", [128, 8])
    d_u01 = dram_in("u01", [128, 128])
    d_mbias = dram_in("mbias", [128, 128])
    d_onesr = dram_in("onesr", [1, 256])
    d_ones8r = dram_in("ones8r", [8, 128])
    d_onespad = dram_in("onespad", [128, 128])
    d_ones16 = dram_in("ones16", [16, 1])
    d_cc = dram_in("cc", [128, 10])
    d_hmE = dram_in("hmE", [128, 8])
    d_hmO = dram_in("hmO", [128, 8])
    d_hsel = dram_in("hsel", [8, 1024])
    d_out = nc.dram_tensor("out", [D, RTOK], F32, kind="ExternalOutput")
    d_tap = {t: nc.dram_tensor(f"tap_{t}", [128, 2048], F32, kind="ExternalOutput")
             for t in tap_names}

    P = Prog()
    for s in ("ldw", "ldx", "lz", "mm", "vv", "ss", "gg", "st", "xf"):
        P.sem(s)
    BK = Banks()

    es = contextlib.ExitStack()

    def sbuf(name, shape, dt=F32):
        return es.enter_context(nc.sbuf_tensor(name, shape, dt))

    s_wzb = sbuf("s_wzb", [128, 3072])
    s_win = sbuf("s_win", [128, 8 * 648])
    s_wzg = sbuf("s_wzg", [128, 16 * DI])
    s_wout = sbuf("s_wout", [128, 4 * D])
    s_cw1 = sbuf("s_cw1", [128, 8 * 256])
    s_cw2 = sbuf("s_cw2", [128, 4])
    s_w1a = sbuf("s_w1a", [64, 16])
    s_w1b = sbuf("s_w1b", [64, 16])
    s_b1t = sbuf("s_b1t", [64, 16])
    s_w2t = sbuf("s_w2t", [64, 16])
    s_b2t = sbuf("s_b2t", [64, 1])
    s_dirss = sbuf("s_dirss", [64, D])
    s_csel = sbuf("s_csel", [64, 8])
    s_qT = sbuf("s_qT", [128, 8 * 64])
    s_csT = sbuf("s_csT", [128, 8 * 8])
    s_dtb = sbuf("s_dtb", [8, 1])
    s_alog = sbuf("s_alog", [8, 1])
    s_negA = sbuf("s_negA", [8, 1])
    s_eA = sbuf("s_eA", [8, 1])
    s_drep = sbuf("s_drep", [128, 4])
    s_gamc = sbuf("s_gamc", [128, 8])
    s_betc = sbuf("s_betc", [128, 8])
    s_cb1 = sbuf("s_cb1", [128, 2])
    s_cb2 = sbuf("s_cb2", [2, 1])
    s_temp = sbuf("s_temp", [1, 1])
    s_it1 = sbuf("s_it1", [1, 1])
    s_it2 = sbuf("s_it2", [1, 1])
    s_itcn = sbuf("s_itcn", [128, 1])
    s_osc = sbuf("s_osc", [64, 1])
    s_ident = sbuf("s_ident", [128, 128])
    s_rev8 = sbuf("s_rev8", [128, 8])
    s_u01 = sbuf("s_u01", [128, 128])
    s_mbias = sbuf("s_mbias", [128, 128])
    s_onesr = sbuf("s_onesr", [1, 256])
    s_ones8r = sbuf("s_ones8r", [8, 128])
    s_onespad = sbuf("s_onespad", [128, 128])
    s_ones16 = sbuf("s_ones16", [16, 1])
    s_cc = sbuf("s_cc", [128, 10])
    s_hmE = sbuf("s_hmE", [128, 8])
    s_hmO = sbuf("s_hmO", [128, 8])
    s_hsel = sbuf("s_hsel", [8, 1024])

    s_xaA = sbuf("s_xaA", [128, 2048])
    s_xaB = sbuf("s_xaB", [128, 2048])
    s_x2 = sbuf("s_x2", [128, 2048])
    s_h = sbuf("s_h", [128, 2048])
    s_big = sbuf("s_big", [128, 4096])
    s_XT = sbuf("s_XT", [128, 1024])
    s_Bf = sbuf("s_Bf", [64, 256])
    s_Cf = sbuf("s_Cf", [64, 256])
    s_dtf = sbuf("s_dtf", [8, 256])
    s_sp1 = sbuf("s_sp1", [8, 256])
    s_sp2 = sbuf("s_sp2", [8, 256])
    s_sp3 = sbuf("s_sp3", [8, 256])
    s_del = sbuf("s_del", [8, 256])
    s_acl = sbuf("s_acl", [8, 256])
    s_clc = sbuf("s_clc", [8, 256])
    s_clT = sbuf("s_clT", [128, 8])
    s_dT = sbuf("s_dT", [128, 8])
    s_dTE = sbuf("s_dTE", [128, 8])
    s_dTO = sbuf("s_dTO", [128, 8])
    s_wd = sbuf("s_wd", [128, 8])
    s_w8 = sbuf("s_w8", [8, 128])
    s_w8b = sbuf("s_w8b", [8, 128])
    s_w8T = sbuf("s_w8T", [128, 8])
    s_cer = sbuf("s_cer", [1, 8])
    s_cer2 = sbuf("s_cer2", [1, 8])
    s_Btm = sbuf("s_Btm", [128, 64])
    s_XdE = sbuf("s_XdE", [128, 512])
    s_XdO = sbuf("s_XdO", [128, 512])
    s_Wx = sbuf("s_Wx", [128, 512])
    s_Epre = sbuf("s_Epre", [128, 512])
    s_Ehat = sbuf("s_Ehat", [128, 512])
    s_G = sbuf("s_G", [128, 1024])
    s_SU = sbuf("s_SU", [128, 128])
    s_Cexp = sbuf("s_Cexp", [64, 1024])
    s_SP = sbuf("s_SP", [64, 512])
    s_SP2 = sbuf("s_SP2", [64, 512])
    s_HPE = sbuf("s_HPE", [64, 512])
    s_HPO = sbuf("s_HPO", [64, 512])
    s_yT = sbuf("s_yT", [128, 1024])
    s_sq = sbuf("s_sq", [128, 512])
    s_srow = sbuf("s_srow", [1, 512])
    s_mb = sbuf("s_mb", [128, 256])
    s_rb = sbuf("s_rb", [128, 256])
    s_vb = sbuf("s_vb", [128, 256])
    s_mrb = sbuf("s_mrb", [128, 256])
    s_wfin = sbuf("s_wfin", [128, 256])

    A_sigs = lambda: s_big[0:64, 0:1024]
    A_q = lambda: s_big[0:64, 1024:2048]
    A_cs = lambda: s_big[0:8, 2048:3072]
    A_dirs = lambda: s_big[0:64, 3072:4096]
    A_u = lambda: s_big[:, 0:1024]
    A_zc = lambda: s_big[:, 1024:1536]
    A_hc = lambda: s_big[:, 1536:2048]
    A_M0 = lambda: s_big[0:64, 2048:2304]
    A_M1 = lambda: s_big[0:64, 2304:2560]
    A_magr = lambda: s_big[0:1, 2048:2304]
    A_sct = lambda: s_big[:, 2560:2720]
    A_gn = lambda: s_big[:, 2720:2880]
    A_ga = lambda: s_big[:, 2880:3008]
    A_gb = lambda: s_big[:, 3008:3072]
    A_gm = lambda: s_big[:, 3072:3080]
    A_selg = lambda: s_big[:, 3080:3088]
    A_ohc = lambda: s_big[:, 3088:3096]
    A_ohl = lambda: s_big[:, 3096:3104]
    A_gt = lambda: s_big[:, 3104:3112]
    A_oht = lambda: s_big[:, 3112:3176]
    A_ohf = lambda: s_big[0:64, 3176:3432]
    A_soh = lambda: s_big[0:64, 3432:3688]
    A_comp = lambda: s_big[0:2, 3688:3944]
    A_hh = lambda: s_big[0:16, 2560:2816]
    A_t1 = lambda: s_big[0:16, 2816:3072]
    A_tmp3 = lambda: s_big
    A_zga = lambda: s_big[:, 0:1024]
    A_zgb = lambda: s_big[0:128, 2048:3072]
    A_u2 = lambda: s_big[:, 1024:2048]

    ps = es.enter_context(nc.psum_tensor("ps", [128, 4096], mybir.dt.float32))

    def PB(bank, off=0, w=512, rows=128, r0=0):
        return ps[r0:r0 + rows, bank * 512 + off: bank * 512 + off + w]

    cc = lambda i, p=128: s_cc[0:p, i:i + 1]

    nw = [0]

    def wload(fn):
        nw[0] += 1
        P.op("sync", fn, inc="ldw", inc_n=16)

    for k in range(8):
        wload(lambda sy, k=k: sy.dma_start(out=s_win[:, k * 648:(k + 1) * 648],
                                           in_=d_win[k * 128:(k + 1) * 128, :]))
    for k in range(16):
        wload(lambda sy, k=k: sy.dma_start(out=s_wzg[:, k * DI:(k + 1) * DI],
                                           in_=d_wzg[k * 128:(k + 1) * 128, :]))
    for k in range(4):
        wload(lambda sy, k=k: sy.dma_start(out=s_wout[:, k * D:(k + 1) * D],
                                           in_=d_wout[k * 128:(k + 1) * 128, :]))
    for k in range(8):
        wload(lambda sy, k=k: sy.dma_start(out=s_cw1[:, k * 256:(k + 1) * 256],
                                           in_=d_cw1[k * 128:(k + 1) * 128, :]))
    for k in range(2):
        wload(lambda sy, k=k: sy.dma_start(out=s_cw2[:, k * 2:(k + 1) * 2],
                                           in_=d_cw2[k * 128:(k + 1) * 128, :]))
    wload(lambda sy: sy.dma_start(out=A_sigs(), in_=d_sigs[:, :]))
    wload(lambda sy: sy.dma_start(out=A_dirs(), in_=d_dirs[:, :]))
    for a, b in [(s_w1a, d_w1a), (s_w1b, d_w1b), (s_b1t, d_b1t), (s_w2t, d_w2t),
                 (s_b2t, d_b2t), (s_csel, d_csel), (s_dtb, d_dtb), (s_alog, d_alog),
                 (s_drep, d_drep), (s_gamc, d_gamc), (s_betc, d_betc), (s_cb1, d_cb1),
                 (s_cb2, d_cb2), (s_temp, d_temp), (s_ident, d_ident), (s_rev8, d_rev8),
                 (s_u01, d_u01), (s_mbias, d_mbias), (s_onesr, d_onesr),
                 (s_ones8r, d_ones8r), (s_onespad, d_onespad), (s_ones16, d_ones16),
                 (s_cc, d_cc), (s_hmE, d_hmE), (s_hmO, d_hmO), (s_hsel, d_hsel)]:
        wload(lambda sy, a=a, b=b: sy.dma_start(out=a[:], in_=b[:]))
    wload(lambda sy: sy.dma_start(out=s_osc[:, :],
                                  in_=bass.AP(d_oscale, 0, [[0, 64], [1, 1]])))
    LDW = nw[0] * 16
    WL = [("ldw", LDW)]

    P.op("vector", lambda v: v.tensor_scalar(s_G[0:64, 0:1024], A_sigs(), cc(CTH, 64),
                                             None, OP.is_gt), waits=WL, inc="vv")
    P.op("vector", lambda v: v.tensor_scalar(s_yT[0:64, 0:1024], A_sigs(), cc(CNTH, 64),
                                             None, OP.is_lt), inc="vv")
    vq = P.op("vector", lambda v: v.tensor_tensor(A_q(), s_G[0:64, 0:1024],
                                                  s_yT[0:64, 0:1024], OP.subtract), inc="vv")
    for half in range(2):
        m = P.op("tensor", lambda t, h=half: t.matmul(PB(0, 0, 512, 8), s_csel[:, :],
                                                      A_q()[:, h * 512:(h + 1) * 512],
                                                      start=True, stop=True),
                 waits=[("vv", vq)] + BK.writer_waits([0]), inc="mm")
        P.op("vector", lambda v: v.tensor_scalar(s_sq[0:8, 0:512], PB(0, 0, 512, 8),
                                                 cc(CZERO, 8), None, OP.is_gt),
             waits=[("mm", m)], inc="vv")
        P.op("vector", lambda v: v.tensor_scalar(s_yT[0:8, 0:512], PB(0, 0, 512, 8),
                                                 cc(CZERO, 8), None, OP.is_lt),
             inc="vv")
        e = P.op("vector", lambda v, h=half: v.tensor_tensor(
            A_cs()[:, h * 512:(h + 1) * 512], s_sq[0:8, 0:512], s_yT[0:8, 0:512],
            OP.subtract), inc="vv")
        BK.consumed([0], "vv", e)
    for c in range(8):
        m = P.op("tensor", lambda t, c=c: t.transpose(PB(1, 0, 64),
                                                      A_q()[:, c * 128:(c + 1) * 128],
                                                      s_ident[0:64, 0:64]),
                 waits=BK.writer_waits([1]), inc="mm")
        v = P.op("vector", lambda v, c=c: v.tensor_copy(s_qT[:, c * 64:(c + 1) * 64],
                                                        PB(1, 0, 64)),
                 waits=[("mm", m)], inc="vv")
        BK.consumed([1], "vv", v)
        m = P.op("tensor", lambda t, c=c: t.transpose(PB(1, 64, 8),
                                                      A_cs()[:, c * 128:(c + 1) * 128],
                                                      s_ident[0:8, 0:8]),
                 waits=BK.writer_waits([1]), inc="mm")
        v = P.op("vector", lambda v, c=c: v.tensor_copy(s_csT[:, c * 8:(c + 1) * 8],
                                                        PB(1, 64, 8)),
                 waits=[("mm", m)], inc="vv")
        BK.consumed([1], "vv", v)
    e = P.op("scalar", lambda s: s.activation(s_eA[:, :], s_alog[:, :], AF.Exp),
             waits=WL, inc="ss")
    P.op("vector", lambda v: v.tensor_scalar(s_negA[:, :], s_eA[:, :], cc(CNEG1, 8), None,
                                             OP.mult), waits=[("ss", e)], inc="vv")
    P.op("vector", lambda v: v.tensor_scalar(s_it1[:, :], s_temp[:, :], cc(C1E6, 1), None,
                                             OP.add), waits=WL, inc="vv")
    vi = P.op("vector", lambda v: v.reciprocal(s_it2[:, :], s_it1[:, :]), inc="vv")
    m = P.op("tensor", lambda t: t.matmul(PB(1, 128, 1), s_onesr[0:1, 0:128], s_it2[:, :],
                                          start=True, stop=True),
             waits=[("vv", vi)] + BK.writer_waits([1]), inc="mm")
    v = P.op("vector", lambda v: v.tensor_scalar(s_itcn[:, :], PB(1, 128, 1), cc(CNEG1),
                                                 None, OP.mult), waits=[("mm", m)], inc="vv")
    BK.consumed([1], "vv", v)
    P.op("vector", lambda v: v.tensor_scalar(s_dirss[:, :], A_dirs(), s_osc[:, :],
                                             None, OP.mult), waits=WL, inc="vv")

    def ln_stats(src, wv=None):
        w01 = BK.writer_waits([0, 1])
        last = None
        f32r = mybir.dt.float32r
        for c in range(8):
            sqw = [("mm", sq_slot[c % 2])] if sq_slot[c % 2] else []
            if c < 2 and wv is not None:
                sqw.append(("vv", wv))
            sc = P.op("scalar", lambda s, c=c: s.activation(
                s_sq[:, (c % 2) * 256:(c % 2) * 256 + 256], src[:, c * 256:(c + 1) * 256],
                AF.Square), waits=sqw, inc="ss")
            P.op("tensor", lambda t, c=c: t.matmul(PB(0, 0, 256), s_onespad[:, :],
                                                   src[:, c * 256:(c + 1) * 256],
                                                   start=(c == 0), stop=(c == 7)),
                 waits=(w01 + ([("vv", wv)] if wv is not None else [])) if c == 0 else [],
                 inc=None)
            last = P.op("tensor", lambda t, c=c: t.matmul(
                PB(1, 0, 256), s_onespad[:, :],
                s_sq[:, (c % 2) * 256:(c % 2) * 256 + 256],
                start=(c == 0), stop=(c == 7)), waits=[("ss", sc)], inc="mm")
            sq_slot[c % 2] = last
        vr = P.op("vector", lambda v: v.tensor_copy(s_srow[:, 0:256], PB(0, 0, 256, 1)),
                  waits=[("mm", last)], inc="vv")
        vr2 = P.op("vector", lambda v: v.tensor_copy(s_srow[:, 256:512], PB(1, 0, 256, 1)),
                   inc="vv")
        BK.consumed([0], "vv", vr)
        BK.consumed([1], "vv", vr2)
        mb = P.op("tensor", lambda t: t.matmul(PB(0, 0, 512), s_onesr[0:1, 0:128],
                                               s_srow[:, :], start=True, stop=True),
                  waits=[("vv", vr2)] + BK.writer_waits([0]), inc="mm")
        P.op("vector", lambda v: v.tensor_scalar(s_mb[:, :], PB(0, 0, 256), cc(C1024),
                                                 None, OP.mult), waits=[("mm", mb)], inc="vv")
        P.op("vector", lambda v: v.tensor_tensor(s_vb[:, :], s_mb[:, :], s_mb[:, :],
                                                 OP.mult), inc="vv")
        vv = P.op("vector", lambda v: v.scalar_tensor_tensor(s_rb[:, :], PB(0, 256, 256),
                                                             cc(C1024), s_vb[:, :],
                                                             OP.mult, OP.subtract), inc="vv")
        BK.consumed([0], "vv", vv)
        sv = P.op("scalar", lambda s: s.activation(s_vb[:, :], s_rb[:, :], AF.Ln,
                                                   bias=cc(CEPS)), waits=[("vv", vv)],
                  inc="ss")
        sv2 = P.op("scalar", lambda s: s.activation(s_rb[:, :], s_vb[:, :], AF.Exp,
                                                    scale=cc(CNEGH)), inc="ss")
        return P.op("vector", lambda v: v.nop(), waits=[("ss", sv2)], inc="vv")

    def ln_apply(src, dst, tmp, extra=()):
        P.op("vector", lambda v: v.tensor_tensor(
            tmp[:, 0:2048].rearrange("p (c f) -> p c f", c=8),
            src[:, :].rearrange("p (c f) -> p c f", c=8),
            rep(s_mb[:, :], 8, 256), OP.subtract),
            waits=list(extra), inc="vv")
        return P.op("vector", lambda v: v.tensor_tensor(
            dst[:, :].rearrange("p (c f) -> p c f", c=8),
            tmp[:, 0:2048].rearrange("p (c f) -> p c f", c=8),
            rep(s_rb[:, :], 8, 256), OP.mult), inc="vv")

    taps_set = set(tap_names)
    TAP_TILE = 0

    def tap(name, ap_fn, tile):
        if name not in taps_set or tile != TAP_TILE:
            return
        src = ap_fn()
        rr, cc_ = src.shape[0], int(np.prod(src.shape[1:]))
        P.op("gpsimd", lambda gp, name=name, src=src, rr=rr, cc_=cc_: gp.dma_start(
            out=d_tap[name][0:rr, 0:cc_], in_=src),
            waits=[("vv", P.count["vv"]), ("ss", P.count["ss"]), ("mm", P.count["mm"])],
            inc="gg", inc_n=16)
        P.op("vector", lambda v: None, waits=[("gg", P.count["gg"])])
        P.op("scalar", lambda s: None, waits=[("gg", P.count["gg"])])
        P.op("tensor", lambda t_: None, waits=[("gg", P.count["gg"])])

    sq_slot = [None, None]
    st_count = 0
    last_sce = [0]
    z_ring = {}
    for t in range(n_tiles):
        s_xa = s_xaA if t % 2 == 0 else s_xaB
        P.op("sync", lambda sy, t=t, s_xa=s_xa: sy.dma_start(
            out=s_xa[:, :].rearrange("p (c f) -> p c f", c=8),
            in_=bass.AP(d_x, t * TT, [[RTOK, 128], [128 * RTOK, 8], [1, TT]])),
            waits=([("xf", t - 1)] if t >= 2 else []), inc="ldx", inc_n=16)
        XL = [("ldx", 16 * (t + 1))]
        P.op("vector", lambda v: None, waits=XL)
        P.op("scalar", lambda s: None, waits=XL)
        P.op("tensor", lambda t_: None, waits=XL)

        ln_stats(s_xa)
        vh1 = ln_apply(s_xa, s_h, s_x2, extra=[("st", st_count)])
        tap("h1", lambda: s_h[:, :], t)

        xe = []
        for fc in range(4):
            bank = 2 + (fc % 2)
            for k in range(8):
                P.op("tensor", lambda t_, fc=fc, k=k, bank=bank: t_.matmul(
                    PB(bank, 0, 256), s_win[:, k * 648 + fc * 128: k * 648 + fc * 128 + 128],
                    s_h[:, k * 256:(k + 1) * 256], start=(k == 0), stop=(k == 7)),
                    waits=([("vv", vh1)] + BK.writer_waits([bank])) if k == 0 else [],
                    inc="mm" if k == 7 else None)
            mmx = P.count["mm"]
            v = P.op("vector", lambda v, fc=fc, bank=bank: v.tensor_copy(
                s_XT[:, fc * 256:(fc + 1) * 256], PB(bank, 0, 256)),
                waits=[("mm", mmx)], inc="vv")
            BK.consumed([bank], "vv", v)
            xe.append(v)
        for c0, w, dst, r in [(512, 64, s_Bf, 64), (576, 64, s_Cf, 64), (640, 8, s_dtf, 8)]:
            for k in range(8):
                P.op("tensor", lambda t_, k=k, c0=c0, w=w, r=r: t_.matmul(
                    PB(4, 0, 256, r), s_win[:, k * 648 + c0: k * 648 + c0 + w],
                    s_h[:, k * 256:(k + 1) * 256], start=(k == 0), stop=(k == 7)),
                    waits=BK.writer_waits([4]) if k == 0 else [], inc="mm" if k == 7 else None)
            mmb = P.count["mm"]
            v = P.op("vector", lambda v, dst=dst, r=r: v.tensor_copy(dst[:, :],
                                                                     PB(4, 0, 256, r)),
                     waits=[("mm", mmb)], inc="vv")
            BK.consumed([4], "vv", v)
        vbc = P.count["vv"]

        P.op("scalar", lambda s: s.activation(s_sp1[:, :], s_dtf[:, :], AF.Relu,
                                              bias=s_dtb[:, :]), waits=[("vv", vbc)],
             inc="ss")
        P.op("scalar", lambda s: s.activation(s_sp2[:, :], s_dtf[:, :], AF.Abs,
                                              bias=s_dtb[:, :]), inc="ss")
        s3 = P.op("scalar", lambda s: s.activation(s_sp3[:, :], s_sp2[:, :], AF.Exp,
                                                   scale=cc(CNEG1, 8)), inc="ss")
        v1 = P.op("vector", lambda v: v.tensor_scalar(s_sp2[:, :], s_sp3[:, :], cc(CONE, 8),
                                                      None, OP.add), waits=[("ss", s3)],
                  inc="vv")
        s4 = P.op("scalar", lambda s: s.activation(s_sp3[:, :], s_sp2[:, :], AF.Ln),
                  waits=[("vv", v1)], inc="ss")
        vdl = P.op("vector", lambda v: v.tensor_tensor(s_del[:, :], s_sp1[:, :], s_sp3[:, :],
                                                       OP.add), waits=[("ss", s4)], inc="vv")
        va = P.op("vector", lambda v: v.tensor_scalar(s_acl[:, :], s_del[:, :], s_negA[:, :],
                                                      None, OP.mult), inc="vv")

        def wz_dma(zf, gz):
            P.op("sync", lambda sy, zf=zf, gz=gz: sy.dma_start(
                out=bass.AP(s_wzb, (gz % 3) * 1024, [[3072, 128], [128, 8], [1, 128]]),
                in_=bass.AP(d_wzs, zf * 1024 * 128, [[128, 128], [128 * 128, 8], [1, 128]])),
                waits=[("mm", z_ring.get(gz - 3, 0))], inc="lz", inc_n=16)

        def scan_chain(c):
            g = t * CPT + c
            first = (g % CHPS == 0)
            co = c * 128
            vcl = P.op("vector", lambda v, co=co: v.tensor_tensor_scan(
                s_clc[:, co:co + 128], s_ones8r[:, :], s_acl[:, co:co + 128], 0.0,
                OP.mult, OP.add), waits=[("vv", va)], inc="vv")
            m = P.op("tensor", lambda t_, co=co: t_.transpose(PB(1, 0, 8),
                                                              s_clc[:, co:co + 128],
                                                              s_ident[0:8, 0:8]),
                     waits=[("vv", vcl)] + BK.writer_waits([1]), inc="mm")
            vclT = P.op("vector", lambda v: v.tensor_copy(s_clT[:, :], PB(1, 0, 8)),
                        waits=[("mm", m)], inc="vv")
            BK.consumed([1], "vv", vclT)
            m = P.op("tensor", lambda t_, co=co: t_.transpose(PB(1, 8, 8),
                                                              s_del[:, co:co + 128],
                                                              s_ident[0:8, 0:8]),
                     waits=BK.writer_waits([1]) + [("vv", vdl)], inc="mm")
            vdT = P.op("vector", lambda v: v.tensor_copy(s_dT[:, :], PB(1, 8, 8)),
                       waits=[("mm", m)], inc="vv")
            BK.consumed([1], "vv", vdT)
            vw = P.op("vector", lambda v, co=co: v.tensor_scalar(
                s_w8[:, :], s_clc[:, co:co + 128], s_clc[:, co + 127:co + 128], None,
                OP.subtract), waits=[("vv", vcl)], inc="vv")
            sw = P.op("scalar", lambda s: s.activation(s_w8b[:, :], s_w8[:, :], AF.Exp,
                                                       scale=cc(CNEG1, 8)),
                      waits=[("vv", vw)], inc="ss")
            m = P.op("tensor", lambda t_: t_.transpose(PB(1, 16, 8), s_w8b[:, :],
                                                       s_ident[0:8, 0:8]),
                     waits=[("ss", sw)] + BK.writer_waits([1]), inc="mm")
            vw8 = P.op("vector", lambda v: v.tensor_copy(s_w8T[:, :], PB(1, 16, 8)),
                       waits=[("mm", m)], inc="vv")
            BK.consumed([1], "vv", vw8)
            m = P.op("tensor", lambda t_, co=co: t_.transpose(PB(1, 64, 64),
                                                              s_Bf[:, co:co + 128],
                                                              s_ident[0:64, 0:64]),
                     waits=BK.writer_waits([1]), inc="mm")
            vB = P.op("vector", lambda v: v.tensor_copy(s_Btm[:, :], PB(1, 64, 64)),
                      waits=[("mm", m)], inc="vv")
            BK.consumed([1], "vv", vB)
            mS = P.op("tensor", lambda t_, co=co: t_.matmul(
                PB(1, 128, 128), s_Bf[:, co:co + 128], s_Cf[:, co:co + 128],
                start=True, stop=True), waits=BK.writer_waits([1]), inc="mm")
            vSU = P.op("vector", lambda v: v.tensor_tensor(s_SU[:, :], PB(1, 128, 128),
                                                           s_u01[:, :], OP.mult),
                       waits=[("mm", mS)], inc="vv")
            BK.consumed([1], "vv", vSU)
            vde = P.op("vector", lambda v: v.tensor_tensor(s_dTE[:, :], s_dT[:, :],
                                                           s_hmE[:, :], OP.mult),
                       waits=[("vv", vdT)], inc="vv")
            vdo = P.op("vector", lambda v: v.tensor_tensor(s_dTO[:, :], s_dT[:, :],
                                                           s_hmO[:, :], OP.mult), inc="vv")
            vwd = P.op("vector", lambda v: v.tensor_tensor(s_wd[:, :], s_w8T[:, :],
                                                           s_dT[:, :], OP.mult),
                       waits=[("vv", vw8)], inc="vv")
            for fc in range(4):
                P.op("tensor", lambda t_, fc=fc, co=co: t_.transpose(
                    PB(0, fc * 128, 128), s_XT[:, fc * 256 + co: fc * 256 + co + 128],
                    s_ident[:, :]),
                    waits=(BK.writer_waits([0]) + [("vv", xe[3])]) if fc == 0 else [],
                    inc="mm" if fc == 3 else None)
            mX = P.count["mm"]
            vXE = P.op("vector", lambda v: v.tensor_tensor(
                s_XdE[:, :].rearrange("p (h d) -> p h d", h=8), PB(0, 0, 512),
                col_rep(s_dTE[:, :], 8, 64), OP.mult),
                waits=[("mm", mX), ("vv", vde)], inc="vv")
            vXO = P.op("vector", lambda v: v.tensor_tensor(
                s_XdO[:, :].rearrange("p (h d) -> p h d", h=8), PB(0, 0, 512),
                col_rep(s_dTO[:, :], 8, 64), OP.mult), waits=[("vv", vdo)], inc="vv")
            vWx = P.op("vector", lambda v: v.tensor_tensor(
                s_Wx[:, :].rearrange("p (h d) -> p h d", h=8), PB(0, 0, 512),
                col_rep(s_wd[:, :], 8, 64), OP.mult), waits=[("vv", vwd)], inc="vv")
            BK.consumed([0], "vv", vWx)
            mb8 = None
            for h in range(8):
                mb8 = P.op("tensor", lambda t_, h=h, co=co: t_.matmul(
                    PB(2 + h // 4, (h % 4) * 128, 128), s_hsel[:, h * 128:(h + 1) * 128],
                    s_clc[:, co:co + 128], start=True, stop=True),
                    waits=(BK.writer_waits([2, 3]) + [("vv", vcl)]) if h == 0 else [],
                    inc="mm" if h == 7 else None)
            se1 = P.op("scalar", lambda s: s.activation(s_G[0:64, 0:512], PB(2, 0, 512, 64),
                                                        AF.Exp), waits=[("mm", mb8)], inc="ss")
            se2 = P.op("scalar", lambda s: s.activation(s_G[0:64, 512:1024],
                                                        PB(3, 0, 512, 64), AF.Exp), inc="ss")
            vce = P.op("vector", lambda v, co=co: v.tensor_tensor(
                s_Cexp[:, :].rearrange("p (h d) -> p h d", h=8),
                rep(s_Cf[:, co:co + 128], 8, 128),
                s_G[0:64, :].rearrange("p (h d) -> p h d", h=8), OP.mult),
                waits=[("ss", se2)], inc="vv")
            vGs = []
            for gi in range(2):
                for hh in range(4):
                    h = gi * 4 + hh
                    P.op("vector", lambda v, h=h, hh=hh, gi=gi: v.scalar_tensor_tensor(
                        s_Epre[:, hh * 128:(hh + 1) * 128], PB(2 + gi, hh * 128, 128),
                        s_clT[:, h:h + 1], s_mbias[:, :], OP.subtract, OP.add),
                        waits=[("vv", vclT), ("mm", mb8)] if (gi == 0 and hh == 0) else [],
                        inc="vv")
                sE = P.op("scalar", lambda s: s.activation(s_Ehat[:, :], s_Epre[:, :],
                                                           AF.Exp),
                          waits=[("vv", P.count["vv"])], inc="ss")
                vG = P.op("vector", lambda v, gi=gi: v.tensor_tensor(
                    s_G[:, gi * 512:(gi + 1) * 512].rearrange("p (h d) -> p h d", h=4),
                    s_Ehat[:, :].rearrange("p (h d) -> p h d", h=4),
                    rep(s_SU[:, :], 4, 128), OP.mult),
                    waits=[("ss", sE), ("vv", vSU)], inc="vv")
                vGs.append(vG)
            ve2 = P.count["vv"]
            BK.consumed([2], "vv", ve2)
            BK.consumed([3], "vv", ve2)
            BK.consumed([2], "ss", se1)
            BK.consumed([3], "ss", se2)
            vpo = None
            if not first:
                P.op("vector", lambda v: v.tensor_tensor(
                    s_HPE[:, :].rearrange("p (h d) -> p h d", h=8),
                    s_SP[:, :].rearrange("p (h d) -> p h d", h=8),
                    col_rep(s_hmE[0:64, :], 8, 64), OP.mult), inc="vv")
                vpo = P.op("vector", lambda v: v.tensor_tensor(
                    s_HPO[:, :].rearrange("p (h d) -> p h d", h=8),
                    s_SP[:, :].rearrange("p (h d) -> p h d", h=8),
                    col_rep(s_hmO[0:64, :], 8, 64), OP.mult), inc="vv")
                P.op("gpsimd", lambda gp: gp.dma_start(out=s_cer[:, :],
                                                       in_=s_clT[127:128, :]),
                     waits=[("vv", vclT), ("ss", last_sce[0])], inc="gg", inc_n=16)
                sce = P.op("scalar", lambda s: s.activation(s_cer2[:, :], s_cer[:, :],
                                                            AF.Exp),
                           waits=[("gg", P.count["gg"])], inc="ss")
                last_sce[0] = sce
                mEH = P.op("tensor", lambda t_: t_.matmul(PB(1, 256, 8, 64),
                                                          s_onesr[0:1, 0:64], s_cer2[:, :],
                                                          start=True, stop=True),
                           waits=[("ss", sce)] + BK.writer_waits([1]), inc="mm")
                vsp = P.op("vector", lambda v: v.tensor_tensor(
                    s_SP2[:, :].rearrange("p (h d) -> p h d", h=8),
                    s_SP[:, :].rearrange("p (h d) -> p h d", h=8),
                    col_rep(PB(1, 256, 8, 64), 8, 64), OP.mult),
                    waits=[("mm", mEH), ("vv", vpo)], inc="vv")
                BK.consumed([1], "vv", vsp)
            return dict(first=first, co=co, vXO=vXO, vWx=vWx, vB=vB, vce=vce,
                        vGs=vGs, vpo=vpo)

        def z_half(h0):
            for zf in range(h0, h0 + 8):
                gz = t * 16 + zf
                zb = 6 + zf % 2
                for k in range(8):
                    P.op("tensor", lambda t_, k=k, zb=zb, gz=gz: t_.matmul(
                        PB(zb, 0, 256),
                        s_wzb[:, (gz % 3) * 1024 + k * 128: (gz % 3) * 1024 + k * 128 + 128],
                        s_h[:, k * 256:(k + 1) * 256], start=(k == 0), stop=(k == 7)),
                        waits=(BK.writer_waits([zb]) + [("lz", 16 * (gz + 1))]) if k == 0
                        else [], inc="mm" if k == 7 else None)
                mz = P.count["mm"]
                z_ring[gz] = mz
                if zf + 3 < 16:
                    wz_dma(zf + 3, t * 16 + zf + 3)
                elif t + 1 < n_tiles:
                    nzf = zf + 3 - 16
                    wz_dma(nzf, (t + 1) * 16 + nzf)
                sz = P.op("scalar", lambda s, zf=zf, zb=zb: s.activation(
                    A_zc()[:, (zf % 2) * 256:(zf % 2) * 256 + 256], PB(zb, 0, 256),
                    AF.Silu),
                    waits=[("mm", mz)] + ([("mm", zgp_mm.get(zf - 2, None) or 0)]
                                          if zf >= 2 else []), inc="ss")
                BK.consumed([zb], "ss", sz)
                for j in range(4):
                    bank, off = 4 + j // 2, (j % 2) * 256
                    P.op("tensor", lambda t_, zf=zf, j=j, bank=bank, off=off: t_.matmul(
                        PB(bank, off, 256),
                        s_wzg[:, zf * DI + j * 128: zf * DI + j * 128 + 128],
                        A_zc()[:, (zf % 2) * 256:(zf % 2) * 256 + 256],
                        start=True, stop=True),
                        waits=([("ss", sz)] if j == 0 else []) +
                              (BK.writer_waits([4, 5]) if j == 0 else []),
                        inc="mm" if j == 3 else None)
                zgp = P.count["mm"]
                zgp_mm[zf] = zgp
                src, dst = (A_zga, A_zgb) if zf % 2 == 1 else (A_zgb, A_zga)
                if zf == 0:
                    vz = P.op("vector", lambda v: v.tensor_copy(
                        A_zga(), ps[:, 4 * 512: 4 * 512 + 1024]),
                        waits=[("mm", zgp)], inc="vv")
                else:
                    vz = P.op("vector", lambda v, src=src, dst=dst: v.tensor_tensor(
                        dst(), src(), ps[:, 4 * 512: 4 * 512 + 1024], OP.add),
                        waits=[("mm", zgp)], inc="vv")
                BK.consumed([4, 5], "vv", vz)
            return P.count["vv"]

        def y_state(ch):
            first, co = ch["first"], ch["co"]
            for fc in range(4):
                hE, hO = 2 * fc, 2 * fc + 1
                w0 = (BK.writer_waits([4]) + [("vv", ch["vXO"]), ("vv", ch["vGs"][1]),
                                              ("vv", ch["vce"])]
                      + ([("vv", ch["vpo"])] if not first else [])) if fc == 0 else []
                P.op("tensor", lambda t_, fc=fc, hE=hE: t_.matmul(
                    PB(4, fc * 128, 128), s_XdE[:, fc * 128:(fc + 1) * 128],
                    s_G[:, hE * 128:(hE + 1) * 128], start=True, stop=False),
                    waits=w0, inc=None)
                P.op("tensor", lambda t_, fc=fc, hO=hO, first=first: t_.matmul(
                    PB(4, fc * 128, 128), s_XdO[:, fc * 128:(fc + 1) * 128],
                    s_G[:, hO * 128:(hO + 1) * 128], start=False, stop=first),
                    inc="mm" if (first and fc == 3) else None)
                if not first:
                    P.op("tensor", lambda t_, fc=fc, hE=hE: t_.matmul(
                        PB(4, fc * 128, 128), s_HPE[:, fc * 128:(fc + 1) * 128],
                        s_Cexp[:, hE * 128:(hE + 1) * 128], start=False, stop=False),
                        inc=None)
                    P.op("tensor", lambda t_, fc=fc, hO=hO: t_.matmul(
                        PB(4, fc * 128, 128), s_HPO[:, fc * 128:(fc + 1) * 128],
                        s_Cexp[:, hO * 128:(hO + 1) * 128], start=False, stop=True),
                        inc="mm" if fc == 3 else None)
            ym = P.count["mm"]
            for fc in range(4):
                P.op("tensor", lambda t_, fc=fc: t_.matmul(
                    PB(5, fc * 128, 128, 64), s_Btm[:, :], s_Wx[:, fc * 128:(fc + 1) * 128],
                    start=True, stop=True),
                    waits=(BK.writer_waits([5]) + [("vv", ch["vWx"]), ("vv", ch["vB"])])
                    if fc == 0 else [], inc="mm" if fc == 3 else None)
            mst = P.count["mm"]
            if not first:
                vH = P.op("vector", lambda v: v.tensor_tensor(s_SP[:, :], s_SP2[:, :],
                                                              PB(5, 0, 512, 64), OP.add),
                          waits=[("mm", mst)], inc="vv")
            else:
                vH = P.op("vector", lambda v: v.tensor_copy(s_SP[:, :], PB(5, 0, 512, 64)),
                          waits=[("mm", mst)], inc="vv")
            BK.consumed([5], "vv", vH)
            for fc in range(4):
                v = P.op("vector", lambda v, fc=fc, co=co: v.scalar_tensor_tensor(
                    s_yT[:, fc * 256 + co: fc * 256 + co + 128],
                    s_XT[:, fc * 256 + co: fc * 256 + co + 128],
                    s_drep[:, fc:fc + 1], PB(4, fc * 128, 128), OP.mult, OP.add),
                    waits=[("mm", ym)], inc="vv")
                BK.consumed([4], "vv", v)

        if t == 0:
            for i in range(3):
                wz_dma(i, i)
        zgp_mm = {}
        ch0 = scan_chain(0)
        z_half(0)
        y_state(ch0)
        ch1 = scan_chain(1)
        vzg = z_half(8)
        y_state(ch1)
        tap("yT", lambda: bass.AP(s_yT, 0, [[1024, 128], [1, 1024]]), t)

        vu = P.op("vector", lambda v: v.tensor_tensor(A_u2(), A_zgb(), s_yT[:, :],
                                                      OP.mult), inc="vv")
        vx2 = None
        for cC in range(8):
            bank = 4 + (cC % 2)
            for k in range(4):
                P.op("tensor", lambda t_, cC=cC, k=k, bank=bank: t_.matmul(
                    PB(bank, 0, 256), s_wout[:, k * D + cC * 128: k * D + cC * 128 + 128],
                    A_u2()[:, k * 256:(k + 1) * 256], start=(k == 0), stop=(k == 3)),
                    waits=(BK.writer_waits([bank]) + ([("vv", vu)] if cC == 0 else []))
                    if k == 0 else [], inc="mm" if k == 3 else None)
            ma = P.count["mm"]
            vx2 = P.op("vector", lambda v, cC=cC, bank=bank, s_xa=s_xa: v.tensor_tensor(
                s_x2[:, cC * 256:(cC + 1) * 256], PB(bank, 0, 256),
                s_xa[:, cC * 256:(cC + 1) * 256], OP.add), waits=[("mm", ma)], inc="vv")
            BK.consumed([bank], "vv", vx2)
        tap("u", lambda: bass.AP(s_big, 0, [[4096, 128], [1, 1024]]), t)
        tap("x2", lambda: s_x2[:, :], t)

        ln_stats(s_x2, wv=vx2)
        vh2 = ln_apply(s_x2, s_h, s_xa)
        tap("h2", lambda: s_h[:, :], t)

        for k in range(8):
            P.op("tensor", lambda t_, k=k: t_.matmul(
                PB(2, 0, 256, 64), s_qT[:, k * 64:(k + 1) * 64],
                s_h[:, k * 256:(k + 1) * 256], start=(k == 0), stop=(k == 7)),
                waits=(BK.writer_waits([2]) + [("vv", vh2)]) if k == 0 else [], inc=None)
            P.op("tensor", lambda t_, k=k: t_.matmul(
                PB(3, 0, 256, 8), s_csT[:, k * 8:(k + 1) * 8],
                s_h[:, k * 256:(k + 1) * 256], start=(k == 0), stop=(k == 7)),
                waits=BK.writer_waits([3]) if k == 0 else [],
                inc="mm" if k == 7 else None)
        msc = P.count["mm"]
        for hf in range(2):
            ho = hf * 128
            so_ = hf * 80
            vts = P.op("vector", lambda v, ho=ho: v.tensor_copy(A_ga()[0:64, 0:128],
                                                                PB(2, ho, 128, 64)),
                       waits=[("mm", msc)], inc="vv")
            vcs3 = P.op("vector", lambda v, ho=ho: v.tensor_copy(s_sq[0:8, 0:128],
                                                                 PB(3, ho, 128, 8)),
                        inc="vv")
            if hf == 1:
                BK.consumed([2], "vv", vcs3)
                BK.consumed([3], "vv", vcs3)
            mt1 = P.op("tensor", lambda t_: t_.transpose(PB(4, 0, 64), A_ga()[0:64, 0:128],
                                                         s_ident[0:64, 0:64]),
                       waits=[("vv", vts)] + BK.writer_waits([4]), inc="mm")
            mt2 = P.op("tensor", lambda t_: t_.transpose(PB(4, 64, 8), s_sq[0:8, 0:128],
                                                         s_ident[0:8, 0:8]),
                       waits=[("vv", vcs3)], inc="mm")
            vev = P.op("vector", lambda v, so_=so_: v.tensor_copy(A_sct()[:, so_:so_ + 72],
                                                                  PB(4, 0, 72)),
                       waits=[("mm", mt2)], inc="vv")
            BK.consumed([4], "vv", vev)
            vcl2 = P.op("vector", lambda v, so_=so_: v.tensor_scalar(
                A_ga()[:, 0:72], A_sct()[:, so_:so_ + 72], s_itcn[:, :], cc(C30),
                OP.mult, OP.min), waits=[("vv", vev)], inc="vv")
            sx = P.op("scalar", lambda s, so_=so_: s.activation(
                A_gn()[:, so_:so_ + 72], A_ga()[:, 0:72], AF.Exp),
                waits=[("vv", vcl2)], inc="ss")
            P.op("vector", lambda v, so_=so_: v.tensor_scalar(
                A_ga()[:, 0:72], A_gn()[:, so_:so_ + 72], cc(CONE), None, OP.add),
                waits=[("ss", sx)], inc="vv")
            P.op("vector", lambda v, so_=so_: v.reciprocal(A_gn()[:, so_:so_ + 72],
                                                           A_ga()[:, 0:72]), inc="vv")
            P.op("vector", lambda v, so_=so_: v.tensor_reduce(
                A_gm()[:, 0:1], A_gn()[:, so_ + 64:so_ + 72], AXL.X, OP.max), inc="vv")
            P.op("vector", lambda v, so_=so_: v.tensor_scalar(
                A_ohc()[:, :], A_gn()[:, so_ + 64:so_ + 72], A_gm()[:, 0:1], None,
                OP.is_equal), inc="vv")
            P.op("vector", lambda v: v.tensor_tensor(A_ohl()[:, :], A_ohc()[:, :],
                                                     s_rev8[:, :], OP.mult), inc="vv")
            P.op("vector", lambda v: v.tensor_reduce(A_gm()[:, 1:2], A_ohl()[:, :], AXL.X,
                                                     OP.max), inc="vv")
            P.op("vector", lambda v: v.tensor_scalar(A_ohc()[:, :], A_ohl()[:, :],
                                                     A_gm()[:, 1:2], None, OP.is_equal),
                 inc="vv")
            ohc_ap = A_ohc()
            gb_ap = A_gb()
            gt_ap = A_gt()
            P.op("vector", lambda v, so_=so_: v.tensor_tensor(
                A_gb()[:, :].rearrange("p (c j) -> p c j", c=8),
                A_gn()[:, so_:so_ + 64].rearrange("p (c j) -> p c j", c=8),
                bass.AP(ohc_ap.tensor, ohc_ap.offset, [ohc_ap.ap[0], [1, 8], [0, 8]]),
                OP.mult), inc="vv")
            P.op("vector", lambda v: v.tensor_reduce(
                A_selg()[:, :].rearrange("p (j o) -> p j o", o=1),
                bass.AP(gb_ap.tensor, gb_ap.offset, [gb_ap.ap[0], [1, 8], [8, 8]]),
                AXL.X, OP.add), inc="vv")
            P.op("vector", lambda v: v.tensor_reduce(A_gm()[:, 2:3], A_selg()[:, :], AXL.X,
                                                     OP.max), inc="vv")
            P.op("vector", lambda v: v.tensor_scalar(A_gt()[:, :], A_selg()[:, :],
                                                     A_gm()[:, 2:3], None, OP.is_equal),
                 inc="vv")
            P.op("vector", lambda v: v.tensor_tensor(A_ohl()[:, :], A_gt()[:, :],
                                                     s_rev8[:, :], OP.mult), inc="vv")
            P.op("vector", lambda v: v.tensor_reduce(A_gm()[:, 3:4], A_ohl()[:, :], AXL.X,
                                                     OP.max), inc="vv")
            P.op("vector", lambda v: v.tensor_scalar(A_gt()[:, :], A_ohl()[:, :],
                                                     A_gm()[:, 3:4], None, OP.is_equal),
                 inc="vv")
            voh = P.op("vector", lambda v: v.tensor_tensor(
                A_oht()[:, :].rearrange("p (c j) -> p c j", c=8),
                bass.AP(ohc_ap.tensor, ohc_ap.offset, [ohc_ap.ap[0], [1, 8], [0, 8]]),
                bass.AP(gt_ap.tensor, gt_ap.offset, [gt_ap.ap[0], [0, 8], [1, 8]]),
                OP.mult), inc="vv")
            mo = P.op("tensor", lambda t_: t_.transpose(PB(4, 128, 128, 64), A_oht()[:, :],
                                                        s_ident[:, :]),
                      waits=[("vv", voh)] + BK.writer_waits([4]), inc="mm")
            vof = P.op("vector", lambda v, ho=ho: v.tensor_copy(A_ohf()[0:64, ho:ho + 128],
                                                                PB(4, 128, 128, 64)),
                       waits=[("mm", mo)], inc="vv")
            BK.consumed([4], "vv", vof)
        vohf = P.count["vv"]
        tap("sct0", lambda: bass.AP(s_big, 2560, [[4096, 128], [1, 72]]), t)
        tap("sct1", lambda: bass.AP(s_big, 2640, [[4096, 128], [1, 72]]), t)
        tap("gn0", lambda: bass.AP(s_big, 2720, [[4096, 128], [1, 72]]), t)
        tap("gn1", lambda: bass.AP(s_big, 2800, [[4096, 128], [1, 72]]), t)
        tap("ohf", lambda: bass.AP(s_big, 3176, [[4096, 64], [1, 256]]), t)

        hcs = []
        for f in range(2):
            bank = 6 + f
            for k in range(8):
                P.op("tensor", lambda t_, f=f, k=k, bank=bank: t_.matmul(
                    PB(bank, 0, 256), s_cw1[:, k * 256 + f * 128: k * 256 + f * 128 + 128],
                    s_h[:, k * 256:(k + 1) * 256], start=(k == 0), stop=(k == 7)),
                    waits=BK.writer_waits([bank]) if k == 0 else [],
                    inc="mm" if k == 7 else None)
            mc = P.count["mm"]
            sg = P.op("scalar", lambda s, f=f, bank=bank: s.activation(
                A_hc()[:, f * 256:(f + 1) * 256], PB(bank, 0, 256), AF.Gelu,
                bias=s_cb1[:, f:f + 1]), waits=[("mm", mc)], inc="ss")
            hcs.append(sg)
            BK.consumed([bank], "ss", sg)
        for k in range(2):
            P.op("tensor", lambda t_, k=k: t_.matmul(
                PB(6, 0, 256, 2), s_cw2[:, k * 2:(k + 1) * 2],
                A_hc()[:, k * 256:(k + 1) * 256], start=(k == 0), stop=(k == 1)),
                waits=[("ss", hcs[k])] + (BK.writer_waits([6]) if k == 0 else []),
                inc="mm" if k == 1 else None)
        mcp = P.count["mm"]
        scp = P.op("scalar", lambda s: s.activation(A_comp()[:, :], PB(6, 0, 256, 2),
                                                    AF.Tanh, bias=s_cb2[:, :]),
                   waits=[("mm", mcp)], inc="ss")
        BK.consumed([6], "ss", scp)
        vM = []
        for i in range(2):
            bank = 6 + i
            mi = P.op("tensor", lambda t_, i=i, bank=bank: t_.matmul(
                PB(bank, 0, 256, 64), s_hsel[0:2, i * 128:i * 128 + 64],
                A_comp()[0:2, :], start=True, stop=True),
                waits=[("ss", scp)] + BK.writer_waits([bank]), inc="mm")
            v = P.op("vector", lambda v, i=i, bank=bank: v.tensor_tensor(
                (A_M0() if i == 0 else A_M1())[:, :], A_ohf(), PB(bank, 0, 256, 64),
                OP.mult), waits=[("mm", mi), ("vv", vohf)], inc="vv")
            BK.consumed([bank], "vv", v)
            vM.append(v)
        P.op("tensor", lambda t_: t_.matmul(PB(7, 0, 256, 16), s_w1a[:, :], A_M0(),
                                            start=True, stop=False),
             waits=[("vv", vM[1])] + BK.writer_waits([7]), inc=None)
        P.op("tensor", lambda t_: t_.matmul(PB(7, 0, 256, 16), s_w1b[:, :], A_M1(),
                                            start=False, stop=False), inc=None)
        mhh = P.op("tensor", lambda t_: t_.matmul(PB(7, 0, 256, 16), s_b1t[:, :], A_ohf(),
                                                  start=False, stop=True), inc="mm")
        shh = P.op("scalar", lambda s: s.activation(A_hh()[:, :], PB(7, 0, 256, 16),
                                                    AF.Relu), waits=[("mm", mhh)], inc="ss")
        BK.consumed([7], "ss", shh)
        mw2 = P.op("tensor", lambda t_: t_.matmul(PB(6, 0, 256, 16), s_w2t[:, :], A_ohf(),
                                                  start=True, stop=True),
                   waits=BK.writer_waits([6]), inc="mm")
        vt1 = P.op("vector", lambda v: v.tensor_tensor(A_t1()[:, :], A_hh(),
                                                       PB(6, 0, 256, 16), OP.mult),
                   waits=[("mm", mw2), ("ss", shh)], inc="vv")
        BK.consumed([6], "vv", vt1)
        P.op("tensor", lambda t_: t_.matmul(PB(5, 0, 256, 1), s_ones16[:, :], A_t1(),
                                            start=True, stop=False),
             waits=[("vv", vt1)] + BK.writer_waits([5]), inc=None)
        mmg = P.op("tensor", lambda t_: t_.matmul(PB(5, 0, 256, 1), s_b2t[:, :], A_ohf(),
                                                  start=False, stop=True), inc="mm")
        vmg = P.op("vector", lambda v: v.tensor_copy(A_magr()[:, :], PB(5, 0, 256, 1)),
                   waits=[("mm", mmg)], inc="vv")
        BK.consumed([5], "vv", vmg)
        mgb = P.op("tensor", lambda t_: t_.matmul(PB(5, 0, 256, 64), s_onesr[0:1, 0:64],
                                                  A_magr(), start=True, stop=True),
                   waits=[("vv", vmg)] + BK.writer_waits([5]), inc="mm")
        vso = P.op("vector", lambda v: v.tensor_tensor(A_soh()[:, :], A_ohf(),
                                                       PB(5, 0, 256, 64), OP.mult),
                   waits=[("mm", mgb)], inc="vv")
        BK.consumed([5], "vv", vso)
        tap("soh", lambda: bass.AP(s_big, 3432, [[4096, 64], [1, 256]]), t)
        vyp = None
        for cC in range(8):
            bank = 4 + (cC % 2)
            mD = P.op("tensor", lambda t_, cC=cC, bank=bank: t_.matmul(
                PB(bank, 0, 256), s_dirss[:, cC * 128:(cC + 1) * 128], A_soh(),
                start=True, stop=True),
                waits=[("vv", vso)] + BK.writer_waits([bank]), inc="mm")
            vyp = P.op("vector", lambda v, cC=cC, bank=bank, s_xa=s_xa: v.tensor_tensor(
                s_xa[:, cC * 256:(cC + 1) * 256], PB(bank, 0, 256),
                s_h[:, cC * 256:(cC + 1) * 256], OP.add), waits=[("mm", mD)], inc="vv")
            BK.consumed([bank], "vv", vyp)
        tap("ypre", lambda: s_xa[:, :], t)

        ln_stats(s_xa, wv=vyp)
        vmr = P.op("vector", lambda v: v.tensor_tensor(s_mrb[:, :], s_mb[:, :], s_rb[:, :],
                                                       OP.mult), inc="vv")
        vt2 = ln_apply(s_xa, s_h, A_tmp3())
        P.op("vector", lambda v: v.nop(), waits=[("vv", vt2)], inc="xf")
        for cC in range(8):
            P.op("vector", lambda v, cC=cC: v.tensor_scalar(
                s_vb[:, :], s_mrb[:, :], s_gamc[:, cC:cC + 1], None, OP.mult),
                waits=[("vv", vmr)], inc="vv")
            P.op("vector", lambda v, cC=cC: v.scalar_tensor_tensor(
                s_wfin[:, :], s_x2[:, cC * 256:(cC + 1) * 256], s_betc[:, cC:cC + 1],
                s_vb[:, :], OP.add, OP.subtract), inc="vv")
            P.op("vector", lambda v, cC=cC: v.scalar_tensor_tensor(
                s_x2[:, cC * 256:(cC + 1) * 256], s_h[:, cC * 256:(cC + 1) * 256],
                s_gamc[:, cC:cC + 1], s_wfin[:, :], OP.mult, OP.add), inc="vv")
        vfin = P.count["vv"]
        tap("out", lambda: s_x2[:, :], t)

        st_count = P.op("gpsimd", lambda g, t=t: g.dma_start(
            out=bass.AP(d_out, t * TT, [[RTOK, 128], [128 * RTOK, 8], [1, TT]]),
            in_=s_x2[:, :].rearrange("p (c f) -> p c f", c=8)),
            waits=[("vv", vfin)], inc="st", inc_n=16)

    P.op("gpsimd", lambda g: None, waits=[("st", P.count["st"])])

    sems = {}
    for s in P.count:
        sems[s] = es.enter_context(nc.semaphore(s))
    with nc.Block() as block:
        @block.tensor
        def _(eng):
            P.emit("tensor", eng, sems)

        @block.vector
        def _(eng):
            P.emit("vector", eng, sems)

        @block.scalar
        def _(eng):
            P.emit("scalar", eng, sems)

        @block.gpsimd
        def _(eng):
            P.emit("gpsimd", eng, sems)

        @block.sync
        def _(eng):
            P.emit("sync", eng, sems)

    nc._keepalive = es
    return nc


def make_inmaps(inputs):
    f32 = np.float32
    x = np.asarray(inputs["L0_attn_x"], f32)
    wz = np.asarray(inputs["L0_attn_Wz"], f32)
    wzs = np.ascontiguousarray(
        wz.reshape(8, 128, 16, 128).transpose(2, 0, 1, 3).reshape(16 * 8 * 128, 128))
    win = np.asarray(inputs["L0_attn_Win"], f32)
    win_r = np.ascontiguousarray(np.concatenate(
        [win[:, 136:648], win[:, 0:64], win[:, 64:128], win[:, 128:136]], axis=1))
    sw1 = np.asarray(inputs["L0_ffn_spline_w1"], f32)
    hm = np.zeros((128, 8), f32)
    hm[:, 0::2] = 1.0
    common = {
        "wzs": wzs,
        "win": win_r,
        "wzg": np.ascontiguousarray(np.asarray(inputs["L0_attn_Wzg"], f32)),
        "wout": np.ascontiguousarray(np.asarray(inputs["L0_attn_Wout"], f32)),
        "cw1": np.ascontiguousarray(np.asarray(inputs["L0_ffn_compress_w1"], f32)),
        "cw2": np.ascontiguousarray(np.asarray(inputs["L0_ffn_compress_w2"], f32)),
        "w1a": np.ascontiguousarray(sw1[:, 0, :]),
        "w1b": np.ascontiguousarray(sw1[:, 1, :]),
        "b1t": np.ascontiguousarray(np.asarray(inputs["L0_ffn_spline_b1"], f32)),
        "w2t": np.ascontiguousarray(np.asarray(inputs["L0_ffn_spline_w2"], f32)[:, :, 0]),
        "b2t": np.ascontiguousarray(np.asarray(inputs["L0_ffn_spline_b2"], f32)),
        "dirs": np.ascontiguousarray(np.asarray(inputs["L0_ffn_directions"], f32)),
        "sigs": np.ascontiguousarray(np.asarray(inputs["L0_ffn_signatures_raw"], f32)),
        "csel": np.ascontiguousarray(
            (np.arange(64)[:, None] // 8 == np.arange(8)[None, :]).astype(f32)),
        "dtb": np.asarray(inputs["L0_attn_dt_bias"], f32).reshape(8, 1),
        "alog": np.asarray(inputs["L0_attn_A_log"], f32).reshape(8, 1),
        "drep": np.ascontiguousarray(
            np.repeat(np.asarray(inputs["L0_attn_D_param"], f32), 64).reshape(4, 128).T),
        "gamc": np.ascontiguousarray(
            np.asarray(inputs["L0_ffn_norm_gamma"], f32).reshape(8, 128).T),
        "betc": np.ascontiguousarray(
            np.asarray(inputs["L0_ffn_norm_beta"], f32).reshape(8, 128).T),
        "cb1": np.ascontiguousarray(
            np.asarray(inputs["L0_ffn_compress_b1"], f32).reshape(2, 128).T),
        "cb2": np.asarray(inputs["L0_ffn_compress_b2"], f32).reshape(2, 1),
        "temp": np.asarray(inputs["L0_ffn_temperature"], f32).reshape(1, 1),
        "oscale": np.asarray(inputs["L0_ffn_output_scale"], f32).reshape(1, 1),
        "ident": np.eye(128, dtype=f32),
        "rev8": np.tile(8.0 - np.arange(8, dtype=f32), (128, 1)),
        "u01": (np.arange(128)[None, :] >= np.arange(128)[:, None]).astype(f32),
        "mbias": np.where(np.arange(128)[None, :] >= np.arange(128)[:, None],
                          0.0, MASKNEG).astype(f32),
        "onesr": np.ones((1, 256), f32),
        "ones8r": np.ones((8, 128), f32),
        "onespad": np.concatenate([np.ones((128, 1), f32), np.zeros((128, 127), f32)], 1),
        "ones16": np.ones((16, 1), f32),
        "cc": np.tile(np.array([[1.0 / 1024, EPS, -1.0, 0.0, 1.0, 1e-6, 0.3, -0.3,
                                 30.0, -0.5]], f32), (128, 1)),
        "hmE": hm,
        "hmO": (1.0 - hm).astype(f32),
        "hsel": np.ascontiguousarray(np.repeat(np.eye(8, dtype=f32), 128, axis=1)),
    }
    in_maps = []
    for i in range(NCORES):
        m = dict(common)
        m["x"] = np.ascontiguousarray(x[BPC * i:BPC * (i + 1)].reshape(RTOK, D).T)
        in_maps.append(m)
    return in_maps


_NC_CACHE = {}


def kernel(**inputs):
    if "full" not in _NC_CACHE:
        _NC_CACHE["full"] = build_program()
    nc = _NC_CACHE["full"]
    in_maps = make_inmaps(inputs)
    res = run_bass_kernel_spmd(nc, in_maps, core_ids=list(range(NCORES)))
    outs = []
    for i in range(NCORES):
        o = np.asarray(res.results[i]["out"])
        outs.append(o.T.reshape(BPC, SEQ, D))
    return np.concatenate(outs, axis=0)


# version 06
# speedup vs baseline: 1.2008x; 1.1032x over previous
import contextlib
import numpy as np
import concourse.bass as bass
import concourse.mybir as mybir
from concourse.bass_utils import run_bass_kernel_spmd

F32 = mybir.dt.float32
AF = mybir.ActivationFunctionType
OP = mybir.AluOpType
AXL = mybir.AxisListType

NCORES = 8
BATCH, SEQ, D = 32, 2048, 1024
BPC = BATCH // NCORES
RTOK = BPC * SEQ
TT = 256
NTILE = RTOK // TT
Q = 128
CPT = TT // Q
NH, HD = 8, 64
DI = NH * HD
DZ = 2 * D
EPS = 1e-5
MASKNEG = -30000.0
CHPS = SEQ // Q

C1024, CEPS, CNEG1, CZERO, CONE, C1E6, CTH, CNTH, C30, CNEGH = 0, 1, 2, 3, 4, 5, 6, 7, 8, 9
CHALF = 10


class Prog:
    ENGINES = ("tensor", "vector", "scalar", "gpsimd", "sync")

    def __init__(self):
        self.ops = {e: [] for e in self.ENGINES}
        self.count = {}
        self.hw = {e: {} for e in self.ENGINES}

    def sem(self, name):
        self.count.setdefault(name, 0)

    def op(self, eng, fn, waits=(), inc=None, inc_n=1):
        ws = []
        for s, c in waits:
            if c is None or c <= 0:
                continue
            assert c <= self.count[s], f"wait {s}>={c}, recorded {self.count[s]}"
            if self.hw[eng].get(s, -1) < c:
                ws.append((s, c))
                self.hw[eng][s] = c
        if inc is not None:
            self.count[inc] += inc_n
        self.ops[eng].append((fn, ws, inc, inc_n))
        return self.count[inc] if inc is not None else None

    def emit(self, eng_name, eng, sems):
        for fn, ws, inc, inc_n in self.ops[eng_name]:
            for s, c in ws:
                eng.wait_ge(sems[s], c)
            instr = fn(eng)
            if inc is not None:
                assert instr is not None, f"inc={inc} op returned None on {eng_name}"
                instr.then_inc(sems[inc], inc_n)


class Banks:
    def __init__(self):
        self.pending = {b: [] for b in range(8)}

    def writer_waits(self, banks):
        ws = []
        for b in banks:
            ws += self.pending[b]
            self.pending[b] = []
        return ws

    def consumed(self, banks, sem, count):
        for b in banks:
            self.pending[b].append((sem, count))


def rep(ap, nrep, inner):
    return bass.AP(ap.tensor, ap.offset, [ap.ap[0], [0, nrep], [ap.ap[-1][0], inner]])


def col_rep(ap, ncols, inner):
    return bass.AP(ap.tensor, ap.offset, [ap.ap[0], [ap.ap[-1][0], ncols], [0, inner]])


def build_program(n_tiles=NTILE, tap_names=()):
    nc = bass.Bass("TRN2", target_bir_lowering=False, detect_race_conditions=False)

    def dram_in(name, shape, dt=F32):
        return nc.dram_tensor(name, list(shape), dt, kind="ExternalInput")

    d_x = dram_in("x", [D, RTOK])
    d_wzs = dram_in("wzs", [16 * 8 * 128, 128])
    d_win = dram_in("win", [D, 648])
    d_wzg = dram_in("wzg", [DZ, DI])
    d_wout = dram_in("wout", [DI, D])
    d_cw1 = dram_in("cw1", [D, 256])
    d_cw2 = dram_in("cw2", [256, 2])
    d_w1a = dram_in("w1a", [64, 16])
    d_w1b = dram_in("w1b", [64, 16])
    d_b1t = dram_in("b1t", [64, 16])
    d_w2t = dram_in("w2t", [64, 16])
    d_b2t = dram_in("b2t", [64, 1])
    d_dirs = dram_in("dirs", [64, D])
    d_sigs = dram_in("sigs", [64, D])
    d_csel = dram_in("csel", [64, 8])
    d_dtb = dram_in("dtb", [8, 1])
    d_alog = dram_in("alog", [8, 1])
    d_drep = dram_in("drep", [128, 4])
    d_gamc = dram_in("gamc", [128, 8])
    d_betc = dram_in("betc", [128, 8])
    d_cb1 = dram_in("cb1", [128, 2])
    d_cb2 = dram_in("cb2", [2, 1])
    d_temp = dram_in("temp", [1, 1])
    d_oscale = dram_in("oscale", [1, 1])
    d_ident = dram_in("ident", [128, 128])
    d_rev8 = dram_in("rev8", [128, 8])
    d_u01 = dram_in("u01", [128, 128])
    d_mbias = dram_in("mbias", [128, 128])
    d_onesr = dram_in("onesr", [1, 256])
    d_ones8r = dram_in("ones8r", [8, 128])
    d_onespad = dram_in("onespad", [128, 128])
    d_ones16 = dram_in("ones16", [16, 1])
    d_cc = dram_in("cc", [128, 12])
    d_hmE = dram_in("hmE", [128, 8])
    d_hmO = dram_in("hmO", [128, 8])
    d_hsel = dram_in("hsel", [8, 1024])
    d_out = nc.dram_tensor("out", [D, RTOK], F32, kind="ExternalOutput")
    d_tap = {t: nc.dram_tensor(f"tap_{t}", [128, 2048], F32, kind="ExternalOutput")
             for t in tap_names}

    P = Prog()
    for s in ("ldw", "ldx", "lz", "mm", "vv", "ss", "gg", "st", "xf"):
        P.sem(s)
    BK = Banks()

    es = contextlib.ExitStack()

    def sbuf(name, shape, dt=F32):
        return es.enter_context(nc.sbuf_tensor(name, shape, dt))

    s_wzb = sbuf("s_wzb", [128, 3072])
    s_win = sbuf("s_win", [128, 8 * 648])
    s_wzg = sbuf("s_wzg", [128, 16 * DI])
    s_wout = sbuf("s_wout", [128, 4 * D])
    s_cw1 = sbuf("s_cw1", [128, 8 * 256])
    s_cw2 = sbuf("s_cw2", [128, 4])
    s_w1a = sbuf("s_w1a", [64, 16])
    s_w1b = sbuf("s_w1b", [64, 16])
    s_b1t = sbuf("s_b1t", [64, 16])
    s_w2t = sbuf("s_w2t", [64, 16])
    s_b2t = sbuf("s_b2t", [64, 1])
    s_dirss = sbuf("s_dirss", [64, D])
    s_csel = sbuf("s_csel", [64, 8])
    s_qT = sbuf("s_qT", [128, 8 * 64])
    s_csT = sbuf("s_csT", [128, 8 * 8])
    s_dtb = sbuf("s_dtb", [8, 1])
    s_alog = sbuf("s_alog", [8, 1])
    s_negA = sbuf("s_negA", [8, 1])
    s_eA = sbuf("s_eA", [8, 1])
    s_drep = sbuf("s_drep", [128, 4])
    s_gamc = sbuf("s_gamc", [128, 8])
    s_betc = sbuf("s_betc", [128, 8])
    s_cb1 = sbuf("s_cb1", [128, 2])
    s_cb2 = sbuf("s_cb2", [2, 1])
    s_temp = sbuf("s_temp", [1, 1])
    s_it1 = sbuf("s_it1", [1, 1])
    s_it2 = sbuf("s_it2", [1, 1])
    s_itcn = sbuf("s_itcn", [128, 1])
    s_osc = sbuf("s_osc", [64, 1])
    s_ident = sbuf("s_ident", [128, 128])
    s_rev8 = sbuf("s_rev8", [128, 8])
    s_u01 = sbuf("s_u01", [128, 128])
    s_mbias = sbuf("s_mbias", [128, 128])
    s_onesr = sbuf("s_onesr", [1, 256])
    s_ones8r = sbuf("s_ones8r", [8, 128])
    s_onespad = sbuf("s_onespad", [128, 128])
    s_ones16 = sbuf("s_ones16", [16, 1])
    s_cc = sbuf("s_cc", [128, 12])
    s_hmE = sbuf("s_hmE", [128, 8])
    s_hmO = sbuf("s_hmO", [128, 8])
    s_hsel = sbuf("s_hsel", [8, 1024])

    s_xaA = sbuf("s_xaA", [128, 2048])
    s_xaB = sbuf("s_xaB", [128, 2048])
    s_x2 = sbuf("s_x2", [128, 2048])
    s_h = sbuf("s_h", [128, 2048])
    s_big = sbuf("s_big", [128, 4096])
    s_XT = sbuf("s_XT", [128, 1024])
    s_Bf = sbuf("s_Bf", [64, 256])
    s_Cf = sbuf("s_Cf", [64, 256])
    s_dtf = sbuf("s_dtf", [8, 256])
    s_sp1 = sbuf("s_sp1", [8, 256])
    s_sp2 = sbuf("s_sp2", [8, 256])
    s_sp3 = sbuf("s_sp3", [8, 256])
    s_del = sbuf("s_del", [8, 256])
    s_acl = sbuf("s_acl", [8, 256])
    s_clc = sbuf("s_clc", [8, 256])
    s_clT = sbuf("s_clT", [128, 8])
    s_dT = sbuf("s_dT", [128, 8])
    s_dTE = sbuf("s_dTE", [128, 8])
    s_dTO = sbuf("s_dTO", [128, 8])
    s_wd = sbuf("s_wd", [128, 8])
    s_w8 = sbuf("s_w8", [8, 128])
    s_w8b = sbuf("s_w8b", [8, 128])
    s_w8T = sbuf("s_w8T", [128, 8])
    s_cer = sbuf("s_cer", [1, 8])
    s_cer2 = sbuf("s_cer2", [1, 8])
    s_Btm = sbuf("s_Btm", [128, 64])
    s_XdE = sbuf("s_XdE", [128, 512])
    s_XdO = sbuf("s_XdO", [128, 512])
    s_Wx = sbuf("s_Wx", [128, 512])
    s_Epre = sbuf("s_Epre", [128, 512])
    s_Ehat = sbuf("s_Ehat", [128, 512])
    s_G = sbuf("s_G", [128, 1024])
    s_SU = sbuf("s_SU", [128, 128])
    s_Cexp = sbuf("s_Cexp", [64, 1024])
    s_SP = sbuf("s_SP", [64, 512])
    s_SP2 = sbuf("s_SP2", [64, 512])
    s_HPE = sbuf("s_HPE", [64, 512])
    s_HPO = sbuf("s_HPO", [64, 512])
    s_yT = sbuf("s_yT", [128, 1024])
    s_sg = sbuf("s_sg", [128, 512])
    s_sq = sbuf("s_sq", [128, 512])
    s_srow = sbuf("s_srow", [1, 512])
    s_mb = sbuf("s_mb", [128, 256])
    s_rb = sbuf("s_rb", [128, 256])
    s_vb = sbuf("s_vb", [128, 256])
    s_mrb = sbuf("s_mrb", [128, 256])
    s_wfin = sbuf("s_wfin", [128, 256])

    A_sigs = lambda: s_big[0:64, 0:1024]
    A_q = lambda: s_big[0:64, 1024:2048]
    A_cs = lambda: s_big[0:8, 2048:3072]
    A_dirs = lambda: s_big[0:64, 3072:4096]
    A_u = lambda: s_big[:, 0:1024]
    A_zc = lambda: s_big[:, 1024:1536]
    A_hc = lambda: s_big[:, 1536:2048]
    A_M0 = lambda: s_big[0:64, 2048:2304]
    A_M1 = lambda: s_big[0:64, 2304:2560]
    A_magr = lambda: s_big[0:1, 2048:2304]
    A_sct = lambda: s_big[:, 2560:2720]
    A_gn = lambda: s_big[:, 2720:2880]
    A_ga = lambda: s_big[:, 2880:3008]
    A_gb = lambda: s_big[:, 3008:3072]
    A_gm = lambda: s_big[:, 3072:3080]
    A_selg = lambda: s_big[:, 3080:3088]
    A_ohc = lambda: s_big[:, 3088:3096]
    A_ohl = lambda: s_big[:, 3096:3104]
    A_gt = lambda: s_big[:, 3104:3112]
    A_oht = lambda: s_big[:, 3112:3176]
    A_ohf = lambda: s_big[0:64, 3176:3432]
    A_soh = lambda: s_big[0:64, 3432:3688]
    A_comp = lambda: s_big[0:2, 3688:3944]
    A_hh = lambda: s_big[0:16, 2560:2816]
    A_t1 = lambda: s_big[0:16, 2816:3072]
    A_tmp3 = lambda: s_big
    A_zga = lambda: s_big[:, 0:1024]
    A_zgb = lambda: s_big[0:128, 2048:3072]
    A_u2 = lambda: s_big[:, 1024:2048]

    ps = es.enter_context(nc.psum_tensor("ps", [128, 4096], mybir.dt.float32))

    def PB(bank, off=0, w=512, rows=128, r0=0):
        return ps[r0:r0 + rows, bank * 512 + off: bank * 512 + off + w]

    cc = lambda i, p=128: s_cc[0:p, i:i + 1]

    nw = [0]

    def wload(fn):
        nw[0] += 1
        P.op("sync", fn, inc="ldw", inc_n=16)

    for k in range(8):
        wload(lambda sy, k=k: sy.dma_start(out=s_win[:, k * 648:(k + 1) * 648],
                                           in_=d_win[k * 128:(k + 1) * 128, :]))
    for k in range(16):
        wload(lambda sy, k=k: sy.dma_start(out=s_wzg[:, k * DI:(k + 1) * DI],
                                           in_=d_wzg[k * 128:(k + 1) * 128, :]))
    for k in range(4):
        wload(lambda sy, k=k: sy.dma_start(out=s_wout[:, k * D:(k + 1) * D],
                                           in_=d_wout[k * 128:(k + 1) * 128, :]))
    for k in range(8):
        wload(lambda sy, k=k: sy.dma_start(out=s_cw1[:, k * 256:(k + 1) * 256],
                                           in_=d_cw1[k * 128:(k + 1) * 128, :]))
    for k in range(2):
        wload(lambda sy, k=k: sy.dma_start(out=s_cw2[:, k * 2:(k + 1) * 2],
                                           in_=d_cw2[k * 128:(k + 1) * 128, :]))
    wload(lambda sy: sy.dma_start(out=A_sigs(), in_=d_sigs[:, :]))
    wload(lambda sy: sy.dma_start(out=A_dirs(), in_=d_dirs[:, :]))
    for a, b in [(s_w1a, d_w1a), (s_w1b, d_w1b), (s_b1t, d_b1t), (s_w2t, d_w2t),
                 (s_b2t, d_b2t), (s_csel, d_csel), (s_dtb, d_dtb), (s_alog, d_alog),
                 (s_drep, d_drep), (s_gamc, d_gamc), (s_betc, d_betc), (s_cb1, d_cb1),
                 (s_cb2, d_cb2), (s_temp, d_temp), (s_ident, d_ident), (s_rev8, d_rev8),
                 (s_u01, d_u01), (s_mbias, d_mbias), (s_onesr, d_onesr),
                 (s_ones8r, d_ones8r), (s_onespad, d_onespad), (s_ones16, d_ones16),
                 (s_cc, d_cc), (s_hmE, d_hmE), (s_hmO, d_hmO), (s_hsel, d_hsel)]:
        wload(lambda sy, a=a, b=b: sy.dma_start(out=a[:], in_=b[:]))
    wload(lambda sy: sy.dma_start(out=s_osc[:, :],
                                  in_=bass.AP(d_oscale, 0, [[0, 64], [1, 1]])))
    LDW = nw[0] * 16
    WL = [("ldw", LDW)]

    P.op("vector", lambda v: v.tensor_scalar(s_G[0:64, 0:1024], A_sigs(), cc(CTH, 64),
                                             None, OP.is_gt), waits=WL, inc="vv")
    P.op("vector", lambda v: v.tensor_scalar(s_yT[0:64, 0:1024], A_sigs(), cc(CNTH, 64),
                                             None, OP.is_lt), inc="vv")
    vq = P.op("vector", lambda v: v.tensor_tensor(A_q(), s_G[0:64, 0:1024],
                                                  s_yT[0:64, 0:1024], OP.subtract), inc="vv")
    for half in range(2):
        m = P.op("tensor", lambda t, h=half: t.matmul(PB(0, 0, 512, 8), s_csel[:, :],
                                                      A_q()[:, h * 512:(h + 1) * 512],
                                                      start=True, stop=True),
                 waits=[("vv", vq)] + BK.writer_waits([0]), inc="mm")
        P.op("vector", lambda v: v.tensor_scalar(s_sq[0:8, 0:512], PB(0, 0, 512, 8),
                                                 cc(CZERO, 8), None, OP.is_gt),
             waits=[("mm", m)], inc="vv")
        P.op("vector", lambda v: v.tensor_scalar(s_yT[0:8, 0:512], PB(0, 0, 512, 8),
                                                 cc(CZERO, 8), None, OP.is_lt),
             inc="vv")
        e = P.op("vector", lambda v, h=half: v.tensor_tensor(
            A_cs()[:, h * 512:(h + 1) * 512], s_sq[0:8, 0:512], s_yT[0:8, 0:512],
            OP.subtract), inc="vv")
        BK.consumed([0], "vv", e)
    for c in range(8):
        m = P.op("tensor", lambda t, c=c: t.transpose(PB(1, 0, 64),
                                                      A_q()[:, c * 128:(c + 1) * 128],
                                                      s_ident[0:64, 0:64]),
                 waits=BK.writer_waits([1]), inc="mm")
        v = P.op("vector", lambda v, c=c: v.tensor_copy(s_qT[:, c * 64:(c + 1) * 64],
                                                        PB(1, 0, 64)),
                 waits=[("mm", m)], inc="vv")
        BK.consumed([1], "vv", v)
        m = P.op("tensor", lambda t, c=c: t.transpose(PB(1, 64, 8),
                                                      A_cs()[:, c * 128:(c + 1) * 128],
                                                      s_ident[0:8, 0:8]),
                 waits=BK.writer_waits([1]), inc="mm")
        v = P.op("vector", lambda v, c=c: v.tensor_copy(s_csT[:, c * 8:(c + 1) * 8],
                                                        PB(1, 64, 8)),
                 waits=[("mm", m)], inc="vv")
        BK.consumed([1], "vv", v)
    e = P.op("scalar", lambda s: s.activation(s_eA[:, :], s_alog[:, :], AF.Exp),
             waits=WL, inc="ss")
    P.op("vector", lambda v: v.tensor_scalar(s_negA[:, :], s_eA[:, :], cc(CNEG1, 8), None,
                                             OP.mult), waits=[("ss", e)], inc="vv")
    P.op("vector", lambda v: v.tensor_scalar(s_it1[:, :], s_temp[:, :], cc(C1E6, 1), None,
                                             OP.add), waits=WL, inc="vv")
    vi = P.op("vector", lambda v: v.reciprocal(s_it2[:, :], s_it1[:, :]), inc="vv")
    m = P.op("tensor", lambda t: t.matmul(PB(1, 128, 1), s_onesr[0:1, 0:128], s_it2[:, :],
                                          start=True, stop=True),
             waits=[("vv", vi)] + BK.writer_waits([1]), inc="mm")
    v = P.op("vector", lambda v: v.tensor_scalar(s_itcn[:, :], PB(1, 128, 1), cc(CNEG1),
                                                 None, OP.mult), waits=[("mm", m)], inc="vv")
    BK.consumed([1], "vv", v)
    P.op("vector", lambda v: v.tensor_scalar(s_dirss[:, :], A_dirs(), s_osc[:, :],
                                             None, OP.mult), waits=WL, inc="vv")

    def ln_stats(src, wv=None):
        w01 = BK.writer_waits([0, 1])
        last = None
        f32r = mybir.dt.float32r
        for c in range(8):
            sqw = [("mm", sq_slot[c % 2])] if sq_slot[c % 2] else []
            if c < 2 and wv is not None:
                sqw.append(("vv", wv))
            sc = P.op("scalar", lambda s, c=c: s.activation(
                s_sq[:, (c % 2) * 256:(c % 2) * 256 + 256], src[:, c * 256:(c + 1) * 256],
                AF.Square), waits=sqw, inc="ss")
            P.op("tensor", lambda t, c=c: t.matmul(PB(0, 0, 256), s_onespad[:, :],
                                                   src[:, c * 256:(c + 1) * 256],
                                                   start=(c == 0), stop=(c == 7)),
                 waits=(w01 + ([("vv", wv)] if wv is not None else [])) if c == 0 else [],
                 inc=None)
            last = P.op("tensor", lambda t, c=c: t.matmul(
                PB(1, 0, 256), s_onespad[:, :],
                s_sq[:, (c % 2) * 256:(c % 2) * 256 + 256],
                start=(c == 0), stop=(c == 7)), waits=[("ss", sc)], inc="mm")
            sq_slot[c % 2] = last
        vr = P.op("vector", lambda v: v.tensor_copy(s_srow[:, 0:256], PB(0, 0, 256, 1)),
                  waits=[("mm", last)], inc="vv")
        vr2 = P.op("vector", lambda v: v.tensor_copy(s_srow[:, 256:512], PB(1, 0, 256, 1)),
                   inc="vv")
        BK.consumed([0], "vv", vr)
        BK.consumed([1], "vv", vr2)
        mb = P.op("tensor", lambda t: t.matmul(PB(0, 0, 512), s_onesr[0:1, 0:128],
                                               s_srow[:, :], start=True, stop=True),
                  waits=[("vv", vr2)] + BK.writer_waits([0]), inc="mm")
        P.op("vector", lambda v: v.tensor_scalar(s_mb[:, :], PB(0, 0, 256), cc(C1024),
                                                 None, OP.mult), waits=[("mm", mb)], inc="vv")
        P.op("vector", lambda v: v.tensor_tensor(s_vb[:, :], s_mb[:, :], s_mb[:, :],
                                                 OP.mult), inc="vv")
        vv = P.op("vector", lambda v: v.scalar_tensor_tensor(s_rb[:, :], PB(0, 256, 256),
                                                             cc(C1024), s_vb[:, :],
                                                             OP.mult, OP.subtract), inc="vv")
        BK.consumed([0], "vv", vv)
        sv = P.op("scalar", lambda s: s.activation(s_vb[:, :], s_rb[:, :], AF.Ln,
                                                   bias=cc(CEPS)), waits=[("vv", vv)],
                  inc="ss")
        sv2 = P.op("scalar", lambda s: s.activation(s_rb[:, :], s_vb[:, :], AF.Exp,
                                                    scale=cc(CNEGH)), inc="ss")
        return P.op("vector", lambda v: v.nop(), waits=[("ss", sv2)], inc="vv")

    def ln_apply(src, dst, tmp, extra=()):
        P.op("vector", lambda v: v.tensor_tensor(
            tmp[:, 0:2048].rearrange("p (c f) -> p c f", c=8),
            src[:, :].rearrange("p (c f) -> p c f", c=8),
            rep(s_mb[:, :], 8, 256), OP.subtract),
            waits=list(extra), inc="vv")
        return P.op("vector", lambda v: v.tensor_tensor(
            dst[:, :].rearrange("p (c f) -> p c f", c=8),
            tmp[:, 0:2048].rearrange("p (c f) -> p c f", c=8),
            rep(s_rb[:, :], 8, 256), OP.mult), inc="vv")

    taps_set = set(tap_names)
    TAP_TILE = 0

    def tap(name, ap_fn, tile):
        if name not in taps_set or tile != TAP_TILE:
            return
        src = ap_fn()
        rr, cc_ = src.shape[0], int(np.prod(src.shape[1:]))
        P.op("gpsimd", lambda gp, name=name, src=src, rr=rr, cc_=cc_: gp.dma_start(
            out=d_tap[name][0:rr, 0:cc_], in_=src),
            waits=[("vv", P.count["vv"]), ("ss", P.count["ss"]), ("mm", P.count["mm"])],
            inc="gg", inc_n=16)
        P.op("vector", lambda v: None, waits=[("gg", P.count["gg"])])
        P.op("scalar", lambda s: None, waits=[("gg", P.count["gg"])])
        P.op("tensor", lambda t_: None, waits=[("gg", P.count["gg"])])

    sq_slot = [None, None]
    st_count = 0
    last_sce = [0]
    z_ring = {}
    for t in range(n_tiles):
        s_xa = s_xaA if t % 2 == 0 else s_xaB
        P.op("sync", lambda sy, t=t, s_xa=s_xa: sy.dma_start(
            out=s_xa[:, :].rearrange("p (c f) -> p c f", c=8),
            in_=bass.AP(d_x, t * TT, [[RTOK, 128], [128 * RTOK, 8], [1, TT]])),
            waits=([("xf", t - 1)] if t >= 2 else []), inc="ldx", inc_n=16)
        XL = [("ldx", 16 * (t + 1))]
        P.op("vector", lambda v: None, waits=XL)
        P.op("scalar", lambda s: None, waits=XL)
        P.op("tensor", lambda t_: None, waits=XL)

        ln_stats(s_xa)
        vh1 = ln_apply(s_xa, s_h, s_x2, extra=[("st", st_count)])
        tap("h1", lambda: s_h[:, :], t)

        xe = []
        for fc in range(4):
            bank = 2 + (fc % 2)
            for k in range(8):
                P.op("tensor", lambda t_, fc=fc, k=k, bank=bank: t_.matmul(
                    PB(bank, 0, 256), s_win[:, k * 648 + fc * 128: k * 648 + fc * 128 + 128],
                    s_h[:, k * 256:(k + 1) * 256], start=(k == 0), stop=(k == 7)),
                    waits=([("vv", vh1)] + BK.writer_waits([bank])) if k == 0 else [],
                    inc="mm" if k == 7 else None)
            mmx = P.count["mm"]
            v = P.op("vector", lambda v, fc=fc, bank=bank: v.tensor_copy(
                s_XT[:, fc * 256:(fc + 1) * 256], PB(bank, 0, 256)),
                waits=[("mm", mmx)], inc="vv")
            BK.consumed([bank], "vv", v)
            xe.append(v)
        for c0, w, dst, r in [(512, 64, s_Bf, 64), (576, 64, s_Cf, 64), (640, 8, s_dtf, 8)]:
            for k in range(8):
                P.op("tensor", lambda t_, k=k, c0=c0, w=w, r=r: t_.matmul(
                    PB(4, 0, 256, r), s_win[:, k * 648 + c0: k * 648 + c0 + w],
                    s_h[:, k * 256:(k + 1) * 256], start=(k == 0), stop=(k == 7)),
                    waits=BK.writer_waits([4]) if k == 0 else [], inc="mm" if k == 7 else None)
            mmb = P.count["mm"]
            v = P.op("vector", lambda v, dst=dst, r=r: v.tensor_copy(dst[:, :],
                                                                     PB(4, 0, 256, r)),
                     waits=[("mm", mmb)], inc="vv")
            BK.consumed([4], "vv", v)
        vbc = P.count["vv"]

        P.op("scalar", lambda s: s.activation(s_sp1[:, :], s_dtf[:, :], AF.Relu,
                                              bias=s_dtb[:, :]), waits=[("vv", vbc)],
             inc="ss")
        P.op("scalar", lambda s: s.activation(s_sp2[:, :], s_dtf[:, :], AF.Abs,
                                              bias=s_dtb[:, :]), inc="ss")
        s3 = P.op("scalar", lambda s: s.activation(s_sp3[:, :], s_sp2[:, :], AF.Exp,
                                                   scale=cc(CNEG1, 8)), inc="ss")
        v1 = P.op("vector", lambda v: v.tensor_scalar(s_sp2[:, :], s_sp3[:, :], cc(CONE, 8),
                                                      None, OP.add), waits=[("ss", s3)],
                  inc="vv")
        s4 = P.op("scalar", lambda s: s.activation(s_sp3[:, :], s_sp2[:, :], AF.Ln),
                  waits=[("vv", v1)], inc="ss")
        vdl = P.op("vector", lambda v: v.tensor_tensor(s_del[:, :], s_sp1[:, :], s_sp3[:, :],
                                                       OP.add), waits=[("ss", s4)], inc="vv")
        va = P.op("vector", lambda v: v.tensor_scalar(s_acl[:, :], s_del[:, :], s_negA[:, :],
                                                      None, OP.mult), inc="vv")

        def wz_dma(zf, gz):
            P.op("sync", lambda sy, zf=zf, gz=gz: sy.dma_start(
                out=bass.AP(s_wzb, (gz % 3) * 1024, [[3072, 128], [128, 8], [1, 128]]),
                in_=bass.AP(d_wzs, zf * 1024 * 128, [[128, 128], [128 * 128, 8], [1, 128]])),
                waits=[("mm", z_ring.get(gz - 3, 0))], inc="lz", inc_n=16)

        def scan_chain(c, out):
            g = t * CPT + c
            first = (g % CHPS == 0)
            co = c * 128
            vcl = P.op("vector", lambda v, co=co: v.tensor_tensor_scan(
                s_clc[:, co:co + 128], s_ones8r[:, :], s_acl[:, co:co + 128], 0.0,
                OP.mult, OP.add), waits=[("vv", va)], inc="vv")
            m = P.op("tensor", lambda t_, co=co: t_.transpose(PB(1, 0, 8),
                                                              s_clc[:, co:co + 128],
                                                              s_ident[0:8, 0:8]),
                     waits=[("vv", vcl)] + BK.writer_waits([1]), inc="mm")
            vclT = P.op("vector", lambda v: v.tensor_copy(s_clT[:, :], PB(1, 0, 8)),
                        waits=[("mm", m)], inc="vv")
            BK.consumed([1], "vv", vclT)
            yield
            m = P.op("tensor", lambda t_, co=co: t_.transpose(PB(1, 8, 8),
                                                              s_del[:, co:co + 128],
                                                              s_ident[0:8, 0:8]),
                     waits=BK.writer_waits([1]) + [("vv", vdl)], inc="mm")
            vdT = P.op("vector", lambda v: v.tensor_copy(s_dT[:, :], PB(1, 8, 8)),
                       waits=[("mm", m)], inc="vv")
            BK.consumed([1], "vv", vdT)
            yield
            vw = P.op("vector", lambda v, co=co: v.tensor_scalar(
                s_w8[:, :], s_clc[:, co:co + 128], s_clc[:, co + 127:co + 128], None,
                OP.subtract), waits=[("vv", vcl)], inc="vv")
            sw = P.op("scalar", lambda s: s.activation(s_w8b[:, :], s_w8[:, :], AF.Exp,
                                                       scale=cc(CNEG1, 8)),
                      waits=[("vv", vw)], inc="ss")
            m = P.op("tensor", lambda t_: t_.transpose(PB(1, 16, 8), s_w8b[:, :],
                                                       s_ident[0:8, 0:8]),
                     waits=[("ss", sw)] + BK.writer_waits([1]), inc="mm")
            vw8 = P.op("vector", lambda v: v.tensor_copy(s_w8T[:, :], PB(1, 16, 8)),
                       waits=[("mm", m)], inc="vv")
            BK.consumed([1], "vv", vw8)
            yield
            m = P.op("tensor", lambda t_, co=co: t_.transpose(PB(1, 64, 64),
                                                              s_Bf[:, co:co + 128],
                                                              s_ident[0:64, 0:64]),
                     waits=BK.writer_waits([1]), inc="mm")
            vB = P.op("vector", lambda v: v.tensor_copy(s_Btm[:, :], PB(1, 64, 64)),
                      waits=[("mm", m)], inc="vv")
            BK.consumed([1], "vv", vB)
            yield
            mS = P.op("tensor", lambda t_, co=co: t_.matmul(
                PB(1, 128, 128), s_Bf[:, co:co + 128], s_Cf[:, co:co + 128],
                start=True, stop=True), waits=BK.writer_waits([1]), inc="mm")
            vSU = P.op("vector", lambda v: v.tensor_tensor(s_SU[:, :], PB(1, 128, 128),
                                                           s_u01[:, :], OP.mult),
                       waits=[("mm", mS)], inc="vv")
            BK.consumed([1], "vv", vSU)
            yield
            vde = P.op("vector", lambda v: v.tensor_tensor(s_dTE[:, :], s_dT[:, :],
                                                           s_hmE[:, :], OP.mult),
                       waits=[("vv", vdT)], inc="vv")
            vdo = P.op("vector", lambda v: v.tensor_tensor(s_dTO[:, :], s_dT[:, :],
                                                           s_hmO[:, :], OP.mult), inc="vv")
            vwd = P.op("vector", lambda v: v.tensor_tensor(s_wd[:, :], s_w8T[:, :],
                                                           s_dT[:, :], OP.mult),
                       waits=[("vv", vw8)], inc="vv")
            for fc in range(4):
                P.op("tensor", lambda t_, fc=fc, co=co: t_.transpose(
                    PB(0, fc * 128, 128), s_XT[:, fc * 256 + co: fc * 256 + co + 128],
                    s_ident[:, :]),
                    waits=(BK.writer_waits([0]) + [("vv", xe[3])]) if fc == 0 else [],
                    inc="mm" if fc == 3 else None)
            mX = P.count["mm"]
            vXE = P.op("vector", lambda v: v.tensor_tensor(
                s_XdE[:, :].rearrange("p (h d) -> p h d", h=8), PB(0, 0, 512),
                col_rep(s_dTE[:, :], 8, 64), OP.mult),
                waits=[("mm", mX), ("vv", vde)], inc="vv")
            vXO = P.op("vector", lambda v: v.tensor_tensor(
                s_XdO[:, :].rearrange("p (h d) -> p h d", h=8), PB(0, 0, 512),
                col_rep(s_dTO[:, :], 8, 64), OP.mult), waits=[("vv", vdo)], inc="vv")
            vWx = P.op("vector", lambda v: v.tensor_tensor(
                s_Wx[:, :].rearrange("p (h d) -> p h d", h=8), PB(0, 0, 512),
                col_rep(s_wd[:, :], 8, 64), OP.mult), waits=[("vv", vwd)], inc="vv")
            BK.consumed([0], "vv", vWx)
            yield
            mb8 = None
            for h in range(8):
                mb8 = P.op("tensor", lambda t_, h=h, co=co: t_.matmul(
                    PB(2 + h // 4, (h % 4) * 128, 128), s_hsel[:, h * 128:(h + 1) * 128],
                    s_clc[:, co:co + 128], start=True, stop=True),
                    waits=(BK.writer_waits([2, 3]) + [("vv", vcl)]) if h == 0 else [],
                    inc="mm" if h == 7 else None)
            se1 = P.op("scalar", lambda s: s.activation(s_G[0:64, 0:512], PB(2, 0, 512, 64),
                                                        AF.Exp), waits=[("mm", mb8)], inc="ss")
            se2 = P.op("scalar", lambda s: s.activation(s_G[0:64, 512:1024],
                                                        PB(3, 0, 512, 64), AF.Exp), inc="ss")
            yield
            vce = P.op("vector", lambda v, co=co: v.tensor_tensor(
                s_Cexp[:, :].rearrange("p (h d) -> p h d", h=8),
                rep(s_Cf[:, co:co + 128], 8, 128),
                s_G[0:64, :].rearrange("p (h d) -> p h d", h=8), OP.mult),
                waits=[("ss", se2)], inc="vv")
            vGs = []
            yield
            for gi in range(2):
                for hh in range(4):
                    h = gi * 4 + hh
                    P.op("vector", lambda v, h=h, hh=hh, gi=gi: v.scalar_tensor_tensor(
                        s_Epre[:, hh * 128:(hh + 1) * 128], PB(2 + gi, hh * 128, 128),
                        s_clT[:, h:h + 1], s_mbias[:, :], OP.subtract, OP.add),
                        waits=[("vv", vclT), ("mm", mb8)] if (gi == 0 and hh == 0) else [],
                        inc="vv")
                sE = P.op("scalar", lambda s: s.activation(s_Ehat[:, :], s_Epre[:, :],
                                                           AF.Exp),
                          waits=[("vv", P.count["vv"])], inc="ss")
                vG = P.op("vector", lambda v, gi=gi: v.tensor_tensor(
                    s_G[:, gi * 512:(gi + 1) * 512].rearrange("p (h d) -> p h d", h=4),
                    s_Ehat[:, :].rearrange("p (h d) -> p h d", h=4),
                    rep(s_SU[:, :], 4, 128), OP.mult),
                    waits=[("ss", sE), ("vv", vSU)], inc="vv")
                vGs.append(vG)
                yield
            ve2 = P.count["vv"]
            BK.consumed([2], "vv", ve2)
            BK.consumed([3], "vv", ve2)
            BK.consumed([2], "ss", se1)
            BK.consumed([3], "ss", se2)
            vpo = None
            if not first:
                P.op("vector", lambda v: v.tensor_tensor(
                    s_HPE[:, :].rearrange("p (h d) -> p h d", h=8),
                    s_SP[:, :].rearrange("p (h d) -> p h d", h=8),
                    col_rep(s_hmE[0:64, :], 8, 64), OP.mult), inc="vv")
                vpo = P.op("vector", lambda v: v.tensor_tensor(
                    s_HPO[:, :].rearrange("p (h d) -> p h d", h=8),
                    s_SP[:, :].rearrange("p (h d) -> p h d", h=8),
                    col_rep(s_hmO[0:64, :], 8, 64), OP.mult), inc="vv")
                P.op("gpsimd", lambda gp: gp.dma_start(out=s_cer[:, :],
                                                       in_=s_clT[127:128, :]),
                     waits=[("vv", vclT), ("ss", last_sce[0])], inc="gg", inc_n=16)
                sce = P.op("scalar", lambda s: s.activation(s_cer2[:, :], s_cer[:, :],
                                                            AF.Exp),
                           waits=[("gg", P.count["gg"])], inc="ss")
                last_sce[0] = sce
                mEH = P.op("tensor", lambda t_: t_.matmul(PB(1, 256, 8, 64),
                                                          s_onesr[0:1, 0:64], s_cer2[:, :],
                                                          start=True, stop=True),
                           waits=[("ss", sce)] + BK.writer_waits([1]), inc="mm")
                vsp = P.op("vector", lambda v: v.tensor_tensor(
                    s_SP2[:, :].rearrange("p (h d) -> p h d", h=8),
                    s_SP[:, :].rearrange("p (h d) -> p h d", h=8),
                    col_rep(PB(1, 256, 8, 64), 8, 64), OP.mult),
                    waits=[("mm", mEH), ("vv", vpo)], inc="vv")
                BK.consumed([1], "vv", vsp)
            out.update(first=first, co=co, vXO=vXO, vWx=vWx, vB=vB, vce=vce,
                       vGs=vGs, vpo=vpo)
            yield

        def z_half(h0, gen):
            def record_zg(zf):
                zgp0 = zgp_mm.get(zf, None)
                for j in range(4):
                    bank, off = 4 + j // 2, (j % 2) * 256
                    P.op("tensor", lambda t_, zf=zf, j=j, bank=bank, off=off: t_.matmul(
                        PB(bank, off, 256),
                        s_wzg[:, zf * DI + j * 128: zf * DI + j * 128 + 128],
                        A_zc()[:, (zf % 2) * 256:(zf % 2) * 256 + 256],
                        start=True, stop=True),
                        waits=([("vv", zc_vv[zf])] if j == 0 else []) +
                              (BK.writer_waits([4, 5]) if j == 0 else []),
                        inc="mm" if j == 3 else None)
                zgp = P.count["mm"]
                zgp_mm[zf] = zgp
                src, dst = (A_zga, A_zgb) if zf % 2 == 1 else (A_zgb, A_zga)
                if zf == 0:
                    vz = P.op("vector", lambda v: v.tensor_copy(
                        A_zga(), ps[:, 4 * 512: 4 * 512 + 1024]),
                        waits=[("mm", zgp)], inc="vv")
                else:
                    vz = P.op("vector", lambda v, src=src, dst=dst: v.tensor_tensor(
                        dst(), src(), ps[:, 4 * 512: 4 * 512 + 1024], OP.add),
                        waits=[("mm", zgp)], inc="vv")
                BK.consumed([4, 5], "vv", vz)

            for zf in range(h0, h0 + 8):
                gz = t * 16 + zf
                zb = 6 + zf % 2
                for k in range(8):
                    P.op("tensor", lambda t_, k=k, zb=zb, gz=gz: t_.matmul(
                        PB(zb, 0, 256),
                        s_wzb[:, (gz % 3) * 1024 + k * 128: (gz % 3) * 1024 + k * 128 + 128],
                        s_h[:, k * 256:(k + 1) * 256], start=(k == 0), stop=(k == 7)),
                        waits=(BK.writer_waits([zb]) + [("lz", 16 * (gz + 1))]) if k == 0
                        else [], inc="mm" if k == 7 else None)
                mz = P.count["mm"]
                z_ring[gz] = mz
                if zf + 3 < 16:
                    wz_dma(zf + 3, t * 16 + zf + 3)
                elif t + 1 < n_tiles:
                    nzf = zf + 3 - 16
                    wz_dma(nzf, (t + 1) * 16 + nzf)
                sth = P.op("scalar", lambda sc, zf=zf, zb=zb: sc.activation(
                    s_big[:, 1536 + (zf % 2) * 256: 1536 + (zf % 2) * 256 + 256], PB(zb, 0, 256), AF.Tanh,
                    scale=cc(CHALF)),
                    waits=[("mm", mz)] + ([("vv", zc_vv.get(zf - 2, None) or 0)]
                                          if zf >= 2 else []), inc="ss")
                P.op("vector", lambda v, zf=zf: v.tensor_scalar(
                    s_sg[:, (zf % 2) * 256:(zf % 2) * 256 + 256],
                    s_big[:, 1536 + (zf % 2) * 256: 1536 + (zf % 2) * 256 + 256],
                    cc(CHALF), cc(CHALF), OP.mult, OP.add),
                    waits=[("ss", sth)], inc="vv")
                vzc = P.op("vector", lambda v, zf=zf, zb=zb: v.tensor_tensor(
                    A_zc()[:, (zf % 2) * 256:(zf % 2) * 256 + 256],
                    s_sg[:, (zf % 2) * 256:(zf % 2) * 256 + 256],
                    PB(zb, 0, 256), OP.mult), inc="vv")
                zc_vv[zf] = vzc
                BK.consumed([zb], "vv", vzc)
                if zf > h0:
                    record_zg(zf - 1)
                next(gen, None)
                next(gen, None)
            record_zg(h0 + 7)
            for _ in gen:
                pass

        def y_state(ch):
            first, co = ch["first"], ch["co"]
            for fc in range(4):
                hE, hO = 2 * fc, 2 * fc + 1
                w0 = (BK.writer_waits([4]) + [("vv", ch["vXO"]), ("vv", ch["vGs"][1]),
                                              ("vv", ch["vce"])]
                      + ([("vv", ch["vpo"])] if not first else [])) if fc == 0 else []
                P.op("tensor", lambda t_, fc=fc, hE=hE: t_.matmul(
                    PB(4, fc * 128, 128), s_XdE[:, fc * 128:(fc + 1) * 128],
                    s_G[:, hE * 128:(hE + 1) * 128], start=True, stop=False),
                    waits=w0, inc=None)
                P.op("tensor", lambda t_, fc=fc, hO=hO, first=first: t_.matmul(
                    PB(4, fc * 128, 128), s_XdO[:, fc * 128:(fc + 1) * 128],
                    s_G[:, hO * 128:(hO + 1) * 128], start=False, stop=first),
                    inc="mm" if (first and fc == 3) else None)
                if not first:
                    P.op("tensor", lambda t_, fc=fc, hE=hE: t_.matmul(
                        PB(4, fc * 128, 128), s_HPE[:, fc * 128:(fc + 1) * 128],
                        s_Cexp[:, hE * 128:(hE + 1) * 128], start=False, stop=False),
                        inc=None)
                    P.op("tensor", lambda t_, fc=fc, hO=hO: t_.matmul(
                        PB(4, fc * 128, 128), s_HPO[:, fc * 128:(fc + 1) * 128],
                        s_Cexp[:, hO * 128:(hO + 1) * 128], start=False, stop=True),
                        inc="mm" if fc == 3 else None)
            ym = P.count["mm"]
            for fc in range(4):
                P.op("tensor", lambda t_, fc=fc: t_.matmul(
                    PB(5, fc * 128, 128, 64), s_Btm[:, :], s_Wx[:, fc * 128:(fc + 1) * 128],
                    start=True, stop=True),
                    waits=(BK.writer_waits([5]) + [("vv", ch["vWx"]), ("vv", ch["vB"])])
                    if fc == 0 else [], inc="mm" if fc == 3 else None)
            mst = P.count["mm"]
            if not first:
                vH = P.op("vector", lambda v: v.tensor_tensor(s_SP[:, :], s_SP2[:, :],
                                                              PB(5, 0, 512, 64), OP.add),
                          waits=[("mm", mst)], inc="vv")
            else:
                vH = P.op("vector", lambda v: v.tensor_copy(s_SP[:, :], PB(5, 0, 512, 64)),
                          waits=[("mm", mst)], inc="vv")
            BK.consumed([5], "vv", vH)
            for fc in range(4):
                v = P.op("vector", lambda v, fc=fc, co=co: v.scalar_tensor_tensor(
                    s_yT[:, fc * 256 + co: fc * 256 + co + 128],
                    s_XT[:, fc * 256 + co: fc * 256 + co + 128],
                    s_drep[:, fc:fc + 1], PB(4, fc * 128, 128), OP.mult, OP.add),
                    waits=[("mm", ym)], inc="vv")
                BK.consumed([4], "vv", v)

        if t == 0:
            for i in range(3):
                wz_dma(i, i)
        zgp_mm = {}
        zc_vv = {}
        ch0 = {}
        z_half(0, scan_chain(0, ch0))
        y_state(ch0)
        ch1 = {}
        z_half(8, scan_chain(1, ch1))
        y_state(ch1)
        tap("yT", lambda: bass.AP(s_yT, 0, [[1024, 128], [1, 1024]]), t)

        vu = P.op("vector", lambda v: v.tensor_tensor(A_u2(), A_zgb(), s_yT[:, :],
                                                      OP.mult), inc="vv")
        vx2 = None
        for cC in range(8):
            bank = 4 + (cC % 2)
            for k in range(4):
                P.op("tensor", lambda t_, cC=cC, k=k, bank=bank: t_.matmul(
                    PB(bank, 0, 256), s_wout[:, k * D + cC * 128: k * D + cC * 128 + 128],
                    A_u2()[:, k * 256:(k + 1) * 256], start=(k == 0), stop=(k == 3)),
                    waits=(BK.writer_waits([bank]) + ([("vv", vu)] if cC == 0 else []))
                    if k == 0 else [], inc="mm" if k == 3 else None)
            ma = P.count["mm"]
            vx2 = P.op("vector", lambda v, cC=cC, bank=bank, s_xa=s_xa: v.tensor_tensor(
                s_x2[:, cC * 256:(cC + 1) * 256], PB(bank, 0, 256),
                s_xa[:, cC * 256:(cC + 1) * 256], OP.add), waits=[("mm", ma)], inc="vv")
            BK.consumed([bank], "vv", vx2)
        tap("u", lambda: bass.AP(s_big, 0, [[4096, 128], [1, 1024]]), t)
        tap("x2", lambda: s_x2[:, :], t)

        ln_stats(s_x2, wv=vx2)
        vh2 = ln_apply(s_x2, s_h, s_xa)
        tap("h2", lambda: s_h[:, :], t)

        for k in range(8):
            P.op("tensor", lambda t_, k=k: t_.matmul(
                PB(2, 0, 256, 64), s_qT[:, k * 64:(k + 1) * 64],
                s_h[:, k * 256:(k + 1) * 256], start=(k == 0), stop=(k == 7)),
                waits=(BK.writer_waits([2]) + [("vv", vh2)]) if k == 0 else [], inc=None)
            P.op("tensor", lambda t_, k=k: t_.matmul(
                PB(3, 0, 256, 8), s_csT[:, k * 8:(k + 1) * 8],
                s_h[:, k * 256:(k + 1) * 256], start=(k == 0), stop=(k == 7)),
                waits=BK.writer_waits([3]) if k == 0 else [],
                inc="mm" if k == 7 else None)
        msc = P.count["mm"]
        for hf in range(2):
            ho = hf * 128
            so_ = hf * 80
            vts = P.op("vector", lambda v, ho=ho: v.tensor_copy(A_ga()[0:64, 0:128],
                                                                PB(2, ho, 128, 64)),
                       waits=[("mm", msc)], inc="vv")
            vcs3 = P.op("vector", lambda v, ho=ho: v.tensor_copy(s_sq[0:8, 0:128],
                                                                 PB(3, ho, 128, 8)),
                        inc="vv")
            if hf == 1:
                BK.consumed([2], "vv", vcs3)
                BK.consumed([3], "vv", vcs3)
            mt1 = P.op("tensor", lambda t_: t_.transpose(PB(4, 0, 64), A_ga()[0:64, 0:128],
                                                         s_ident[0:64, 0:64]),
                       waits=[("vv", vts)] + BK.writer_waits([4]), inc="mm")
            mt2 = P.op("tensor", lambda t_: t_.transpose(PB(4, 64, 8), s_sq[0:8, 0:128],
                                                         s_ident[0:8, 0:8]),
                       waits=[("vv", vcs3)], inc="mm")
            vev = P.op("vector", lambda v, so_=so_: v.tensor_copy(A_sct()[:, so_:so_ + 72],
                                                                  PB(4, 0, 72)),
                       waits=[("mm", mt2)], inc="vv")
            BK.consumed([4], "vv", vev)
            vcl2 = P.op("vector", lambda v, so_=so_: v.tensor_scalar(
                A_ga()[:, 0:72], A_sct()[:, so_:so_ + 72], s_itcn[:, :], cc(C30),
                OP.mult, OP.min), waits=[("vv", vev)], inc="vv")
            sx = P.op("scalar", lambda s, so_=so_: s.activation(
                A_gn()[:, so_:so_ + 72], A_ga()[:, 0:72], AF.Exp),
                waits=[("vv", vcl2)], inc="ss")
            P.op("vector", lambda v, so_=so_: v.tensor_scalar(
                A_ga()[:, 0:72], A_gn()[:, so_:so_ + 72], cc(CONE), None, OP.add),
                waits=[("ss", sx)], inc="vv")
            P.op("vector", lambda v, so_=so_: v.reciprocal(A_gn()[:, so_:so_ + 72],
                                                           A_ga()[:, 0:72]), inc="vv")
            P.op("vector", lambda v, so_=so_: v.tensor_reduce(
                A_gm()[:, 0:1], A_gn()[:, so_ + 64:so_ + 72], AXL.X, OP.max), inc="vv")
            P.op("vector", lambda v, so_=so_: v.tensor_scalar(
                A_ohc()[:, :], A_gn()[:, so_ + 64:so_ + 72], A_gm()[:, 0:1], None,
                OP.is_equal), inc="vv")
            P.op("vector", lambda v: v.tensor_tensor(A_ohl()[:, :], A_ohc()[:, :],
                                                     s_rev8[:, :], OP.mult), inc="vv")
            P.op("vector", lambda v: v.tensor_reduce(A_gm()[:, 1:2], A_ohl()[:, :], AXL.X,
                                                     OP.max), inc="vv")
            P.op("vector", lambda v: v.tensor_scalar(A_ohc()[:, :], A_ohl()[:, :],
                                                     A_gm()[:, 1:2], None, OP.is_equal),
                 inc="vv")
            ohc_ap = A_ohc()
            gb_ap = A_gb()
            gt_ap = A_gt()
            P.op("vector", lambda v, so_=so_: v.tensor_tensor(
                A_gb()[:, :].rearrange("p (c j) -> p c j", c=8),
                A_gn()[:, so_:so_ + 64].rearrange("p (c j) -> p c j", c=8),
                bass.AP(ohc_ap.tensor, ohc_ap.offset, [ohc_ap.ap[0], [1, 8], [0, 8]]),
                OP.mult), inc="vv")
            P.op("vector", lambda v: v.tensor_reduce(
                A_selg()[:, :].rearrange("p (j o) -> p j o", o=1),
                bass.AP(gb_ap.tensor, gb_ap.offset, [gb_ap.ap[0], [1, 8], [8, 8]]),
                AXL.X, OP.add), inc="vv")
            P.op("vector", lambda v: v.tensor_reduce(A_gm()[:, 2:3], A_selg()[:, :], AXL.X,
                                                     OP.max), inc="vv")
            P.op("vector", lambda v: v.tensor_scalar(A_gt()[:, :], A_selg()[:, :],
                                                     A_gm()[:, 2:3], None, OP.is_equal),
                 inc="vv")
            P.op("vector", lambda v: v.tensor_tensor(A_ohl()[:, :], A_gt()[:, :],
                                                     s_rev8[:, :], OP.mult), inc="vv")
            P.op("vector", lambda v: v.tensor_reduce(A_gm()[:, 3:4], A_ohl()[:, :], AXL.X,
                                                     OP.max), inc="vv")
            P.op("vector", lambda v: v.tensor_scalar(A_gt()[:, :], A_ohl()[:, :],
                                                     A_gm()[:, 3:4], None, OP.is_equal),
                 inc="vv")
            voh = P.op("vector", lambda v: v.tensor_tensor(
                A_oht()[:, :].rearrange("p (c j) -> p c j", c=8),
                bass.AP(ohc_ap.tensor, ohc_ap.offset, [ohc_ap.ap[0], [1, 8], [0, 8]]),
                bass.AP(gt_ap.tensor, gt_ap.offset, [gt_ap.ap[0], [0, 8], [1, 8]]),
                OP.mult), inc="vv")
            mo = P.op("tensor", lambda t_: t_.transpose(PB(4, 128, 128, 64), A_oht()[:, :],
                                                        s_ident[:, :]),
                      waits=[("vv", voh)] + BK.writer_waits([4]), inc="mm")
            vof = P.op("vector", lambda v, ho=ho: v.tensor_copy(A_ohf()[0:64, ho:ho + 128],
                                                                PB(4, 128, 128, 64)),
                       waits=[("mm", mo)], inc="vv")
            BK.consumed([4], "vv", vof)
        vohf = P.count["vv"]
        tap("sct0", lambda: bass.AP(s_big, 2560, [[4096, 128], [1, 72]]), t)
        tap("sct1", lambda: bass.AP(s_big, 2640, [[4096, 128], [1, 72]]), t)
        tap("gn0", lambda: bass.AP(s_big, 2720, [[4096, 128], [1, 72]]), t)
        tap("gn1", lambda: bass.AP(s_big, 2800, [[4096, 128], [1, 72]]), t)
        tap("ohf", lambda: bass.AP(s_big, 3176, [[4096, 64], [1, 256]]), t)

        hcs = []
        for f in range(2):
            bank = 6 + f
            for k in range(8):
                P.op("tensor", lambda t_, f=f, k=k, bank=bank: t_.matmul(
                    PB(bank, 0, 256), s_cw1[:, k * 256 + f * 128: k * 256 + f * 128 + 128],
                    s_h[:, k * 256:(k + 1) * 256], start=(k == 0), stop=(k == 7)),
                    waits=BK.writer_waits([bank]) if k == 0 else [],
                    inc="mm" if k == 7 else None)
            mc = P.count["mm"]
            sg = P.op("scalar", lambda s, f=f, bank=bank: s.activation(
                A_hc()[:, f * 256:(f + 1) * 256], PB(bank, 0, 256), AF.Gelu,
                bias=s_cb1[:, f:f + 1]), waits=[("mm", mc)], inc="ss")
            hcs.append(sg)
            BK.consumed([bank], "ss", sg)
        for k in range(2):
            P.op("tensor", lambda t_, k=k: t_.matmul(
                PB(6, 0, 256, 2), s_cw2[:, k * 2:(k + 1) * 2],
                A_hc()[:, k * 256:(k + 1) * 256], start=(k == 0), stop=(k == 1)),
                waits=[("ss", hcs[k])] + (BK.writer_waits([6]) if k == 0 else []),
                inc="mm" if k == 1 else None)
        mcp = P.count["mm"]
        scp = P.op("scalar", lambda s: s.activation(A_comp()[:, :], PB(6, 0, 256, 2),
                                                    AF.Tanh, bias=s_cb2[:, :]),
                   waits=[("mm", mcp)], inc="ss")
        BK.consumed([6], "ss", scp)
        vM = []
        for i in range(2):
            bank = 6 + i
            mi = P.op("tensor", lambda t_, i=i, bank=bank: t_.matmul(
                PB(bank, 0, 256, 64), s_hsel[0:2, i * 128:i * 128 + 64],
                A_comp()[0:2, :], start=True, stop=True),
                waits=[("ss", scp)] + BK.writer_waits([bank]), inc="mm")
            v = P.op("vector", lambda v, i=i, bank=bank: v.tensor_tensor(
                (A_M0() if i == 0 else A_M1())[:, :], A_ohf(), PB(bank, 0, 256, 64),
                OP.mult), waits=[("mm", mi), ("vv", vohf)], inc="vv")
            BK.consumed([bank], "vv", v)
            vM.append(v)
        P.op("tensor", lambda t_: t_.matmul(PB(7, 0, 256, 16), s_w1a[:, :], A_M0(),
                                            start=True, stop=False),
             waits=[("vv", vM[1])] + BK.writer_waits([7]), inc=None)
        P.op("tensor", lambda t_: t_.matmul(PB(7, 0, 256, 16), s_w1b[:, :], A_M1(),
                                            start=False, stop=False), inc=None)
        mhh = P.op("tensor", lambda t_: t_.matmul(PB(7, 0, 256, 16), s_b1t[:, :], A_ohf(),
                                                  start=False, stop=True), inc="mm")
        shh = P.op("scalar", lambda s: s.activation(A_hh()[:, :], PB(7, 0, 256, 16),
                                                    AF.Relu), waits=[("mm", mhh)], inc="ss")
        BK.consumed([7], "ss", shh)
        mw2 = P.op("tensor", lambda t_: t_.matmul(PB(6, 0, 256, 16), s_w2t[:, :], A_ohf(),
                                                  start=True, stop=True),
                   waits=BK.writer_waits([6]), inc="mm")
        vt1 = P.op("vector", lambda v: v.tensor_tensor(A_t1()[:, :], A_hh(),
                                                       PB(6, 0, 256, 16), OP.mult),
                   waits=[("mm", mw2), ("ss", shh)], inc="vv")
        BK.consumed([6], "vv", vt1)
        P.op("tensor", lambda t_: t_.matmul(PB(5, 0, 256, 1), s_ones16[:, :], A_t1(),
                                            start=True, stop=False),
             waits=[("vv", vt1)] + BK.writer_waits([5]), inc=None)
        mmg = P.op("tensor", lambda t_: t_.matmul(PB(5, 0, 256, 1), s_b2t[:, :], A_ohf(),
                                                  start=False, stop=True), inc="mm")
        vmg = P.op("vector", lambda v: v.tensor_copy(A_magr()[:, :], PB(5, 0, 256, 1)),
                   waits=[("mm", mmg)], inc="vv")
        BK.consumed([5], "vv", vmg)
        mgb = P.op("tensor", lambda t_: t_.matmul(PB(5, 0, 256, 64), s_onesr[0:1, 0:64],
                                                  A_magr(), start=True, stop=True),
                   waits=[("vv", vmg)] + BK.writer_waits([5]), inc="mm")
        vso = P.op("vector", lambda v: v.tensor_tensor(A_soh()[:, :], A_ohf(),
                                                       PB(5, 0, 256, 64), OP.mult),
                   waits=[("mm", mgb)], inc="vv")
        BK.consumed([5], "vv", vso)
        tap("soh", lambda: bass.AP(s_big, 3432, [[4096, 64], [1, 256]]), t)
        vyp = None
        for cC in range(8):
            bank = 4 + (cC % 2)
            mD = P.op("tensor", lambda t_, cC=cC, bank=bank: t_.matmul(
                PB(bank, 0, 256), s_dirss[:, cC * 128:(cC + 1) * 128], A_soh(),
                start=True, stop=True),
                waits=[("vv", vso)] + BK.writer_waits([bank]), inc="mm")
            vyp = P.op("vector", lambda v, cC=cC, bank=bank, s_xa=s_xa: v.tensor_tensor(
                s_xa[:, cC * 256:(cC + 1) * 256], PB(bank, 0, 256),
                s_h[:, cC * 256:(cC + 1) * 256], OP.add), waits=[("mm", mD)], inc="vv")
            BK.consumed([bank], "vv", vyp)
        tap("ypre", lambda: s_xa[:, :], t)

        ln_stats(s_xa, wv=vyp)
        vmr = P.op("vector", lambda v: v.tensor_tensor(s_mrb[:, :], s_mb[:, :], s_rb[:, :],
                                                       OP.mult), inc="vv")
        vt2 = ln_apply(s_xa, s_h, A_tmp3())
        P.op("vector", lambda v: v.nop(), waits=[("vv", vt2)], inc="xf")
        for cC in range(8):
            P.op("vector", lambda v, cC=cC: v.tensor_scalar(
                s_vb[:, :], s_mrb[:, :], s_gamc[:, cC:cC + 1], None, OP.mult),
                waits=[("vv", vmr)], inc="vv")
            P.op("vector", lambda v, cC=cC: v.scalar_tensor_tensor(
                s_wfin[:, :], s_x2[:, cC * 256:(cC + 1) * 256], s_betc[:, cC:cC + 1],
                s_vb[:, :], OP.add, OP.subtract), inc="vv")
            P.op("vector", lambda v, cC=cC: v.scalar_tensor_tensor(
                s_x2[:, cC * 256:(cC + 1) * 256], s_h[:, cC * 256:(cC + 1) * 256],
                s_gamc[:, cC:cC + 1], s_wfin[:, :], OP.mult, OP.add), inc="vv")
        vfin = P.count["vv"]
        tap("out", lambda: s_x2[:, :], t)

        st_count = P.op("gpsimd", lambda g, t=t: g.dma_start(
            out=bass.AP(d_out, t * TT, [[RTOK, 128], [128 * RTOK, 8], [1, TT]]),
            in_=s_x2[:, :].rearrange("p (c f) -> p c f", c=8)),
            waits=[("vv", vfin)], inc="st", inc_n=16)

    P.op("gpsimd", lambda g: None, waits=[("st", P.count["st"])])

    sems = {}
    for s in P.count:
        sems[s] = es.enter_context(nc.semaphore(s))
    with nc.Block() as block:
        @block.tensor
        def _(eng):
            P.emit("tensor", eng, sems)

        @block.vector
        def _(eng):
            P.emit("vector", eng, sems)

        @block.scalar
        def _(eng):
            P.emit("scalar", eng, sems)

        @block.gpsimd
        def _(eng):
            P.emit("gpsimd", eng, sems)

        @block.sync
        def _(eng):
            P.emit("sync", eng, sems)

    nc._keepalive = es
    return nc


def make_inmaps(inputs):
    f32 = np.float32
    x = np.asarray(inputs["L0_attn_x"], f32)
    wz = np.asarray(inputs["L0_attn_Wz"], f32)
    wzs = np.ascontiguousarray(
        wz.reshape(8, 128, 16, 128).transpose(2, 0, 1, 3).reshape(16 * 8 * 128, 128))
    win = np.asarray(inputs["L0_attn_Win"], f32)
    win_r = np.ascontiguousarray(np.concatenate(
        [win[:, 136:648], win[:, 0:64], win[:, 64:128], win[:, 128:136]], axis=1))
    sw1 = np.asarray(inputs["L0_ffn_spline_w1"], f32)
    hm = np.zeros((128, 8), f32)
    hm[:, 0::2] = 1.0
    common = {
        "wzs": wzs,
        "win": win_r,
        "wzg": np.ascontiguousarray(np.asarray(inputs["L0_attn_Wzg"], f32)),
        "wout": np.ascontiguousarray(np.asarray(inputs["L0_attn_Wout"], f32)),
        "cw1": np.ascontiguousarray(np.asarray(inputs["L0_ffn_compress_w1"], f32)),
        "cw2": np.ascontiguousarray(np.asarray(inputs["L0_ffn_compress_w2"], f32)),
        "w1a": np.ascontiguousarray(sw1[:, 0, :]),
        "w1b": np.ascontiguousarray(sw1[:, 1, :]),
        "b1t": np.ascontiguousarray(np.asarray(inputs["L0_ffn_spline_b1"], f32)),
        "w2t": np.ascontiguousarray(np.asarray(inputs["L0_ffn_spline_w2"], f32)[:, :, 0]),
        "b2t": np.ascontiguousarray(np.asarray(inputs["L0_ffn_spline_b2"], f32)),
        "dirs": np.ascontiguousarray(np.asarray(inputs["L0_ffn_directions"], f32)),
        "sigs": np.ascontiguousarray(np.asarray(inputs["L0_ffn_signatures_raw"], f32)),
        "csel": np.ascontiguousarray(
            (np.arange(64)[:, None] // 8 == np.arange(8)[None, :]).astype(f32)),
        "dtb": np.asarray(inputs["L0_attn_dt_bias"], f32).reshape(8, 1),
        "alog": np.asarray(inputs["L0_attn_A_log"], f32).reshape(8, 1),
        "drep": np.ascontiguousarray(
            np.repeat(np.asarray(inputs["L0_attn_D_param"], f32), 64).reshape(4, 128).T),
        "gamc": np.ascontiguousarray(
            np.asarray(inputs["L0_ffn_norm_gamma"], f32).reshape(8, 128).T),
        "betc": np.ascontiguousarray(
            np.asarray(inputs["L0_ffn_norm_beta"], f32).reshape(8, 128).T),
        "cb1": np.ascontiguousarray(
            np.asarray(inputs["L0_ffn_compress_b1"], f32).reshape(2, 128).T),
        "cb2": np.asarray(inputs["L0_ffn_compress_b2"], f32).reshape(2, 1),
        "temp": np.asarray(inputs["L0_ffn_temperature"], f32).reshape(1, 1),
        "oscale": np.asarray(inputs["L0_ffn_output_scale"], f32).reshape(1, 1),
        "ident": np.eye(128, dtype=f32),
        "rev8": np.tile(8.0 - np.arange(8, dtype=f32), (128, 1)),
        "u01": (np.arange(128)[None, :] >= np.arange(128)[:, None]).astype(f32),
        "mbias": np.where(np.arange(128)[None, :] >= np.arange(128)[:, None],
                          0.0, MASKNEG).astype(f32),
        "onesr": np.ones((1, 256), f32),
        "ones8r": np.ones((8, 128), f32),
        "onespad": np.concatenate([np.ones((128, 1), f32), np.zeros((128, 127), f32)], 1),
        "ones16": np.ones((16, 1), f32),
        "cc": np.tile(np.array([[1.0 / 1024, EPS, -1.0, 0.0, 1.0, 1e-6, 0.3, -0.3,
                                 30.0, -0.5, 0.5, 0.0]], f32), (128, 1)),
        "hmE": hm,
        "hmO": (1.0 - hm).astype(f32),
        "hsel": np.ascontiguousarray(np.repeat(np.eye(8, dtype=f32), 128, axis=1)),
    }
    in_maps = []
    for i in range(NCORES):
        m = dict(common)
        m["x"] = np.ascontiguousarray(x[BPC * i:BPC * (i + 1)].reshape(RTOK, D).T)
        in_maps.append(m)
    return in_maps


_NC_CACHE = {}


def kernel(**inputs):
    if "full" not in _NC_CACHE:
        _NC_CACHE["full"] = build_program()
    nc = _NC_CACHE["full"]
    in_maps = make_inmaps(inputs)
    res = run_bass_kernel_spmd(nc, in_maps, core_ids=list(range(NCORES)))
    outs = []
    for i in range(NCORES):
        o = np.asarray(res.results[i]["out"])
        outs.append(o.T.reshape(BPC, SEQ, D))
    return np.concatenate(outs, axis=0)


# version 07
# speedup vs baseline: 1.2171x; 1.0135x over previous
import contextlib
import numpy as np
import concourse.bass as bass
import concourse.mybir as mybir
from concourse.bass_utils import run_bass_kernel_spmd

F32 = mybir.dt.float32
AF = mybir.ActivationFunctionType
OP = mybir.AluOpType
AXL = mybir.AxisListType

NCORES = 8
BATCH, SEQ, D = 32, 2048, 1024
BPC = BATCH // NCORES
RTOK = BPC * SEQ
TT = 256
NTILE = RTOK // TT
Q = 128
CPT = TT // Q
NH, HD = 8, 64
DI = NH * HD
DZ = 2 * D
EPS = 1e-5
MASKNEG = -30000.0
CHPS = SEQ // Q

C1024, CEPS, CNEG1, CZERO, CONE, C1E6, CTH, CNTH, C30, CNEGH = 0, 1, 2, 3, 4, 5, 6, 7, 8, 9
CHALF = 10


class Prog:
    ENGINES = ("tensor", "vector", "scalar", "gpsimd", "sync")

    def __init__(self):
        self.ops = {e: [] for e in self.ENGINES}
        self.count = {}
        self.hw = {e: {} for e in self.ENGINES}

    def sem(self, name):
        self.count.setdefault(name, 0)

    def op(self, eng, fn, waits=(), inc=None, inc_n=1):
        ws = []
        for s, c in waits:
            if c is None or c <= 0:
                continue
            assert c <= self.count[s], f"wait {s}>={c}, recorded {self.count[s]}"
            if self.hw[eng].get(s, -1) < c:
                ws.append((s, c))
                self.hw[eng][s] = c
        if inc is not None:
            self.count[inc] += inc_n
        self.ops[eng].append((fn, ws, inc, inc_n))
        return self.count[inc] if inc is not None else None

    def emit(self, eng_name, eng, sems):
        for fn, ws, inc, inc_n in self.ops[eng_name]:
            for s, c in ws:
                eng.wait_ge(sems[s], c)
            instr = fn(eng)
            if inc is not None:
                assert instr is not None, f"inc={inc} op returned None on {eng_name}"
                instr.then_inc(sems[inc], inc_n)


class Banks:
    def __init__(self):
        self.pending = {b: [] for b in range(8)}

    def writer_waits(self, banks):
        ws = []
        for b in banks:
            ws += self.pending[b]
            self.pending[b] = []
        return ws

    def consumed(self, banks, sem, count):
        for b in banks:
            self.pending[b].append((sem, count))


def rep(ap, nrep, inner):
    return bass.AP(ap.tensor, ap.offset, [ap.ap[0], [0, nrep], [ap.ap[-1][0], inner]])


def col_rep(ap, ncols, inner):
    return bass.AP(ap.tensor, ap.offset, [ap.ap[0], [ap.ap[-1][0], ncols], [0, inner]])


def build_program(n_tiles=NTILE, tap_names=()):
    nc = bass.Bass("TRN2", target_bir_lowering=False, detect_race_conditions=False)

    def dram_in(name, shape, dt=F32):
        return nc.dram_tensor(name, list(shape), dt, kind="ExternalInput")

    d_x = dram_in("x", [D, RTOK])
    d_wzs = dram_in("wzs", [16 * 8 * 128, 128])
    d_win = dram_in("win", [D, 648])
    d_wzg = dram_in("wzg", [DZ, DI])
    d_wout = dram_in("wout", [DI, D])
    d_cw1 = dram_in("cw1", [D, 256])
    d_cw2 = dram_in("cw2", [256, 2])
    d_w1a = dram_in("w1a", [64, 16])
    d_w1b = dram_in("w1b", [64, 16])
    d_b1t = dram_in("b1t", [64, 16])
    d_w2t = dram_in("w2t", [64, 16])
    d_b2t = dram_in("b2t", [64, 1])
    d_dirs = dram_in("dirs", [64, D])
    d_sigs = dram_in("sigs", [64, D])
    d_csel = dram_in("csel", [64, 8])
    d_dtb = dram_in("dtb", [8, 1])
    d_alog = dram_in("alog", [8, 1])
    d_drep = dram_in("drep", [128, 4])
    d_gamc = dram_in("gamc", [128, 8])
    d_betc = dram_in("betc", [128, 8])
    d_cb1 = dram_in("cb1", [128, 2])
    d_cb2 = dram_in("cb2", [2, 1])
    d_temp = dram_in("temp", [1, 1])
    d_oscale = dram_in("oscale", [1, 1])
    d_ident = dram_in("ident", [128, 128])
    d_rev8 = dram_in("rev8", [128, 8])
    d_u01 = dram_in("u01", [128, 128])
    d_mbias = dram_in("mbias", [128, 128])
    d_onesr = dram_in("onesr", [1, 256])
    d_ones8r = dram_in("ones8r", [8, 128])
    d_onespad = dram_in("onespad", [128, 128])
    d_ones16 = dram_in("ones16", [16, 1])
    d_cc = dram_in("cc", [128, 12])
    d_hmE = dram_in("hmE", [128, 8])
    d_hmO = dram_in("hmO", [128, 8])
    d_hsel = dram_in("hsel", [8, 1024])
    d_out = nc.dram_tensor("out", [D, RTOK], F32, kind="ExternalOutput")
    d_tap = {t: nc.dram_tensor(f"tap_{t}", [128, 2048], F32, kind="ExternalOutput")
             for t in tap_names}

    P = Prog()
    for s in ("ldw", "ldx", "lz", "mm", "vv", "ss", "gg", "st", "xf"):
        P.sem(s)
    BK = Banks()

    es = contextlib.ExitStack()

    def sbuf(name, shape, dt=F32):
        return es.enter_context(nc.sbuf_tensor(name, shape, dt))

    s_wzb = sbuf("s_wzb", [128, 3072])
    s_win = sbuf("s_win", [128, 8 * 648])
    s_wzg = sbuf("s_wzg", [128, 16 * DI])
    s_wout = sbuf("s_wout", [128, 4 * D])
    s_cw1 = sbuf("s_cw1", [128, 8 * 256])
    s_cw2 = sbuf("s_cw2", [128, 4])
    s_w1a = sbuf("s_w1a", [64, 16])
    s_w1b = sbuf("s_w1b", [64, 16])
    s_b1t = sbuf("s_b1t", [64, 16])
    s_w2t = sbuf("s_w2t", [64, 16])
    s_b2t = sbuf("s_b2t", [64, 1])
    s_dirss = sbuf("s_dirss", [64, D])
    s_csel = sbuf("s_csel", [64, 8])
    s_qT = sbuf("s_qT", [128, 8 * 64])
    s_csT = sbuf("s_csT", [128, 8 * 8])
    s_dtb = sbuf("s_dtb", [8, 1])
    s_alog = sbuf("s_alog", [8, 1])
    s_negA = sbuf("s_negA", [8, 1])
    s_eA = sbuf("s_eA", [8, 1])
    s_drep = sbuf("s_drep", [128, 4])
    s_gamc = sbuf("s_gamc", [128, 8])
    s_betc = sbuf("s_betc", [128, 8])
    s_cb1 = sbuf("s_cb1", [128, 2])
    s_cb2 = sbuf("s_cb2", [2, 1])
    s_temp = sbuf("s_temp", [1, 1])
    s_it1 = sbuf("s_it1", [1, 1])
    s_it2 = sbuf("s_it2", [1, 1])
    s_itcn = sbuf("s_itcn", [128, 1])
    s_osc = sbuf("s_osc", [64, 1])
    s_ident = sbuf("s_ident", [128, 128])
    s_rev8 = sbuf("s_rev8", [128, 8])
    s_u01 = sbuf("s_u01", [128, 128])
    s_mbias = sbuf("s_mbias", [128, 128])
    s_onesr = sbuf("s_onesr", [1, 256])
    s_ones8r = sbuf("s_ones8r", [8, 128])
    s_onespad = sbuf("s_onespad", [128, 128])
    s_ones16 = sbuf("s_ones16", [16, 1])
    s_cc = sbuf("s_cc", [128, 12])
    s_hmE = sbuf("s_hmE", [128, 8])
    s_hmO = sbuf("s_hmO", [128, 8])
    s_hsel = sbuf("s_hsel", [8, 1024])

    s_xaA = sbuf("s_xaA", [128, 2048])
    s_xaB = sbuf("s_xaB", [128, 2048])
    s_x2 = sbuf("s_x2", [128, 2048])
    s_h = sbuf("s_h", [128, 2048])
    s_big = sbuf("s_big", [128, 4096])
    s_XT = sbuf("s_XT", [128, 1024])
    s_Bf = sbuf("s_Bf", [64, 256])
    s_Cf = sbuf("s_Cf", [64, 256])
    s_dtf = sbuf("s_dtf", [8, 256])
    s_sp1 = sbuf("s_sp1", [8, 256])
    s_sp2 = sbuf("s_sp2", [8, 256])
    s_sp3 = sbuf("s_sp3", [8, 256])
    s_del = sbuf("s_del", [8, 256])
    s_acl = sbuf("s_acl", [8, 256])
    s_clc = sbuf("s_clc", [8, 256])
    s_clT = sbuf("s_clT", [128, 8])
    s_dT = sbuf("s_dT", [128, 8])
    s_dTE = sbuf("s_dTE", [128, 8])
    s_dTO = sbuf("s_dTO", [128, 8])
    s_wd = sbuf("s_wd", [128, 8])
    s_w8 = sbuf("s_w8", [8, 128])
    s_w8b = sbuf("s_w8b", [8, 128])
    s_w8T = sbuf("s_w8T", [128, 8])
    s_cer = sbuf("s_cer", [1, 8])
    s_cer2 = sbuf("s_cer2", [1, 8])
    s_Btm = sbuf("s_Btm", [128, 64])
    s_XdE = sbuf("s_XdE", [128, 512])
    s_XdO = sbuf("s_XdO", [128, 512])
    s_Wx = sbuf("s_Wx", [128, 512])
    s_Epre = sbuf("s_Epre", [128, 512])
    s_Ehat = sbuf("s_Ehat", [128, 512])
    s_G = sbuf("s_G", [128, 1024])
    s_SU = sbuf("s_SU", [128, 128])
    s_Cexp = sbuf("s_Cexp", [64, 1024])
    s_SP = sbuf("s_SP", [64, 512])
    s_SP2 = sbuf("s_SP2", [64, 512])
    s_HPE = sbuf("s_HPE", [64, 512])
    s_HPO = sbuf("s_HPO", [64, 512])
    s_yT = sbuf("s_yT", [128, 1024])
    s_sg = sbuf("s_sg", [128, 512])
    s_sq = sbuf("s_sq", [128, 512])
    s_srow = sbuf("s_srow", [1, 512])
    s_mb = sbuf("s_mb", [128, 256])
    s_rb = sbuf("s_rb", [128, 256])
    s_vb = sbuf("s_vb", [128, 256])
    s_mrb = sbuf("s_mrb", [128, 256])
    s_wfin = sbuf("s_wfin", [128, 256])

    A_sigs = lambda: s_big[0:64, 0:1024]
    A_q = lambda: s_big[0:64, 1024:2048]
    A_cs = lambda: s_big[0:8, 2048:3072]
    A_dirs = lambda: s_big[0:64, 3072:4096]
    A_u = lambda: s_big[:, 0:1024]
    A_zc = lambda: s_big[:, 1024:1536]
    A_hc = lambda: s_big[:, 1536:2048]
    A_M0 = lambda: s_big[0:64, 2048:2304]
    A_M1 = lambda: s_big[0:64, 2304:2560]
    A_magr = lambda: s_big[0:1, 2048:2304]
    A_sct = lambda: s_big[:, 2560:2720]
    A_gn = lambda: s_big[:, 2720:2880]
    A_ga = lambda: s_big[:, 2880:3008]
    A_gb = lambda: s_big[:, 3008:3072]
    A_gm = lambda: s_big[:, 3072:3080]
    A_selg = lambda: s_big[:, 3080:3088]
    A_ohc = lambda: s_big[:, 3088:3096]
    A_ohl = lambda: s_big[:, 3096:3104]
    A_gt = lambda: s_big[:, 3104:3112]
    A_oht = lambda: s_big[:, 3112:3176]
    A_ohf = lambda: s_big[0:64, 3176:3432]
    A_soh = lambda: s_big[0:64, 3432:3688]
    A_comp = lambda: s_big[0:2, 3688:3944]
    A_hh = lambda: s_big[0:16, 2560:2816]
    A_t1 = lambda: s_big[0:16, 2816:3072]
    A_tmp3 = lambda: s_big
    A_zga = lambda: s_big[:, 0:1024]
    A_zgb = lambda: s_big[0:128, 2048:3072]
    A_u2 = lambda: s_big[:, 1024:2048]

    ps = es.enter_context(nc.psum_tensor("ps", [128, 4096], mybir.dt.float32))

    def PB(bank, off=0, w=512, rows=128, r0=0):
        return ps[r0:r0 + rows, bank * 512 + off: bank * 512 + off + w]

    cc = lambda i, p=128: s_cc[0:p, i:i + 1]

    nw = [0]

    def wload(fn):
        nw[0] += 1
        P.op("sync", fn, inc="ldw", inc_n=16)

    for k in range(8):
        wload(lambda sy, k=k: sy.dma_start(out=s_win[:, k * 648:(k + 1) * 648],
                                           in_=d_win[k * 128:(k + 1) * 128, :]))
    for k in range(16):
        wload(lambda sy, k=k: sy.dma_start(out=s_wzg[:, k * DI:(k + 1) * DI],
                                           in_=d_wzg[k * 128:(k + 1) * 128, :]))
    for k in range(4):
        wload(lambda sy, k=k: sy.dma_start(out=s_wout[:, k * D:(k + 1) * D],
                                           in_=d_wout[k * 128:(k + 1) * 128, :]))
    for k in range(8):
        wload(lambda sy, k=k: sy.dma_start(out=s_cw1[:, k * 256:(k + 1) * 256],
                                           in_=d_cw1[k * 128:(k + 1) * 128, :]))
    for k in range(2):
        wload(lambda sy, k=k: sy.dma_start(out=s_cw2[:, k * 2:(k + 1) * 2],
                                           in_=d_cw2[k * 128:(k + 1) * 128, :]))
    wload(lambda sy: sy.dma_start(out=A_sigs(), in_=d_sigs[:, :]))
    wload(lambda sy: sy.dma_start(out=A_dirs(), in_=d_dirs[:, :]))
    for a, b in [(s_w1a, d_w1a), (s_w1b, d_w1b), (s_b1t, d_b1t), (s_w2t, d_w2t),
                 (s_b2t, d_b2t), (s_csel, d_csel), (s_dtb, d_dtb), (s_alog, d_alog),
                 (s_drep, d_drep), (s_gamc, d_gamc), (s_betc, d_betc), (s_cb1, d_cb1),
                 (s_cb2, d_cb2), (s_temp, d_temp), (s_ident, d_ident), (s_rev8, d_rev8),
                 (s_u01, d_u01), (s_mbias, d_mbias), (s_onesr, d_onesr),
                 (s_ones8r, d_ones8r), (s_onespad, d_onespad), (s_ones16, d_ones16),
                 (s_cc, d_cc), (s_hmE, d_hmE), (s_hmO, d_hmO), (s_hsel, d_hsel)]:
        wload(lambda sy, a=a, b=b: sy.dma_start(out=a[:], in_=b[:]))
    wload(lambda sy: sy.dma_start(out=s_osc[:, :],
                                  in_=bass.AP(d_oscale, 0, [[0, 64], [1, 1]])))
    LDW = nw[0] * 16
    WL = [("ldw", LDW)]

    P.op("vector", lambda v: v.tensor_scalar(s_G[0:64, 0:1024], A_sigs(), cc(CTH, 64),
                                             None, OP.is_gt), waits=WL, inc="vv")
    P.op("vector", lambda v: v.tensor_scalar(s_yT[0:64, 0:1024], A_sigs(), cc(CNTH, 64),
                                             None, OP.is_lt), inc="vv")
    vq = P.op("vector", lambda v: v.tensor_tensor(A_q(), s_G[0:64, 0:1024],
                                                  s_yT[0:64, 0:1024], OP.subtract), inc="vv")
    for half in range(2):
        m = P.op("tensor", lambda t, h=half: t.matmul(PB(0, 0, 512, 8), s_csel[:, :],
                                                      A_q()[:, h * 512:(h + 1) * 512],
                                                      start=True, stop=True),
                 waits=[("vv", vq)] + BK.writer_waits([0]), inc="mm")
        P.op("vector", lambda v: v.tensor_scalar(s_sq[0:8, 0:512], PB(0, 0, 512, 8),
                                                 cc(CZERO, 8), None, OP.is_gt),
             waits=[("mm", m)], inc="vv")
        P.op("vector", lambda v: v.tensor_scalar(s_yT[0:8, 0:512], PB(0, 0, 512, 8),
                                                 cc(CZERO, 8), None, OP.is_lt),
             inc="vv")
        e = P.op("vector", lambda v, h=half: v.tensor_tensor(
            A_cs()[:, h * 512:(h + 1) * 512], s_sq[0:8, 0:512], s_yT[0:8, 0:512],
            OP.subtract), inc="vv")
        BK.consumed([0], "vv", e)
    for c in range(8):
        m = P.op("tensor", lambda t, c=c: t.transpose(PB(1, 0, 64),
                                                      A_q()[:, c * 128:(c + 1) * 128],
                                                      s_ident[0:64, 0:64]),
                 waits=BK.writer_waits([1]), inc="mm")
        v = P.op("vector", lambda v, c=c: v.tensor_copy(s_qT[:, c * 64:(c + 1) * 64],
                                                        PB(1, 0, 64)),
                 waits=[("mm", m)], inc="vv")
        BK.consumed([1], "vv", v)
        m = P.op("tensor", lambda t, c=c: t.transpose(PB(1, 64, 8),
                                                      A_cs()[:, c * 128:(c + 1) * 128],
                                                      s_ident[0:8, 0:8]),
                 waits=BK.writer_waits([1]), inc="mm")
        v = P.op("vector", lambda v, c=c: v.tensor_copy(s_csT[:, c * 8:(c + 1) * 8],
                                                        PB(1, 64, 8)),
                 waits=[("mm", m)], inc="vv")
        BK.consumed([1], "vv", v)
    e = P.op("scalar", lambda s: s.activation(s_eA[:, :], s_alog[:, :], AF.Exp),
             waits=WL, inc="ss")
    P.op("vector", lambda v: v.tensor_scalar(s_negA[:, :], s_eA[:, :], cc(CNEG1, 8), None,
                                             OP.mult), waits=[("ss", e)], inc="vv")
    P.op("vector", lambda v: v.tensor_scalar(s_it1[:, :], s_temp[:, :], cc(C1E6, 1), None,
                                             OP.add), waits=WL, inc="vv")
    vi = P.op("vector", lambda v: v.reciprocal(s_it2[:, :], s_it1[:, :]), inc="vv")
    m = P.op("tensor", lambda t: t.matmul(PB(1, 128, 1), s_onesr[0:1, 0:128], s_it2[:, :],
                                          start=True, stop=True),
             waits=[("vv", vi)] + BK.writer_waits([1]), inc="mm")
    v = P.op("vector", lambda v: v.tensor_scalar(s_itcn[:, :], PB(1, 128, 1), cc(CNEG1),
                                                 None, OP.mult), waits=[("mm", m)], inc="vv")
    BK.consumed([1], "vv", v)
    P.op("vector", lambda v: v.tensor_scalar(s_dirss[:, :], A_dirs(), s_osc[:, :],
                                             None, OP.mult), waits=WL, inc="vv")

    def ln_stats(src, wv=None):
        w01 = BK.writer_waits([0, 1])
        last = None
        f32r = mybir.dt.float32r
        for c in range(8):
            sqw = [("mm", sq_slot[c % 2])] if sq_slot[c % 2] else []
            if c < 2 and wv is not None:
                sqw.append(("vv", wv))
            sc = P.op("scalar", lambda s, c=c: s.activation(
                s_sq[:, (c % 2) * 256:(c % 2) * 256 + 256], src[:, c * 256:(c + 1) * 256],
                AF.Square), waits=sqw, inc="ss")
            P.op("tensor", lambda t, c=c: t.matmul(PB(0, 0, 256), s_onespad[:, :],
                                                   src[:, c * 256:(c + 1) * 256],
                                                   start=(c == 0), stop=(c == 7)),
                 waits=(w01 + ([("vv", wv)] if wv is not None else [])) if c == 0 else [],
                 inc=None)
            last = P.op("tensor", lambda t, c=c: t.matmul(
                PB(1, 0, 256), s_onespad[:, :],
                s_sq[:, (c % 2) * 256:(c % 2) * 256 + 256],
                start=(c == 0), stop=(c == 7)), waits=[("ss", sc)], inc="mm")
            sq_slot[c % 2] = last
        vr = P.op("vector", lambda v: v.tensor_copy(s_srow[:, 0:256], PB(0, 0, 256, 1)),
                  waits=[("mm", last)], inc="vv")
        vr2 = P.op("vector", lambda v: v.tensor_copy(s_srow[:, 256:512], PB(1, 0, 256, 1)),
                   inc="vv")
        BK.consumed([0], "vv", vr)
        BK.consumed([1], "vv", vr2)
        mb = P.op("tensor", lambda t: t.matmul(PB(0, 0, 512), s_onesr[0:1, 0:128],
                                               s_srow[:, :], start=True, stop=True),
                  waits=[("vv", vr2)] + BK.writer_waits([0]), inc="mm")
        P.op("vector", lambda v: v.tensor_scalar(s_mb[:, :], PB(0, 0, 256), cc(C1024),
                                                 None, OP.mult), waits=[("mm", mb)], inc="vv")
        P.op("vector", lambda v: v.tensor_tensor(s_vb[:, :], s_mb[:, :], s_mb[:, :],
                                                 OP.mult), inc="vv")
        vv = P.op("vector", lambda v: v.scalar_tensor_tensor(s_rb[:, :], PB(0, 256, 256),
                                                             cc(C1024), s_vb[:, :],
                                                             OP.mult, OP.subtract), inc="vv")
        BK.consumed([0], "vv", vv)
        sv = P.op("scalar", lambda s: s.activation(s_vb[:, :], s_rb[:, :], AF.Ln,
                                                   bias=cc(CEPS)), waits=[("vv", vv)],
                  inc="ss")
        sv2 = P.op("scalar", lambda s: s.activation(s_rb[:, :], s_vb[:, :], AF.Exp,
                                                    scale=cc(CNEGH)), inc="ss")
        return P.op("vector", lambda v: v.nop(), waits=[("ss", sv2)], inc="vv")

    def ln_apply(src, dst, tmp, extra=()):
        P.op("vector", lambda v: v.tensor_tensor(
            tmp[:, 0:2048].rearrange("p (c f) -> p c f", c=8),
            src[:, :].rearrange("p (c f) -> p c f", c=8),
            rep(s_mb[:, :], 8, 256), OP.subtract),
            waits=list(extra), inc="vv")
        return P.op("vector", lambda v: v.tensor_tensor(
            dst[:, :].rearrange("p (c f) -> p c f", c=8),
            tmp[:, 0:2048].rearrange("p (c f) -> p c f", c=8),
            rep(s_rb[:, :], 8, 256), OP.mult), inc="vv")

    taps_set = set(tap_names)
    TAP_TILE = 0

    def tap(name, ap_fn, tile):
        if name not in taps_set or tile != TAP_TILE:
            return
        src = ap_fn()
        rr, cc_ = src.shape[0], int(np.prod(src.shape[1:]))
        P.op("gpsimd", lambda gp, name=name, src=src, rr=rr, cc_=cc_: gp.dma_start(
            out=d_tap[name][0:rr, 0:cc_], in_=src),
            waits=[("vv", P.count["vv"]), ("ss", P.count["ss"]), ("mm", P.count["mm"])],
            inc="gg", inc_n=16)
        P.op("vector", lambda v: None, waits=[("gg", P.count["gg"])])
        P.op("scalar", lambda s: None, waits=[("gg", P.count["gg"])])
        P.op("tensor", lambda t_: None, waits=[("gg", P.count["gg"])])

    sq_slot = [None, None]
    st_count = 0
    last_sce = [0]
    z_ring = {}
    for t in range(n_tiles):
        s_xa = s_xaA if t % 2 == 0 else s_xaB
        P.op("sync", lambda sy, t=t, s_xa=s_xa: sy.dma_start(
            out=s_xa[:, :].rearrange("p (c f) -> p c f", c=8),
            in_=bass.AP(d_x, t * TT, [[RTOK, 128], [128 * RTOK, 8], [1, TT]])),
            waits=([("xf", t - 1)] if t >= 2 else []), inc="ldx", inc_n=16)
        XL = [("ldx", 16 * (t + 1))]
        P.op("vector", lambda v: None, waits=XL)
        P.op("scalar", lambda s: None, waits=XL)
        P.op("tensor", lambda t_: None, waits=XL)

        ln_stats(s_xa)
        vh1 = ln_apply(s_xa, s_h, s_x2, extra=[("st", st_count)])
        tap("h1", lambda: s_h[:, :], t)

        xe = []
        for fc in range(4):
            bank = 2 + (fc % 2)
            for k in range(8):
                P.op("tensor", lambda t_, fc=fc, k=k, bank=bank: t_.matmul(
                    PB(bank, 0, 256), s_win[:, k * 648 + fc * 128: k * 648 + fc * 128 + 128],
                    s_h[:, k * 256:(k + 1) * 256], start=(k == 0), stop=(k == 7)),
                    waits=([("vv", vh1)] + BK.writer_waits([bank])) if k == 0 else [],
                    inc="mm" if k == 7 else None)
            mmx = P.count["mm"]
            v = P.op("vector", lambda v, fc=fc, bank=bank: v.tensor_copy(
                s_XT[:, fc * 256:(fc + 1) * 256], PB(bank, 0, 256)),
                waits=[("mm", mmx)], inc="vv")
            BK.consumed([bank], "vv", v)
            xe.append(v)
        for c0, w, dst, r in [(640, 8, s_dtf, 8), (512, 64, s_Bf, 64), (576, 64, s_Cf, 64)]:
            for k in range(8):
                P.op("tensor", lambda t_, k=k, c0=c0, w=w, r=r: t_.matmul(
                    PB(4, 0, 256, r), s_win[:, k * 648 + c0: k * 648 + c0 + w],
                    s_h[:, k * 256:(k + 1) * 256], start=(k == 0), stop=(k == 7)),
                    waits=BK.writer_waits([4]) if k == 0 else [], inc="mm" if k == 7 else None)
            mmb = P.count["mm"]
            v = P.op("vector", lambda v, dst=dst, r=r: v.tensor_copy(dst[:, :],
                                                                     PB(4, 0, 256, r)),
                     waits=[("mm", mmb)], inc="vv")
            BK.consumed([4], "vv", v)
            if dst is s_dtf:
                vdtf = v
        vbc = P.count["vv"]

        P.op("scalar", lambda s: s.activation(s_sp1[:, :], s_dtf[:, :], AF.Relu,
                                              bias=s_dtb[:, :]), waits=[("vv", vdtf)],
             inc="ss")
        P.op("scalar", lambda s: s.activation(s_sp2[:, :], s_dtf[:, :], AF.Abs,
                                              bias=s_dtb[:, :]), inc="ss")
        s3 = P.op("scalar", lambda s: s.activation(s_sp3[:, :], s_sp2[:, :], AF.Exp,
                                                   scale=cc(CNEG1, 8)), inc="ss")
        v1 = P.op("vector", lambda v: v.tensor_scalar(s_sp2[:, :], s_sp3[:, :], cc(CONE, 8),
                                                      None, OP.add), waits=[("ss", s3)],
                  inc="vv")
        s4 = P.op("scalar", lambda s: s.activation(s_sp3[:, :], s_sp2[:, :], AF.Ln),
                  waits=[("vv", v1)], inc="ss")
        vdl = P.op("vector", lambda v: v.tensor_tensor(s_del[:, :], s_sp1[:, :], s_sp3[:, :],
                                                       OP.add), waits=[("ss", s4)], inc="vv")
        va = P.op("vector", lambda v: v.tensor_scalar(s_acl[:, :], s_del[:, :], s_negA[:, :],
                                                      None, OP.mult), inc="vv")

        def wz_dma(zf, gz):
            P.op("sync", lambda sy, zf=zf, gz=gz: sy.dma_start(
                out=bass.AP(s_wzb, (gz % 3) * 1024, [[3072, 128], [128, 8], [1, 128]]),
                in_=bass.AP(d_wzs, zf * 1024 * 128, [[128, 128], [128 * 128, 8], [1, 128]])),
                waits=[("mm", z_ring.get(gz - 3, 0))], inc="lz", inc_n=16)

        def scan_chain(c, out):
            g = t * CPT + c
            first = (g % CHPS == 0)
            co = c * 128
            vcl = P.op("vector", lambda v, co=co: v.tensor_tensor_scan(
                s_clc[:, co:co + 128], s_ones8r[:, :], s_acl[:, co:co + 128], 0.0,
                OP.mult, OP.add), waits=[("vv", va)], inc="vv")
            m = P.op("tensor", lambda t_, co=co: t_.transpose(PB(1, 0, 8),
                                                              s_clc[:, co:co + 128],
                                                              s_ident[0:8, 0:8]),
                     waits=[("vv", vcl)] + BK.writer_waits([1]), inc="mm")
            vclT = P.op("vector", lambda v: v.tensor_copy(s_clT[:, :], PB(1, 0, 8)),
                        waits=[("mm", m)], inc="vv")
            BK.consumed([1], "vv", vclT)
            yield
            m = P.op("tensor", lambda t_, co=co: t_.transpose(PB(1, 8, 8),
                                                              s_del[:, co:co + 128],
                                                              s_ident[0:8, 0:8]),
                     waits=BK.writer_waits([1]) + [("vv", vdl)], inc="mm")
            vdT = P.op("vector", lambda v: v.tensor_copy(s_dT[:, :], PB(1, 8, 8)),
                       waits=[("mm", m)], inc="vv")
            BK.consumed([1], "vv", vdT)
            yield
            vw = P.op("vector", lambda v, co=co: v.tensor_scalar(
                s_w8[:, :], s_clc[:, co:co + 128], s_clc[:, co + 127:co + 128], None,
                OP.subtract), waits=[("vv", vcl)], inc="vv")
            sw = P.op("scalar", lambda s: s.activation(s_w8b[:, :], s_w8[:, :], AF.Exp,
                                                       scale=cc(CNEG1, 8)),
                      waits=[("vv", vw)], inc="ss")
            m = P.op("tensor", lambda t_: t_.transpose(PB(1, 16, 8), s_w8b[:, :],
                                                       s_ident[0:8, 0:8]),
                     waits=[("ss", sw)] + BK.writer_waits([1]), inc="mm")
            vw8 = P.op("vector", lambda v: v.tensor_copy(s_w8T[:, :], PB(1, 16, 8)),
                       waits=[("mm", m)], inc="vv")
            BK.consumed([1], "vv", vw8)
            yield
            m = P.op("tensor", lambda t_, co=co: t_.transpose(PB(1, 64, 64),
                                                              s_Bf[:, co:co + 128],
                                                              s_ident[0:64, 0:64]),
                     waits=BK.writer_waits([1]), inc="mm")
            vB = P.op("vector", lambda v: v.tensor_copy(s_Btm[:, :], PB(1, 64, 64)),
                      waits=[("mm", m)], inc="vv")
            BK.consumed([1], "vv", vB)
            yield
            mS = P.op("tensor", lambda t_, co=co: t_.matmul(
                PB(1, 128, 128), s_Bf[:, co:co + 128], s_Cf[:, co:co + 128],
                start=True, stop=True), waits=BK.writer_waits([1]), inc="mm")
            vSU = P.op("vector", lambda v: v.tensor_tensor(s_SU[:, :], PB(1, 128, 128),
                                                           s_u01[:, :], OP.mult),
                       waits=[("mm", mS)], inc="vv")
            BK.consumed([1], "vv", vSU)
            yield
            vde = P.op("vector", lambda v: v.tensor_tensor(s_dTE[:, :], s_dT[:, :],
                                                           s_hmE[:, :], OP.mult),
                       waits=[("vv", vdT)], inc="vv")
            vdo = P.op("vector", lambda v: v.tensor_tensor(s_dTO[:, :], s_dT[:, :],
                                                           s_hmO[:, :], OP.mult), inc="vv")
            vwd = P.op("vector", lambda v: v.tensor_tensor(s_wd[:, :], s_w8T[:, :],
                                                           s_dT[:, :], OP.mult),
                       waits=[("vv", vw8)], inc="vv")
            for fc in range(4):
                P.op("tensor", lambda t_, fc=fc, co=co: t_.transpose(
                    PB(0, fc * 128, 128), s_XT[:, fc * 256 + co: fc * 256 + co + 128],
                    s_ident[:, :]),
                    waits=(BK.writer_waits([0]) + [("vv", xe[3])]) if fc == 0 else [],
                    inc="mm" if fc == 3 else None)
            mX = P.count["mm"]
            vXE = P.op("vector", lambda v: v.tensor_tensor(
                s_XdE[:, :].rearrange("p (h d) -> p h d", h=8), PB(0, 0, 512),
                col_rep(s_dTE[:, :], 8, 64), OP.mult),
                waits=[("mm", mX), ("vv", vde)], inc="vv")
            vXO = P.op("vector", lambda v: v.tensor_tensor(
                s_XdO[:, :].rearrange("p (h d) -> p h d", h=8), PB(0, 0, 512),
                col_rep(s_dTO[:, :], 8, 64), OP.mult), waits=[("vv", vdo)], inc="vv")
            vWx = P.op("vector", lambda v: v.tensor_tensor(
                s_Wx[:, :].rearrange("p (h d) -> p h d", h=8), PB(0, 0, 512),
                col_rep(s_wd[:, :], 8, 64), OP.mult), waits=[("vv", vwd)], inc="vv")
            BK.consumed([0], "vv", vWx)
            yield
            mb8 = None
            for h in range(8):
                mb8 = P.op("tensor", lambda t_, h=h, co=co: t_.matmul(
                    PB(2 + h // 4, (h % 4) * 128, 128), s_hsel[:, h * 128:(h + 1) * 128],
                    s_clc[:, co:co + 128], start=True, stop=True),
                    waits=(BK.writer_waits([2, 3]) + [("vv", vcl)]) if h == 0 else [],
                    inc="mm" if h == 7 else None)
            se1 = P.op("scalar", lambda s: s.activation(s_G[0:64, 0:512], PB(2, 0, 512, 64),
                                                        AF.Exp), waits=[("mm", mb8)], inc="ss")
            se2 = P.op("scalar", lambda s: s.activation(s_G[0:64, 512:1024],
                                                        PB(3, 0, 512, 64), AF.Exp), inc="ss")
            yield
            vce = P.op("vector", lambda v, co=co: v.tensor_tensor(
                s_Cexp[:, :].rearrange("p (h d) -> p h d", h=8),
                rep(s_Cf[:, co:co + 128], 8, 128),
                s_G[0:64, :].rearrange("p (h d) -> p h d", h=8), OP.mult),
                waits=[("ss", se2)], inc="vv")
            vGs = []
            yield
            for gi in range(2):
                for hh in range(4):
                    h = gi * 4 + hh
                    P.op("vector", lambda v, h=h, hh=hh, gi=gi: v.scalar_tensor_tensor(
                        s_Epre[:, hh * 128:(hh + 1) * 128], PB(2 + gi, hh * 128, 128),
                        s_clT[:, h:h + 1], s_mbias[:, :], OP.subtract, OP.add),
                        waits=[("vv", vclT), ("mm", mb8)] if (gi == 0 and hh == 0) else [],
                        inc="vv")
                sE = P.op("scalar", lambda s: s.activation(s_Ehat[:, :], s_Epre[:, :],
                                                           AF.Exp),
                          waits=[("vv", P.count["vv"])], inc="ss")
                vG = P.op("vector", lambda v, gi=gi: v.tensor_tensor(
                    s_G[:, gi * 512:(gi + 1) * 512].rearrange("p (h d) -> p h d", h=4),
                    s_Ehat[:, :].rearrange("p (h d) -> p h d", h=4),
                    rep(s_SU[:, :], 4, 128), OP.mult),
                    waits=[("ss", sE), ("vv", vSU)], inc="vv")
                vGs.append(vG)
                yield
            ve2 = P.count["vv"]
            BK.consumed([2], "vv", ve2)
            BK.consumed([3], "vv", ve2)
            BK.consumed([2], "ss", se1)
            BK.consumed([3], "ss", se2)
            vpo = None
            if not first:
                P.op("vector", lambda v: v.tensor_tensor(
                    s_HPE[:, :].rearrange("p (h d) -> p h d", h=8),
                    s_SP[:, :].rearrange("p (h d) -> p h d", h=8),
                    col_rep(s_hmE[0:64, :], 8, 64), OP.mult), inc="vv")
                vpo = P.op("vector", lambda v: v.tensor_tensor(
                    s_HPO[:, :].rearrange("p (h d) -> p h d", h=8),
                    s_SP[:, :].rearrange("p (h d) -> p h d", h=8),
                    col_rep(s_hmO[0:64, :], 8, 64), OP.mult), inc="vv")
                P.op("gpsimd", lambda gp: gp.dma_start(out=s_cer[:, :],
                                                       in_=s_clT[127:128, :]),
                     waits=[("vv", vclT), ("ss", last_sce[0])], inc="gg", inc_n=16)
                sce = P.op("scalar", lambda s: s.activation(s_cer2[:, :], s_cer[:, :],
                                                            AF.Exp),
                           waits=[("gg", P.count["gg"])], inc="ss")
                last_sce[0] = sce
                mEH = P.op("tensor", lambda t_: t_.matmul(PB(1, 256, 8, 64),
                                                          s_onesr[0:1, 0:64], s_cer2[:, :],
                                                          start=True, stop=True),
                           waits=[("ss", sce)] + BK.writer_waits([1]), inc="mm")
                vsp = P.op("vector", lambda v: v.tensor_tensor(
                    s_SP2[:, :].rearrange("p (h d) -> p h d", h=8),
                    s_SP[:, :].rearrange("p (h d) -> p h d", h=8),
                    col_rep(PB(1, 256, 8, 64), 8, 64), OP.mult),
                    waits=[("mm", mEH), ("vv", vpo)], inc="vv")
                BK.consumed([1], "vv", vsp)
            out.update(first=first, co=co, vXO=vXO, vWx=vWx, vB=vB, vce=vce,
                       vGs=vGs, vpo=vpo)
            yield

        def z_half(h0, gen):
            def record_zg(zf):
                zgp0 = zgp_mm.get(zf, None)
                for j in range(4):
                    bank, off = 4 + j // 2, (j % 2) * 256
                    P.op("tensor", lambda t_, zf=zf, j=j, bank=bank, off=off: t_.matmul(
                        PB(bank, off, 256),
                        s_wzg[:, zf * DI + j * 128: zf * DI + j * 128 + 128],
                        A_zc()[:, (zf % 2) * 256:(zf % 2) * 256 + 256],
                        start=True, stop=True),
                        waits=([("vv", zc_vv[zf])] if j == 0 else []) +
                              (BK.writer_waits([4, 5]) if j == 0 else []),
                        inc="mm" if j == 3 else None)
                zgp = P.count["mm"]
                zgp_mm[zf] = zgp
                src, dst = (A_zga, A_zgb) if zf % 2 == 1 else (A_zgb, A_zga)
                if zf == 0:
                    vz = P.op("vector", lambda v: v.tensor_copy(
                        A_zga(), ps[:, 4 * 512: 4 * 512 + 1024]),
                        waits=[("mm", zgp)], inc="vv")
                else:
                    vz = P.op("vector", lambda v, src=src, dst=dst: v.tensor_tensor(
                        dst(), src(), ps[:, 4 * 512: 4 * 512 + 1024], OP.add),
                        waits=[("mm", zgp)], inc="vv")
                BK.consumed([4, 5], "vv", vz)

            for zf in range(h0, h0 + 8):
                gz = t * 16 + zf
                zb = 6 + zf % 2
                for k in range(8):
                    P.op("tensor", lambda t_, k=k, zb=zb, gz=gz: t_.matmul(
                        PB(zb, 0, 256),
                        s_wzb[:, (gz % 3) * 1024 + k * 128: (gz % 3) * 1024 + k * 128 + 128],
                        s_h[:, k * 256:(k + 1) * 256], start=(k == 0), stop=(k == 7)),
                        waits=(BK.writer_waits([zb]) + [("lz", 16 * (gz + 1))]) if k == 0
                        else [], inc="mm" if k == 7 else None)
                mz = P.count["mm"]
                z_ring[gz] = mz
                if zf + 3 < 16:
                    wz_dma(zf + 3, t * 16 + zf + 3)
                elif t + 1 < n_tiles:
                    nzf = zf + 3 - 16
                    wz_dma(nzf, (t + 1) * 16 + nzf)
                sth = P.op("scalar", lambda sc, zf=zf, zb=zb: sc.activation(
                    s_big[:, 1536 + (zf % 2) * 256: 1536 + (zf % 2) * 256 + 256], PB(zb, 0, 256), AF.Tanh,
                    scale=cc(CHALF)),
                    waits=[("mm", mz)] + ([("vv", zc_vv.get(zf - 2, None) or 0)]
                                          if zf >= 2 else []), inc="ss")
                P.op("vector", lambda v, zf=zf: v.tensor_scalar(
                    s_sg[:, (zf % 2) * 256:(zf % 2) * 256 + 256],
                    s_big[:, 1536 + (zf % 2) * 256: 1536 + (zf % 2) * 256 + 256],
                    cc(CHALF), cc(CHALF), OP.mult, OP.add),
                    waits=[("ss", sth)], inc="vv")
                vzc = P.op("vector", lambda v, zf=zf, zb=zb: v.tensor_tensor(
                    A_zc()[:, (zf % 2) * 256:(zf % 2) * 256 + 256],
                    s_sg[:, (zf % 2) * 256:(zf % 2) * 256 + 256],
                    PB(zb, 0, 256), OP.mult), inc="vv")
                zc_vv[zf] = vzc
                BK.consumed([zb], "vv", vzc)
                if zf > h0:
                    record_zg(zf - 1)
                next(gen, None)
                next(gen, None)
            record_zg(h0 + 7)
            for _ in gen:
                pass

        def y_state(ch):
            first, co = ch["first"], ch["co"]
            for fc in range(4):
                hE, hO = 2 * fc, 2 * fc + 1
                w0 = (BK.writer_waits([4]) + [("vv", ch["vXO"]), ("vv", ch["vGs"][1]),
                                              ("vv", ch["vce"])]
                      + ([("vv", ch["vpo"])] if not first else [])) if fc == 0 else []
                P.op("tensor", lambda t_, fc=fc, hE=hE: t_.matmul(
                    PB(4, fc * 128, 128), s_XdE[:, fc * 128:(fc + 1) * 128],
                    s_G[:, hE * 128:(hE + 1) * 128], start=True, stop=False),
                    waits=w0, inc=None)
                P.op("tensor", lambda t_, fc=fc, hO=hO, first=first: t_.matmul(
                    PB(4, fc * 128, 128), s_XdO[:, fc * 128:(fc + 1) * 128],
                    s_G[:, hO * 128:(hO + 1) * 128], start=False, stop=first),
                    inc="mm" if (first and fc == 3) else None)
                if not first:
                    P.op("tensor", lambda t_, fc=fc, hE=hE: t_.matmul(
                        PB(4, fc * 128, 128), s_HPE[:, fc * 128:(fc + 1) * 128],
                        s_Cexp[:, hE * 128:(hE + 1) * 128], start=False, stop=False),
                        inc=None)
                    P.op("tensor", lambda t_, fc=fc, hO=hO: t_.matmul(
                        PB(4, fc * 128, 128), s_HPO[:, fc * 128:(fc + 1) * 128],
                        s_Cexp[:, hO * 128:(hO + 1) * 128], start=False, stop=True),
                        inc="mm" if fc == 3 else None)
            ym = P.count["mm"]
            for fc in range(4):
                P.op("tensor", lambda t_, fc=fc: t_.matmul(
                    PB(5, fc * 128, 128, 64), s_Btm[:, :], s_Wx[:, fc * 128:(fc + 1) * 128],
                    start=True, stop=True),
                    waits=(BK.writer_waits([5]) + [("vv", ch["vWx"]), ("vv", ch["vB"])])
                    if fc == 0 else [], inc="mm" if fc == 3 else None)
            mst = P.count["mm"]
            if not first:
                vH = P.op("vector", lambda v: v.tensor_tensor(s_SP[:, :], s_SP2[:, :],
                                                              PB(5, 0, 512, 64), OP.add),
                          waits=[("mm", mst)], inc="vv")
            else:
                vH = P.op("vector", lambda v: v.tensor_copy(s_SP[:, :], PB(5, 0, 512, 64)),
                          waits=[("mm", mst)], inc="vv")
            BK.consumed([5], "vv", vH)
            for fc in range(4):
                v = P.op("vector", lambda v, fc=fc, co=co: v.scalar_tensor_tensor(
                    s_yT[:, fc * 256 + co: fc * 256 + co + 128],
                    s_XT[:, fc * 256 + co: fc * 256 + co + 128],
                    s_drep[:, fc:fc + 1], PB(4, fc * 128, 128), OP.mult, OP.add),
                    waits=[("mm", ym)], inc="vv")
                BK.consumed([4], "vv", v)

        if t == 0:
            for i in range(3):
                wz_dma(i, i)
        zgp_mm = {}
        zc_vv = {}
        ch0 = {}
        z_half(0, scan_chain(0, ch0))
        y_state(ch0)
        ch1 = {}
        z_half(8, scan_chain(1, ch1))
        y_state(ch1)
        tap("yT", lambda: bass.AP(s_yT, 0, [[1024, 128], [1, 1024]]), t)

        vu = P.op("vector", lambda v: v.tensor_tensor(A_u2(), A_zgb(), s_yT[:, :],
                                                      OP.mult), inc="vv")
        vx2 = None
        for cC in range(8):
            bank = 4 + (cC % 2)
            for k in range(4):
                P.op("tensor", lambda t_, cC=cC, k=k, bank=bank: t_.matmul(
                    PB(bank, 0, 256), s_wout[:, k * D + cC * 128: k * D + cC * 128 + 128],
                    A_u2()[:, k * 256:(k + 1) * 256], start=(k == 0), stop=(k == 3)),
                    waits=(BK.writer_waits([bank]) + ([("vv", vu)] if cC == 0 else []))
                    if k == 0 else [], inc="mm" if k == 3 else None)
            ma = P.count["mm"]
            vx2 = P.op("vector", lambda v, cC=cC, bank=bank, s_xa=s_xa: v.tensor_tensor(
                s_x2[:, cC * 256:(cC + 1) * 256], PB(bank, 0, 256),
                s_xa[:, cC * 256:(cC + 1) * 256], OP.add), waits=[("mm", ma)], inc="vv")
            BK.consumed([bank], "vv", vx2)
        tap("u", lambda: bass.AP(s_big, 0, [[4096, 128], [1, 1024]]), t)
        tap("x2", lambda: s_x2[:, :], t)

        ln_stats(s_x2, wv=vx2)
        vh2 = ln_apply(s_x2, s_h, s_xa)
        tap("h2", lambda: s_h[:, :], t)

        hcs = []
        for f in range(2):
            bank = 6 + f
            for k in range(8):
                P.op("tensor", lambda t_, f=f, k=k, bank=bank: t_.matmul(
                    PB(bank, 0, 256), s_cw1[:, k * 256 + f * 128: k * 256 + f * 128 + 128],
                    s_h[:, k * 256:(k + 1) * 256], start=(k == 0), stop=(k == 7)),
                    waits=BK.writer_waits([bank]) if k == 0 else [],
                    inc="mm" if k == 7 else None)
            mc = P.count["mm"]
            sg = P.op("scalar", lambda s, f=f, bank=bank: s.activation(
                A_hc()[:, f * 256:(f + 1) * 256], PB(bank, 0, 256), AF.Gelu,
                bias=s_cb1[:, f:f + 1]), waits=[("mm", mc)], inc="ss")
            hcs.append(sg)
            BK.consumed([bank], "ss", sg)
        for k in range(2):
            P.op("tensor", lambda t_, k=k: t_.matmul(
                PB(6, 0, 256, 2), s_cw2[:, k * 2:(k + 1) * 2],
                A_hc()[:, k * 256:(k + 1) * 256], start=(k == 0), stop=(k == 1)),
                waits=[("ss", hcs[k])] + (BK.writer_waits([6]) if k == 0 else []),
                inc="mm" if k == 1 else None)
        mcp = P.count["mm"]
        scp = P.op("scalar", lambda s: s.activation(A_comp()[:, :], PB(6, 0, 256, 2),
                                                    AF.Tanh, bias=s_cb2[:, :]),
                   waits=[("mm", mcp)], inc="ss")
        BK.consumed([6], "ss", scp)
        for k in range(8):
            P.op("tensor", lambda t_, k=k: t_.matmul(
                PB(2, 0, 256, 64), s_qT[:, k * 64:(k + 1) * 64],
                s_h[:, k * 256:(k + 1) * 256], start=(k == 0), stop=(k == 7)),
                waits=(BK.writer_waits([2]) + [("vv", vh2)]) if k == 0 else [], inc=None)
            P.op("tensor", lambda t_, k=k: t_.matmul(
                PB(3, 0, 256, 8), s_csT[:, k * 8:(k + 1) * 8],
                s_h[:, k * 256:(k + 1) * 256], start=(k == 0), stop=(k == 7)),
                waits=BK.writer_waits([3]) if k == 0 else [],
                inc="mm" if k == 7 else None)
        msc = P.count["mm"]
        for hf in range(2):
            ho = hf * 128
            so_ = hf * 80
            vts = P.op("vector", lambda v, ho=ho: v.tensor_copy(A_ga()[0:64, 0:128],
                                                                PB(2, ho, 128, 64)),
                       waits=[("mm", msc)], inc="vv")
            vcs3 = P.op("vector", lambda v, ho=ho: v.tensor_copy(s_sq[0:8, 0:128],
                                                                 PB(3, ho, 128, 8)),
                        inc="vv")
            if hf == 1:
                BK.consumed([2], "vv", vcs3)
                BK.consumed([3], "vv", vcs3)
            mt1 = P.op("tensor", lambda t_: t_.transpose(PB(4, 0, 64), A_ga()[0:64, 0:128],
                                                         s_ident[0:64, 0:64]),
                       waits=[("vv", vts)] + BK.writer_waits([4]), inc="mm")
            mt2 = P.op("tensor", lambda t_: t_.transpose(PB(4, 64, 8), s_sq[0:8, 0:128],
                                                         s_ident[0:8, 0:8]),
                       waits=[("vv", vcs3)], inc="mm")
            vev = P.op("vector", lambda v, so_=so_: v.tensor_copy(A_sct()[:, so_:so_ + 72],
                                                                  PB(4, 0, 72)),
                       waits=[("mm", mt2)], inc="vv")
            BK.consumed([4], "vv", vev)
            vcl2 = P.op("vector", lambda v, so_=so_: v.tensor_scalar(
                A_ga()[:, 0:72], A_sct()[:, so_:so_ + 72], s_itcn[:, :], cc(C30),
                OP.mult, OP.min), waits=[("vv", vev)], inc="vv")
            sx = P.op("scalar", lambda s, so_=so_: s.activation(
                A_gn()[:, so_:so_ + 72], A_ga()[:, 0:72], AF.Exp),
                waits=[("vv", vcl2)], inc="ss")
            P.op("vector", lambda v, so_=so_: v.tensor_scalar(
                A_ga()[:, 0:72], A_gn()[:, so_:so_ + 72], cc(CONE), None, OP.add),
                waits=[("ss", sx)], inc="vv")
            P.op("vector", lambda v, so_=so_: v.reciprocal(A_gn()[:, so_:so_ + 72],
                                                           A_ga()[:, 0:72]), inc="vv")
            P.op("vector", lambda v, so_=so_: v.tensor_reduce(
                A_gm()[:, 0:1], A_gn()[:, so_ + 64:so_ + 72], AXL.X, OP.max), inc="vv")
            P.op("vector", lambda v, so_=so_: v.tensor_scalar(
                A_ohc()[:, :], A_gn()[:, so_ + 64:so_ + 72], A_gm()[:, 0:1], None,
                OP.is_equal), inc="vv")
            P.op("vector", lambda v: v.tensor_tensor(A_ohl()[:, :], A_ohc()[:, :],
                                                     s_rev8[:, :], OP.mult), inc="vv")
            P.op("vector", lambda v: v.tensor_reduce(A_gm()[:, 1:2], A_ohl()[:, :], AXL.X,
                                                     OP.max), inc="vv")
            P.op("vector", lambda v: v.tensor_scalar(A_ohc()[:, :], A_ohl()[:, :],
                                                     A_gm()[:, 1:2], None, OP.is_equal),
                 inc="vv")
            ohc_ap = A_ohc()
            gb_ap = A_gb()
            gt_ap = A_gt()
            P.op("vector", lambda v, so_=so_: v.tensor_tensor(
                A_gb()[:, :].rearrange("p (c j) -> p c j", c=8),
                A_gn()[:, so_:so_ + 64].rearrange("p (c j) -> p c j", c=8),
                bass.AP(ohc_ap.tensor, ohc_ap.offset, [ohc_ap.ap[0], [1, 8], [0, 8]]),
                OP.mult), inc="vv")
            P.op("vector", lambda v: v.tensor_reduce(
                A_selg()[:, :].rearrange("p (j o) -> p j o", o=1),
                bass.AP(gb_ap.tensor, gb_ap.offset, [gb_ap.ap[0], [1, 8], [8, 8]]),
                AXL.X, OP.add), inc="vv")
            P.op("vector", lambda v: v.tensor_reduce(A_gm()[:, 2:3], A_selg()[:, :], AXL.X,
                                                     OP.max), inc="vv")
            P.op("vector", lambda v: v.tensor_scalar(A_gt()[:, :], A_selg()[:, :],
                                                     A_gm()[:, 2:3], None, OP.is_equal),
                 inc="vv")
            P.op("vector", lambda v: v.tensor_tensor(A_ohl()[:, :], A_gt()[:, :],
                                                     s_rev8[:, :], OP.mult), inc="vv")
            P.op("vector", lambda v: v.tensor_reduce(A_gm()[:, 3:4], A_ohl()[:, :], AXL.X,
                                                     OP.max), inc="vv")
            P.op("vector", lambda v: v.tensor_scalar(A_gt()[:, :], A_ohl()[:, :],
                                                     A_gm()[:, 3:4], None, OP.is_equal),
                 inc="vv")
            voh = P.op("vector", lambda v: v.tensor_tensor(
                A_oht()[:, :].rearrange("p (c j) -> p c j", c=8),
                bass.AP(ohc_ap.tensor, ohc_ap.offset, [ohc_ap.ap[0], [1, 8], [0, 8]]),
                bass.AP(gt_ap.tensor, gt_ap.offset, [gt_ap.ap[0], [0, 8], [1, 8]]),
                OP.mult), inc="vv")
            mo = P.op("tensor", lambda t_: t_.transpose(PB(4, 128, 128, 64), A_oht()[:, :],
                                                        s_ident[:, :]),
                      waits=[("vv", voh)] + BK.writer_waits([4]), inc="mm")
            vof = P.op("vector", lambda v, ho=ho: v.tensor_copy(A_ohf()[0:64, ho:ho + 128],
                                                                PB(4, 128, 128, 64)),
                       waits=[("mm", mo)], inc="vv")
            BK.consumed([4], "vv", vof)
        vohf = P.count["vv"]
        tap("sct0", lambda: bass.AP(s_big, 2560, [[4096, 128], [1, 72]]), t)
        tap("sct1", lambda: bass.AP(s_big, 2640, [[4096, 128], [1, 72]]), t)
        tap("gn0", lambda: bass.AP(s_big, 2720, [[4096, 128], [1, 72]]), t)
        tap("gn1", lambda: bass.AP(s_big, 2800, [[4096, 128], [1, 72]]), t)
        tap("ohf", lambda: bass.AP(s_big, 3176, [[4096, 64], [1, 256]]), t)

        vM = []
        for i in range(2):
            bank = 6 + i
            mi = P.op("tensor", lambda t_, i=i, bank=bank: t_.matmul(
                PB(bank, 0, 256, 64), s_hsel[0:2, i * 128:i * 128 + 64],
                A_comp()[0:2, :], start=True, stop=True),
                waits=[("ss", scp)] + BK.writer_waits([bank]), inc="mm")
            v = P.op("vector", lambda v, i=i, bank=bank: v.tensor_tensor(
                (A_M0() if i == 0 else A_M1())[:, :], A_ohf(), PB(bank, 0, 256, 64),
                OP.mult), waits=[("mm", mi), ("vv", vohf)], inc="vv")
            BK.consumed([bank], "vv", v)
            vM.append(v)
        P.op("tensor", lambda t_: t_.matmul(PB(7, 0, 256, 16), s_w1a[:, :], A_M0(),
                                            start=True, stop=False),
             waits=[("vv", vM[1])] + BK.writer_waits([7]), inc=None)
        P.op("tensor", lambda t_: t_.matmul(PB(7, 0, 256, 16), s_w1b[:, :], A_M1(),
                                            start=False, stop=False), inc=None)
        mhh = P.op("tensor", lambda t_: t_.matmul(PB(7, 0, 256, 16), s_b1t[:, :], A_ohf(),
                                                  start=False, stop=True), inc="mm")
        shh = P.op("scalar", lambda s: s.activation(A_hh()[:, :], PB(7, 0, 256, 16),
                                                    AF.Relu), waits=[("mm", mhh)], inc="ss")
        BK.consumed([7], "ss", shh)
        mw2 = P.op("tensor", lambda t_: t_.matmul(PB(6, 0, 256, 16), s_w2t[:, :], A_ohf(),
                                                  start=True, stop=True),
                   waits=BK.writer_waits([6]), inc="mm")
        vt1 = P.op("vector", lambda v: v.tensor_tensor(A_t1()[:, :], A_hh(),
                                                       PB(6, 0, 256, 16), OP.mult),
                   waits=[("mm", mw2), ("ss", shh)], inc="vv")
        BK.consumed([6], "vv", vt1)
        P.op("tensor", lambda t_: t_.matmul(PB(5, 0, 256, 1), s_ones16[:, :], A_t1(),
                                            start=True, stop=False),
             waits=[("vv", vt1)] + BK.writer_waits([5]), inc=None)
        mmg = P.op("tensor", lambda t_: t_.matmul(PB(5, 0, 256, 1), s_b2t[:, :], A_ohf(),
                                                  start=False, stop=True), inc="mm")
        vmg = P.op("vector", lambda v: v.tensor_copy(A_magr()[:, :], PB(5, 0, 256, 1)),
                   waits=[("mm", mmg)], inc="vv")
        BK.consumed([5], "vv", vmg)
        mgb = P.op("tensor", lambda t_: t_.matmul(PB(5, 0, 256, 64), s_onesr[0:1, 0:64],
                                                  A_magr(), start=True, stop=True),
                   waits=[("vv", vmg)] + BK.writer_waits([5]), inc="mm")
        vso = P.op("vector", lambda v: v.tensor_tensor(A_soh()[:, :], A_ohf(),
                                                       PB(5, 0, 256, 64), OP.mult),
                   waits=[("mm", mgb)], inc="vv")
        BK.consumed([5], "vv", vso)
        tap("soh", lambda: bass.AP(s_big, 3432, [[4096, 64], [1, 256]]), t)
        vyp = None
        for cC in range(8):
            bank = 4 + (cC % 2)
            mD = P.op("tensor", lambda t_, cC=cC, bank=bank: t_.matmul(
                PB(bank, 0, 256), s_dirss[:, cC * 128:(cC + 1) * 128], A_soh(),
                start=True, stop=True),
                waits=[("vv", vso)] + BK.writer_waits([bank]), inc="mm")
            vyp = P.op("vector", lambda v, cC=cC, bank=bank, s_xa=s_xa: v.tensor_tensor(
                s_xa[:, cC * 256:(cC + 1) * 256], PB(bank, 0, 256),
                s_h[:, cC * 256:(cC + 1) * 256], OP.add), waits=[("mm", mD)], inc="vv")
            BK.consumed([bank], "vv", vyp)
        tap("ypre", lambda: s_xa[:, :], t)

        ln_stats(s_xa, wv=vyp)
        vmr = P.op("vector", lambda v: v.tensor_tensor(s_mrb[:, :], s_mb[:, :], s_rb[:, :],
                                                       OP.mult), inc="vv")
        vt2 = ln_apply(s_xa, s_h, A_tmp3())
        P.op("vector", lambda v: v.nop(), waits=[("vv", vt2)], inc="xf")
        for cC in range(8):
            P.op("vector", lambda v, cC=cC: v.tensor_scalar(
                s_vb[:, :], s_mrb[:, :], s_gamc[:, cC:cC + 1], None, OP.mult),
                waits=[("vv", vmr)], inc="vv")
            P.op("vector", lambda v, cC=cC: v.scalar_tensor_tensor(
                s_wfin[:, :], s_x2[:, cC * 256:(cC + 1) * 256], s_betc[:, cC:cC + 1],
                s_vb[:, :], OP.add, OP.subtract), inc="vv")
            P.op("vector", lambda v, cC=cC: v.scalar_tensor_tensor(
                s_x2[:, cC * 256:(cC + 1) * 256], s_h[:, cC * 256:(cC + 1) * 256],
                s_gamc[:, cC:cC + 1], s_wfin[:, :], OP.mult, OP.add), inc="vv")
        vfin = P.count["vv"]
        tap("out", lambda: s_x2[:, :], t)

        st_count = P.op("gpsimd", lambda g, t=t: g.dma_start(
            out=bass.AP(d_out, t * TT, [[RTOK, 128], [128 * RTOK, 8], [1, TT]]),
            in_=s_x2[:, :].rearrange("p (c f) -> p c f", c=8)),
            waits=[("vv", vfin)], inc="st", inc_n=16)

    P.op("gpsimd", lambda g: None, waits=[("st", P.count["st"])])

    sems = {}
    for s in P.count:
        sems[s] = es.enter_context(nc.semaphore(s))
    with nc.Block() as block:
        @block.tensor
        def _(eng):
            P.emit("tensor", eng, sems)

        @block.vector
        def _(eng):
            P.emit("vector", eng, sems)

        @block.scalar
        def _(eng):
            P.emit("scalar", eng, sems)

        @block.gpsimd
        def _(eng):
            P.emit("gpsimd", eng, sems)

        @block.sync
        def _(eng):
            P.emit("sync", eng, sems)

    nc._keepalive = es
    return nc


def make_inmaps(inputs):
    f32 = np.float32
    x = np.asarray(inputs["L0_attn_x"], f32)
    wz = np.asarray(inputs["L0_attn_Wz"], f32)
    wzs = np.ascontiguousarray(
        wz.reshape(8, 128, 16, 128).transpose(2, 0, 1, 3).reshape(16 * 8 * 128, 128))
    win = np.asarray(inputs["L0_attn_Win"], f32)
    win_r = np.ascontiguousarray(np.concatenate(
        [win[:, 136:648], win[:, 0:64], win[:, 64:128], win[:, 128:136]], axis=1))
    sw1 = np.asarray(inputs["L0_ffn_spline_w1"], f32)
    hm = np.zeros((128, 8), f32)
    hm[:, 0::2] = 1.0
    common = {
        "wzs": wzs,
        "win": win_r,
        "wzg": np.ascontiguousarray(np.asarray(inputs["L0_attn_Wzg"], f32)),
        "wout": np.ascontiguousarray(np.asarray(inputs["L0_attn_Wout"], f32)),
        "cw1": np.ascontiguousarray(np.asarray(inputs["L0_ffn_compress_w1"], f32)),
        "cw2": np.ascontiguousarray(np.asarray(inputs["L0_ffn_compress_w2"], f32)),
        "w1a": np.ascontiguousarray(sw1[:, 0, :]),
        "w1b": np.ascontiguousarray(sw1[:, 1, :]),
        "b1t": np.ascontiguousarray(np.asarray(inputs["L0_ffn_spline_b1"], f32)),
        "w2t": np.ascontiguousarray(np.asarray(inputs["L0_ffn_spline_w2"], f32)[:, :, 0]),
        "b2t": np.ascontiguousarray(np.asarray(inputs["L0_ffn_spline_b2"], f32)),
        "dirs": np.ascontiguousarray(np.asarray(inputs["L0_ffn_directions"], f32)),
        "sigs": np.ascontiguousarray(np.asarray(inputs["L0_ffn_signatures_raw"], f32)),
        "csel": np.ascontiguousarray(
            (np.arange(64)[:, None] // 8 == np.arange(8)[None, :]).astype(f32)),
        "dtb": np.asarray(inputs["L0_attn_dt_bias"], f32).reshape(8, 1),
        "alog": np.asarray(inputs["L0_attn_A_log"], f32).reshape(8, 1),
        "drep": np.ascontiguousarray(
            np.repeat(np.asarray(inputs["L0_attn_D_param"], f32), 64).reshape(4, 128).T),
        "gamc": np.ascontiguousarray(
            np.asarray(inputs["L0_ffn_norm_gamma"], f32).reshape(8, 128).T),
        "betc": np.ascontiguousarray(
            np.asarray(inputs["L0_ffn_norm_beta"], f32).reshape(8, 128).T),
        "cb1": np.ascontiguousarray(
            np.asarray(inputs["L0_ffn_compress_b1"], f32).reshape(2, 128).T),
        "cb2": np.asarray(inputs["L0_ffn_compress_b2"], f32).reshape(2, 1),
        "temp": np.asarray(inputs["L0_ffn_temperature"], f32).reshape(1, 1),
        "oscale": np.asarray(inputs["L0_ffn_output_scale"], f32).reshape(1, 1),
        "ident": np.eye(128, dtype=f32),
        "rev8": np.tile(8.0 - np.arange(8, dtype=f32), (128, 1)),
        "u01": (np.arange(128)[None, :] >= np.arange(128)[:, None]).astype(f32),
        "mbias": np.where(np.arange(128)[None, :] >= np.arange(128)[:, None],
                          0.0, MASKNEG).astype(f32),
        "onesr": np.ones((1, 256), f32),
        "ones8r": np.ones((8, 128), f32),
        "onespad": np.concatenate([np.ones((128, 1), f32), np.zeros((128, 127), f32)], 1),
        "ones16": np.ones((16, 1), f32),
        "cc": np.tile(np.array([[1.0 / 1024, EPS, -1.0, 0.0, 1.0, 1e-6, 0.3, -0.3,
                                 30.0, -0.5, 0.5, 0.0]], f32), (128, 1)),
        "hmE": hm,
        "hmO": (1.0 - hm).astype(f32),
        "hsel": np.ascontiguousarray(np.repeat(np.eye(8, dtype=f32), 128, axis=1)),
    }
    in_maps = []
    for i in range(NCORES):
        m = dict(common)
        m["x"] = np.ascontiguousarray(x[BPC * i:BPC * (i + 1)].reshape(RTOK, D).T)
        in_maps.append(m)
    return in_maps


_NC_CACHE = {}


def kernel(**inputs):
    if "full" not in _NC_CACHE:
        _NC_CACHE["full"] = build_program()
    nc = _NC_CACHE["full"]
    in_maps = make_inmaps(inputs)
    res = run_bass_kernel_spmd(nc, in_maps, core_ids=list(range(NCORES)))
    outs = []
    for i in range(NCORES):
        o = np.asarray(res.results[i]["out"])
        outs.append(o.T.reshape(BPC, SEQ, D))
    return np.concatenate(outs, axis=0)


# version 08
# speedup vs baseline: 1.2559x; 1.0319x over previous
import contextlib
import numpy as np
import concourse.bass as bass
import concourse.mybir as mybir
from concourse.bass_utils import run_bass_kernel_spmd

F32 = mybir.dt.float32
AF = mybir.ActivationFunctionType
OP = mybir.AluOpType
AXL = mybir.AxisListType

NCORES = 8
BATCH, SEQ, D = 32, 2048, 1024
BPC = BATCH // NCORES
RTOK = BPC * SEQ
TT = 256
NTILE = RTOK // TT
Q = 128
CPT = TT // Q
NH, HD = 8, 64
DI = NH * HD
DZ = 2 * D
EPS = 1e-5
MASKNEG = -30000.0
CHPS = SEQ // Q

C1024, CEPS, CNEG1, CZERO, CONE, C1E6, CTH, CNTH, C30, CNEGH = 0, 1, 2, 3, 4, 5, 6, 7, 8, 9
CHALF = 10


class Prog:
    ENGINES = ("tensor", "vector", "scalar", "gpsimd", "sync")

    def __init__(self):
        self.ops = {e: [] for e in self.ENGINES}
        self.count = {}
        self.hw = {e: {} for e in self.ENGINES}

    def sem(self, name):
        self.count.setdefault(name, 0)

    def op(self, eng, fn, waits=(), inc=None, inc_n=1):
        ws = []
        for s, c in waits:
            if c is None or c <= 0:
                continue
            assert c <= self.count[s], f"wait {s}>={c}, recorded {self.count[s]}"
            if self.hw[eng].get(s, -1) < c:
                ws.append((s, c))
                self.hw[eng][s] = c
        if inc is not None:
            self.count[inc] += inc_n
        self.ops[eng].append((fn, ws, inc, inc_n))
        return self.count[inc] if inc is not None else None

    def emit(self, eng_name, eng, sems):
        for fn, ws, inc, inc_n in self.ops[eng_name]:
            for s, c in ws:
                eng.wait_ge(sems[s], c)
            instr = fn(eng)
            if inc is not None:
                assert instr is not None, f"inc={inc} op returned None on {eng_name}"
                instr.then_inc(sems[inc], inc_n)


class Banks:
    def __init__(self):
        self.pending = {b: [] for b in range(8)}

    def writer_waits(self, banks):
        ws = []
        for b in banks:
            ws += self.pending[b]
            self.pending[b] = []
        return ws

    def consumed(self, banks, sem, count):
        for b in banks:
            self.pending[b].append((sem, count))


def rep(ap, nrep, inner):
    return bass.AP(ap.tensor, ap.offset, [ap.ap[0], [0, nrep], [ap.ap[-1][0], inner]])


def col_rep(ap, ncols, inner):
    return bass.AP(ap.tensor, ap.offset, [ap.ap[0], [ap.ap[-1][0], ncols], [0, inner]])


def build_program(n_tiles=NTILE, tap_names=()):
    nc = bass.Bass("TRN2", target_bir_lowering=False, detect_race_conditions=False)

    def dram_in(name, shape, dt=F32):
        return nc.dram_tensor(name, list(shape), dt, kind="ExternalInput")

    d_x = dram_in("x", [D, RTOK])
    d_wzs = dram_in("wzs", [16 * 8 * 128, 128])
    d_win = dram_in("win", [D, 648])
    d_wzg = dram_in("wzg", [DZ, DI])
    d_wout = dram_in("wout", [DI, D])
    d_cw1 = dram_in("cw1", [D, 256])
    d_cw2 = dram_in("cw2", [256, 2])
    d_w1a = dram_in("w1a", [64, 16])
    d_w1b = dram_in("w1b", [64, 16])
    d_b1t = dram_in("b1t", [64, 16])
    d_w2t = dram_in("w2t", [64, 16])
    d_b2t = dram_in("b2t", [64, 1])
    d_dirs = dram_in("dirs", [64, D])
    d_sigs = dram_in("sigs", [64, D])
    d_csel = dram_in("csel", [64, 8])
    d_dtb = dram_in("dtb", [8, 1])
    d_alog = dram_in("alog", [8, 1])
    d_drep = dram_in("drep", [128, 4])
    d_gamc = dram_in("gamc", [128, 8])
    d_betc = dram_in("betc", [128, 8])
    d_cb1 = dram_in("cb1", [128, 2])
    d_cb2 = dram_in("cb2", [2, 1])
    d_temp = dram_in("temp", [1, 1])
    d_oscale = dram_in("oscale", [1, 1])
    d_ident = dram_in("ident", [128, 128])
    d_rev8 = dram_in("rev8", [128, 8])
    d_u01 = dram_in("u01", [128, 128])
    d_mbias = dram_in("mbias", [128, 128])
    d_onesr = dram_in("onesr", [1, 256])
    d_ones8r = dram_in("ones8r", [8, 128])
    d_onespad = dram_in("onespad", [128, 128])
    d_ones16 = dram_in("ones16", [16, 1])
    d_cc = dram_in("cc", [128, 12])
    d_hmE = dram_in("hmE", [128, 8])
    d_hmO = dram_in("hmO", [128, 8])
    d_hsel = dram_in("hsel", [8, 1024])
    d_out = nc.dram_tensor("out", [D, RTOK], F32, kind="ExternalOutput")
    d_tap = {t: nc.dram_tensor(f"tap_{t}", [128, 2048], F32, kind="ExternalOutput")
             for t in tap_names}

    P = Prog()
    for s in ("ldw", "ldx", "lz", "mm", "vv", "ss", "gg", "st", "xf"):
        P.sem(s)
    BK = Banks()

    es = contextlib.ExitStack()

    def sbuf(name, shape, dt=F32):
        return es.enter_context(nc.sbuf_tensor(name, shape, dt))

    s_wzb = sbuf("s_wzb", [128, 3072])
    s_win = sbuf("s_win", [128, 8 * 648])
    s_wzg = sbuf("s_wzg", [128, 16 * DI])
    s_wout = sbuf("s_wout", [128, 4 * D])
    s_cw1 = sbuf("s_cw1", [128, 8 * 256])
    s_cw2 = sbuf("s_cw2", [128, 4])
    s_w1a = sbuf("s_w1a", [64, 16])
    s_w1b = sbuf("s_w1b", [64, 16])
    s_b1t = sbuf("s_b1t", [64, 16])
    s_w2t = sbuf("s_w2t", [64, 16])
    s_b2t = sbuf("s_b2t", [64, 1])
    s_dirss = sbuf("s_dirss", [64, D])
    s_csel = sbuf("s_csel", [64, 8])
    s_qT = sbuf("s_qT", [128, 8 * 64])
    s_csT = sbuf("s_csT", [128, 8 * 8])
    s_dtb = sbuf("s_dtb", [8, 1])
    s_alog = sbuf("s_alog", [8, 1])
    s_negA = sbuf("s_negA", [8, 1])
    s_eA = sbuf("s_eA", [8, 1])
    s_drep = sbuf("s_drep", [128, 4])
    s_gamc = sbuf("s_gamc", [128, 8])
    s_betc = sbuf("s_betc", [128, 8])
    s_cb1 = sbuf("s_cb1", [128, 2])
    s_cb2 = sbuf("s_cb2", [2, 1])
    s_temp = sbuf("s_temp", [1, 1])
    s_it1 = sbuf("s_it1", [1, 1])
    s_it2 = sbuf("s_it2", [1, 1])
    s_itcn = sbuf("s_itcn", [128, 1])
    s_osc = sbuf("s_osc", [64, 1])
    s_ident = sbuf("s_ident", [128, 128])
    s_rev8 = sbuf("s_rev8", [128, 8])
    s_u01 = sbuf("s_u01", [128, 128])
    s_mbias = sbuf("s_mbias", [128, 128])
    s_onesr = sbuf("s_onesr", [1, 256])
    s_ones8r = sbuf("s_ones8r", [8, 128])
    s_onespad = sbuf("s_onespad", [128, 128])
    s_ones16 = sbuf("s_ones16", [16, 1])
    s_cc = sbuf("s_cc", [128, 12])
    s_hmE = sbuf("s_hmE", [128, 8])
    s_hmO = sbuf("s_hmO", [128, 8])
    s_hsel = sbuf("s_hsel", [8, 1024])

    s_xaA = sbuf("s_xaA", [128, 2048])
    s_xaB = sbuf("s_xaB", [128, 2048])
    s_x2 = sbuf("s_x2", [128, 2048])
    s_h = sbuf("s_h", [128, 2048])
    s_big = sbuf("s_big", [128, 4096])
    s_XT = sbuf("s_XT", [128, 1024])
    s_Bf = sbuf("s_Bf", [64, 256])
    s_Cf = sbuf("s_Cf", [64, 256])
    s_dtf = sbuf("s_dtf", [8, 256])
    s_sp1 = sbuf("s_sp1", [8, 256])
    s_sp2 = sbuf("s_sp2", [8, 256])
    s_sp3 = sbuf("s_sp3", [8, 256])
    s_del = sbuf("s_del", [8, 256])
    s_acl = sbuf("s_acl", [8, 256])
    s_clc = sbuf("s_clc", [8, 256])
    s_clT = sbuf("s_clT", [128, 8])
    s_dT = sbuf("s_dT", [128, 8])
    s_dTE = sbuf("s_dTE", [128, 8])
    s_dTO = sbuf("s_dTO", [128, 8])
    s_wd = sbuf("s_wd", [128, 8])
    s_w8 = sbuf("s_w8", [8, 128])
    s_w8b = sbuf("s_w8b", [8, 128])
    s_w8T = sbuf("s_w8T", [128, 8])
    s_cer = sbuf("s_cer", [1, 8])
    s_cer2 = sbuf("s_cer2", [1, 8])
    s_Btm = sbuf("s_Btm", [128, 64])
    s_XdE = sbuf("s_XdE", [128, 512])
    s_XdO = sbuf("s_XdO", [128, 512])
    s_Wx = sbuf("s_Wx", [128, 512])
    s_Epre = sbuf("s_Epre", [128, 512])
    s_Ehat = sbuf("s_Ehat", [128, 512])
    s_G = sbuf("s_G", [128, 1024])
    s_SU = sbuf("s_SU", [128, 128])
    s_Cexp = sbuf("s_Cexp", [64, 1024])
    s_SP = sbuf("s_SP", [64, 512])
    s_SP2 = sbuf("s_SP2", [64, 512])
    s_HPE = sbuf("s_HPE", [64, 512])
    s_HPO = sbuf("s_HPO", [64, 512])
    s_yT = sbuf("s_yT", [128, 1024])
    s_sg = sbuf("s_sg", [128, 512])
    s_sq = sbuf("s_sq", [128, 512])
    s_srow = sbuf("s_srow", [1, 512])
    s_mb = sbuf("s_mb", [128, 256])
    s_rb = sbuf("s_rb", [128, 256])
    s_vb = sbuf("s_vb", [128, 256])
    s_mrb = sbuf("s_mrb", [128, 256])
    s_wfin = sbuf("s_wfin", [128, 256])

    A_sigs = lambda: s_big[0:64, 0:1024]
    A_q = lambda: s_big[0:64, 1024:2048]
    A_cs = lambda: s_big[0:8, 2048:3072]
    A_dirs = lambda: s_big[0:64, 3072:4096]
    A_u = lambda: s_big[:, 0:1024]
    A_zc = lambda: s_big[:, 1024:1536]
    A_hc = lambda: s_big[:, 1536:2048]
    A_M0 = lambda: s_big[0:64, 2048:2304]
    A_M1 = lambda: s_big[0:64, 2304:2560]
    A_magr = lambda: s_big[0:1, 2048:2304]
    A_sct = lambda: s_big[:, 2560:2720]
    A_gn = lambda: s_big[:, 2720:2880]
    A_ga = lambda: s_big[:, 2880:3008]
    A_gb = lambda: s_big[:, 3008:3072]
    A_gm = lambda: s_big[:, 3072:3080]
    A_selg = lambda: s_big[:, 3080:3088]
    A_ohc = lambda: s_big[:, 3088:3096]
    A_ohl = lambda: s_big[:, 3096:3104]
    A_gt = lambda: s_big[:, 3104:3112]
    A_oht = lambda: s_big[:, 3112:3176]
    A_ohf = lambda: s_big[0:64, 3176:3432]
    A_soh = lambda: s_big[0:64, 3432:3688]
    A_comp = lambda: s_big[0:2, 3688:3944]
    A_hh = lambda: s_big[0:16, 2560:2816]
    A_t1 = lambda: s_big[0:16, 2816:3072]
    A_tmp3 = lambda: s_big
    A_zga = lambda: s_big[:, 0:1024]
    A_zgb = lambda: s_big[0:128, 2048:3072]
    A_u2 = lambda: s_big[:, 1024:2048]

    ps = es.enter_context(nc.psum_tensor("ps", [128, 4096], mybir.dt.float32))

    def PB(bank, off=0, w=512, rows=128, r0=0):
        return ps[r0:r0 + rows, bank * 512 + off: bank * 512 + off + w]

    cc = lambda i, p=128: s_cc[0:p, i:i + 1]

    nw = [0]

    def wload(fn):
        nw[0] += 1
        P.op("sync", fn, inc="ldw", inc_n=16)

    for k in range(8):
        wload(lambda sy, k=k: sy.dma_start(out=s_win[:, k * 648:(k + 1) * 648],
                                           in_=d_win[k * 128:(k + 1) * 128, :]))
    for k in range(16):
        wload(lambda sy, k=k: sy.dma_start(out=s_wzg[:, k * DI:(k + 1) * DI],
                                           in_=d_wzg[k * 128:(k + 1) * 128, :]))
    for k in range(4):
        wload(lambda sy, k=k: sy.dma_start(out=s_wout[:, k * D:(k + 1) * D],
                                           in_=d_wout[k * 128:(k + 1) * 128, :]))
    for k in range(8):
        wload(lambda sy, k=k: sy.dma_start(out=s_cw1[:, k * 256:(k + 1) * 256],
                                           in_=d_cw1[k * 128:(k + 1) * 128, :]))
    for k in range(2):
        wload(lambda sy, k=k: sy.dma_start(out=s_cw2[:, k * 2:(k + 1) * 2],
                                           in_=d_cw2[k * 128:(k + 1) * 128, :]))
    wload(lambda sy: sy.dma_start(out=A_sigs(), in_=d_sigs[:, :]))
    wload(lambda sy: sy.dma_start(out=A_dirs(), in_=d_dirs[:, :]))
    for a, b in [(s_w1a, d_w1a), (s_w1b, d_w1b), (s_b1t, d_b1t), (s_w2t, d_w2t),
                 (s_b2t, d_b2t), (s_csel, d_csel), (s_dtb, d_dtb), (s_alog, d_alog),
                 (s_drep, d_drep), (s_gamc, d_gamc), (s_betc, d_betc), (s_cb1, d_cb1),
                 (s_cb2, d_cb2), (s_temp, d_temp), (s_ident, d_ident), (s_rev8, d_rev8),
                 (s_u01, d_u01), (s_mbias, d_mbias), (s_onesr, d_onesr),
                 (s_ones8r, d_ones8r), (s_onespad, d_onespad), (s_ones16, d_ones16),
                 (s_cc, d_cc), (s_hmE, d_hmE), (s_hmO, d_hmO), (s_hsel, d_hsel)]:
        wload(lambda sy, a=a, b=b: sy.dma_start(out=a[:], in_=b[:]))
    wload(lambda sy: sy.dma_start(out=s_osc[:, :],
                                  in_=bass.AP(d_oscale, 0, [[0, 64], [1, 1]])))
    LDW = nw[0] * 16
    WL = [("ldw", LDW)]

    P.op("vector", lambda v: v.tensor_scalar(s_G[0:64, 0:1024], A_sigs(), cc(CTH, 64),
                                             None, OP.is_gt), waits=WL, inc="vv")
    P.op("vector", lambda v: v.tensor_scalar(s_yT[0:64, 0:1024], A_sigs(), cc(CNTH, 64),
                                             None, OP.is_lt), inc="vv")
    vq = P.op("vector", lambda v: v.tensor_tensor(A_q(), s_G[0:64, 0:1024],
                                                  s_yT[0:64, 0:1024], OP.subtract), inc="vv")
    for half in range(2):
        m = P.op("tensor", lambda t, h=half: t.matmul(PB(0, 0, 512, 8), s_csel[:, :],
                                                      A_q()[:, h * 512:(h + 1) * 512],
                                                      start=True, stop=True),
                 waits=[("vv", vq)] + BK.writer_waits([0]), inc="mm")
        P.op("vector", lambda v: v.tensor_scalar(s_sq[0:8, 0:512], PB(0, 0, 512, 8),
                                                 cc(CZERO, 8), None, OP.is_gt),
             waits=[("mm", m)], inc="vv")
        P.op("vector", lambda v: v.tensor_scalar(s_yT[0:8, 0:512], PB(0, 0, 512, 8),
                                                 cc(CZERO, 8), None, OP.is_lt),
             inc="vv")
        e = P.op("vector", lambda v, h=half: v.tensor_tensor(
            A_cs()[:, h * 512:(h + 1) * 512], s_sq[0:8, 0:512], s_yT[0:8, 0:512],
            OP.subtract), inc="vv")
        BK.consumed([0], "vv", e)
    for c in range(8):
        m = P.op("tensor", lambda t, c=c: t.transpose(PB(1, 0, 64),
                                                      A_q()[:, c * 128:(c + 1) * 128],
                                                      s_ident[0:64, 0:64]),
                 waits=BK.writer_waits([1]), inc="mm")
        v = P.op("vector", lambda v, c=c: v.tensor_copy(s_qT[:, c * 64:(c + 1) * 64],
                                                        PB(1, 0, 64)),
                 waits=[("mm", m)], inc="vv")
        BK.consumed([1], "vv", v)
        m = P.op("tensor", lambda t, c=c: t.transpose(PB(1, 64, 8),
                                                      A_cs()[:, c * 128:(c + 1) * 128],
                                                      s_ident[0:8, 0:8]),
                 waits=BK.writer_waits([1]), inc="mm")
        v = P.op("vector", lambda v, c=c: v.tensor_copy(s_csT[:, c * 8:(c + 1) * 8],
                                                        PB(1, 64, 8)),
                 waits=[("mm", m)], inc="vv")
        BK.consumed([1], "vv", v)
    e = P.op("scalar", lambda s: s.activation(s_eA[:, :], s_alog[:, :], AF.Exp),
             waits=WL, inc="ss")
    P.op("vector", lambda v: v.tensor_scalar(s_negA[:, :], s_eA[:, :], cc(CNEG1, 8), None,
                                             OP.mult), waits=[("ss", e)], inc="vv")
    P.op("vector", lambda v: v.tensor_scalar(s_it1[:, :], s_temp[:, :], cc(C1E6, 1), None,
                                             OP.add), waits=WL, inc="vv")
    vi = P.op("vector", lambda v: v.reciprocal(s_it2[:, :], s_it1[:, :]), inc="vv")
    m = P.op("tensor", lambda t: t.matmul(PB(1, 128, 1), s_onesr[0:1, 0:128], s_it2[:, :],
                                          start=True, stop=True),
             waits=[("vv", vi)] + BK.writer_waits([1]), inc="mm")
    v = P.op("vector", lambda v: v.tensor_scalar(s_itcn[:, :], PB(1, 128, 1), cc(CNEG1),
                                                 None, OP.mult), waits=[("mm", m)], inc="vv")
    BK.consumed([1], "vv", v)
    P.op("vector", lambda v: v.tensor_scalar(s_dirss[:, :], A_dirs(), s_osc[:, :],
                                             None, OP.mult), waits=WL, inc="vv")

    def ln_stats(src, wv=None):
        w01 = BK.writer_waits([0, 1])
        last = None
        f32r = mybir.dt.float32r
        for c in range(8):
            sqw = [("mm", sq_slot[c % 2])] if sq_slot[c % 2] else []
            if c < 2 and wv is not None:
                sqw.append(("vv", wv))
            sc = P.op("scalar", lambda s, c=c: s.activation(
                s_sq[:, (c % 2) * 256:(c % 2) * 256 + 256], src[:, c * 256:(c + 1) * 256],
                AF.Square), waits=sqw, inc="ss")
            P.op("tensor", lambda t, c=c: t.matmul(PB(0, 0, 256), s_onespad[:, :],
                                                   src[:, c * 256:(c + 1) * 256],
                                                   start=(c == 0), stop=(c == 7)),
                 waits=(w01 + ([("vv", wv)] if wv is not None else [])) if c == 0 else [],
                 inc=None)
            last = P.op("tensor", lambda t, c=c: t.matmul(
                PB(1, 0, 256), s_onespad[:, :],
                s_sq[:, (c % 2) * 256:(c % 2) * 256 + 256],
                start=(c == 0), stop=(c == 7)), waits=[("ss", sc)], inc="mm")
            sq_slot[c % 2] = last
        vr = P.op("vector", lambda v: v.tensor_copy(s_srow[:, 0:256], PB(0, 0, 256, 1)),
                  waits=[("mm", last)], inc="vv")
        vr2 = P.op("vector", lambda v: v.tensor_copy(s_srow[:, 256:512], PB(1, 0, 256, 1)),
                   inc="vv")
        BK.consumed([0], "vv", vr)
        BK.consumed([1], "vv", vr2)
        mb = P.op("tensor", lambda t: t.matmul(PB(0, 0, 512), s_onesr[0:1, 0:128],
                                               s_srow[:, :], start=True, stop=True),
                  waits=[("vv", vr2)] + BK.writer_waits([0]), inc="mm")
        P.op("vector", lambda v: v.tensor_scalar(s_mb[:, :], PB(0, 0, 256), cc(C1024),
                                                 None, OP.mult), waits=[("mm", mb)], inc="vv")
        P.op("vector", lambda v: v.tensor_tensor(s_vb[:, :], s_mb[:, :], s_mb[:, :],
                                                 OP.mult), inc="vv")
        vv = P.op("vector", lambda v: v.scalar_tensor_tensor(s_rb[:, :], PB(0, 256, 256),
                                                             cc(C1024), s_vb[:, :],
                                                             OP.mult, OP.subtract), inc="vv")
        BK.consumed([0], "vv", vv)
        sv = P.op("scalar", lambda s: s.activation(s_vb[:, :], s_rb[:, :], AF.Ln,
                                                   bias=cc(CEPS)), waits=[("vv", vv)],
                  inc="ss")
        sv2 = P.op("scalar", lambda s: s.activation(s_rb[:, :], s_vb[:, :], AF.Exp,
                                                    scale=cc(CNEGH)), inc="ss")
        return P.op("vector", lambda v: v.nop(), waits=[("ss", sv2)], inc="vv")

    def ln_apply(src, dst, tmp, extra=()):
        P.op("vector", lambda v: v.tensor_tensor(
            tmp[:, 0:2048].rearrange("p (c f) -> p c f", c=8),
            src[:, :].rearrange("p (c f) -> p c f", c=8),
            rep(s_mb[:, :], 8, 256), OP.subtract),
            waits=list(extra), inc="vv")
        return P.op("vector", lambda v: v.tensor_tensor(
            dst[:, :].rearrange("p (c f) -> p c f", c=8),
            tmp[:, 0:2048].rearrange("p (c f) -> p c f", c=8),
            rep(s_rb[:, :], 8, 256), OP.mult), inc="vv")

    taps_set = set(tap_names)
    TAP_TILE = 0

    def tap(name, ap_fn, tile):
        if name not in taps_set or tile != TAP_TILE:
            return
        src = ap_fn()
        rr, cc_ = src.shape[0], int(np.prod(src.shape[1:]))
        P.op("gpsimd", lambda gp, name=name, src=src, rr=rr, cc_=cc_: gp.dma_start(
            out=d_tap[name][0:rr, 0:cc_], in_=src),
            waits=[("vv", P.count["vv"]), ("ss", P.count["ss"]), ("mm", P.count["mm"])],
            inc="gg", inc_n=16)
        P.op("vector", lambda v: None, waits=[("gg", P.count["gg"])])
        P.op("scalar", lambda s: None, waits=[("gg", P.count["gg"])])
        P.op("tensor", lambda t_: None, waits=[("gg", P.count["gg"])])

    sq_slot = [None, None]
    st_count = 0
    last_sce = [0]
    z_ring = {}
    for t in range(n_tiles):
        s_xa = s_xaA if t % 2 == 0 else s_xaB
        P.op("sync", lambda sy, t=t, s_xa=s_xa: sy.dma_start(
            out=s_xa[:, :].rearrange("p (c f) -> p c f", c=8),
            in_=bass.AP(d_x, t * TT, [[RTOK, 128], [128 * RTOK, 8], [1, TT]])),
            waits=([("xf", t - 1)] if t >= 2 else []), inc="ldx", inc_n=16)
        XL = [("ldx", 16 * (t + 1))]
        P.op("vector", lambda v: None, waits=XL)
        P.op("scalar", lambda s: None, waits=XL)
        P.op("tensor", lambda t_: None, waits=XL)

        ln_stats(s_xa)
        vh1 = ln_apply(s_xa, s_h, s_x2, extra=[("st", st_count)])
        tap("h1", lambda: s_h[:, :], t)

        xe = []
        for fc in range(4):
            bank = 2 + (fc % 2)
            for k in range(8):
                P.op("tensor", lambda t_, fc=fc, k=k, bank=bank: t_.matmul(
                    PB(bank, 0, 256), s_win[:, k * 648 + fc * 128: k * 648 + fc * 128 + 128],
                    s_h[:, k * 256:(k + 1) * 256], start=(k == 0), stop=(k == 7)),
                    waits=([("vv", vh1)] + BK.writer_waits([bank])) if k == 0 else [],
                    inc="mm" if k == 7 else None)
            mmx = P.count["mm"]
            v = P.op("vector", lambda v, fc=fc, bank=bank: v.tensor_copy(
                s_XT[:, fc * 256:(fc + 1) * 256], PB(bank, 0, 256)),
                waits=[("mm", mmx)], inc="vv")
            BK.consumed([bank], "vv", v)
            xe.append(v)
        for c0, w, dst, r in [(640, 8, s_dtf, 8), (512, 64, s_Bf, 64), (576, 64, s_Cf, 64)]:
            for k in range(8):
                P.op("tensor", lambda t_, k=k, c0=c0, w=w, r=r: t_.matmul(
                    PB(4, 0, 256, r), s_win[:, k * 648 + c0: k * 648 + c0 + w],
                    s_h[:, k * 256:(k + 1) * 256], start=(k == 0), stop=(k == 7)),
                    waits=BK.writer_waits([4]) if k == 0 else [], inc="mm" if k == 7 else None)
            mmb = P.count["mm"]
            v = P.op("vector", lambda v, dst=dst, r=r: v.tensor_copy(dst[:, :],
                                                                     PB(4, 0, 256, r)),
                     waits=[("mm", mmb)], inc="vv")
            BK.consumed([4], "vv", v)
            if dst is s_dtf:
                vdtf = v
            elif dst is s_Bf:
                vBf = v
            else:
                vCf = v
        vbc = P.count["vv"]

        P.op("scalar", lambda s: s.activation(s_sp1[:, :], s_dtf[:, :], AF.Relu,
                                              bias=s_dtb[:, :]), waits=[("vv", vdtf)],
             inc="ss")
        P.op("scalar", lambda s: s.activation(s_sp2[:, :], s_dtf[:, :], AF.Abs,
                                              bias=s_dtb[:, :]), inc="ss")
        s3 = P.op("scalar", lambda s: s.activation(s_sp3[:, :], s_sp2[:, :], AF.Exp,
                                                   scale=cc(CNEG1, 8)), inc="ss")
        v1 = P.op("vector", lambda v: v.tensor_scalar(s_sp2[:, :], s_sp3[:, :], cc(CONE, 8),
                                                      None, OP.add), waits=[("ss", s3)],
                  inc="vv")
        s4 = P.op("scalar", lambda s: s.activation(s_sp3[:, :], s_sp2[:, :], AF.Ln),
                  waits=[("vv", v1)], inc="ss")
        vdl = P.op("vector", lambda v: v.tensor_tensor(s_del[:, :], s_sp1[:, :], s_sp3[:, :],
                                                       OP.add), waits=[("ss", s4)], inc="vv")
        va = P.op("vector", lambda v: v.tensor_scalar(s_acl[:, :], s_del[:, :], s_negA[:, :],
                                                      None, OP.mult), inc="vv")

        def wz_dma(zf, gz):
            P.op("sync", lambda sy, zf=zf, gz=gz: sy.dma_start(
                out=bass.AP(s_wzb, (gz % 3) * 1024, [[3072, 128], [128, 8], [1, 128]]),
                in_=bass.AP(d_wzs, zf * 1024 * 128, [[128, 128], [128 * 128, 8], [1, 128]])),
                waits=[("mm", z_ring.get(gz - 3, 0))], inc="lz", inc_n=16)

        def scan_chain(c, out):
            g = t * CPT + c
            first = (g % CHPS == 0)
            co = c * 128
            vcl = P.op("vector", lambda v, co=co: v.tensor_tensor_scan(
                s_clc[:, co:co + 128], s_ones8r[:, :], s_acl[:, co:co + 128], 0.0,
                OP.mult, OP.add), waits=[("vv", va)], inc="vv")
            m = P.op("tensor", lambda t_, co=co: t_.transpose(PB(1, 0, 8),
                                                              s_clc[:, co:co + 128],
                                                              s_ident[0:8, 0:8]),
                     waits=[("vv", vcl)] + BK.writer_waits([1]), inc="mm")
            vclT = P.op("vector", lambda v: v.tensor_copy(s_clT[:, :], PB(1, 0, 8)),
                        waits=[("mm", m)], inc="vv")
            BK.consumed([1], "vv", vclT)
            yield
            m = P.op("tensor", lambda t_, co=co: t_.transpose(PB(1, 8, 8),
                                                              s_del[:, co:co + 128],
                                                              s_ident[0:8, 0:8]),
                     waits=BK.writer_waits([1]) + [("vv", vdl)], inc="mm")
            vdT = P.op("vector", lambda v: v.tensor_copy(s_dT[:, :], PB(1, 8, 8)),
                       waits=[("mm", m)], inc="vv")
            BK.consumed([1], "vv", vdT)
            yield
            vw = P.op("vector", lambda v, co=co: v.tensor_scalar(
                s_w8[:, :], s_clc[:, co:co + 128], s_clc[:, co + 127:co + 128], None,
                OP.subtract), waits=[("vv", vcl)], inc="vv")
            sw = P.op("scalar", lambda s: s.activation(s_w8b[:, :], s_w8[:, :], AF.Exp,
                                                       scale=cc(CNEG1, 8)),
                      waits=[("vv", vw)], inc="ss")
            m = P.op("tensor", lambda t_: t_.transpose(PB(1, 16, 8), s_w8b[:, :],
                                                       s_ident[0:8, 0:8]),
                     waits=[("ss", sw)] + BK.writer_waits([1]), inc="mm")
            vw8 = P.op("vector", lambda v: v.tensor_copy(s_w8T[:, :], PB(1, 16, 8)),
                       waits=[("mm", m)], inc="vv")
            BK.consumed([1], "vv", vw8)
            yield
            m = P.op("tensor", lambda t_, co=co: t_.transpose(PB(1, 64, 64),
                                                              s_Bf[:, co:co + 128],
                                                              s_ident[0:64, 0:64]),
                     waits=BK.writer_waits([1]) + [("vv", vBf)], inc="mm")
            vB = P.op("vector", lambda v: v.tensor_copy(s_Btm[:, :], PB(1, 64, 64)),
                      waits=[("mm", m)], inc="vv")
            BK.consumed([1], "vv", vB)
            yield
            mS = P.op("tensor", lambda t_, co=co: t_.matmul(
                PB(1, 128, 128), s_Bf[:, co:co + 128], s_Cf[:, co:co + 128],
                start=True, stop=True),
                waits=BK.writer_waits([1]) + [("vv", vCf)], inc="mm")
            vSU = P.op("vector", lambda v: v.tensor_tensor(s_SU[:, :], PB(1, 128, 128),
                                                           s_u01[:, :], OP.mult),
                       waits=[("mm", mS)], inc="vv")
            BK.consumed([1], "vv", vSU)
            yield
            vde = P.op("vector", lambda v: v.tensor_tensor(s_dTE[:, :], s_dT[:, :],
                                                           s_hmE[:, :], OP.mult),
                       waits=[("vv", vdT)], inc="vv")
            vdo = P.op("vector", lambda v: v.tensor_tensor(s_dTO[:, :], s_dT[:, :],
                                                           s_hmO[:, :], OP.mult), inc="vv")
            vwd = P.op("vector", lambda v: v.tensor_tensor(s_wd[:, :], s_w8T[:, :],
                                                           s_dT[:, :], OP.mult),
                       waits=[("vv", vw8)], inc="vv")
            for fc in range(4):
                P.op("tensor", lambda t_, fc=fc, co=co: t_.transpose(
                    PB(0, fc * 128, 128), s_XT[:, fc * 256 + co: fc * 256 + co + 128],
                    s_ident[:, :]),
                    waits=(BK.writer_waits([0]) + [("vv", xe[3])]) if fc == 0 else [],
                    inc="mm" if fc == 3 else None)
            mX = P.count["mm"]
            vXE = P.op("vector", lambda v: v.tensor_tensor(
                s_XdE[:, :].rearrange("p (h d) -> p h d", h=8), PB(0, 0, 512),
                col_rep(s_dTE[:, :], 8, 64), OP.mult),
                waits=[("mm", mX), ("vv", vde)], inc="vv")
            vXO = P.op("vector", lambda v: v.tensor_tensor(
                s_XdO[:, :].rearrange("p (h d) -> p h d", h=8), PB(0, 0, 512),
                col_rep(s_dTO[:, :], 8, 64), OP.mult), waits=[("vv", vdo)], inc="vv")
            vWx = P.op("vector", lambda v: v.tensor_tensor(
                s_Wx[:, :].rearrange("p (h d) -> p h d", h=8), PB(0, 0, 512),
                col_rep(s_wd[:, :], 8, 64), OP.mult), waits=[("vv", vwd)], inc="vv")
            BK.consumed([0], "vv", vWx)
            yield
            mb8 = None
            for h in range(8):
                mb8 = P.op("tensor", lambda t_, h=h, co=co: t_.matmul(
                    PB(2 + h // 4, (h % 4) * 128, 128), s_hsel[:, h * 128:(h + 1) * 128],
                    s_clc[:, co:co + 128], start=True, stop=True),
                    waits=(BK.writer_waits([2, 3]) + [("vv", vcl)]) if h == 0 else [],
                    inc="mm" if h == 7 else None)
            se1 = P.op("scalar", lambda s: s.activation(s_G[0:64, 0:512], PB(2, 0, 512, 64),
                                                        AF.Exp), waits=[("mm", mb8)], inc="ss")
            se2 = P.op("scalar", lambda s: s.activation(s_G[0:64, 512:1024],
                                                        PB(3, 0, 512, 64), AF.Exp), inc="ss")
            yield
            vce = P.op("vector", lambda v, co=co: v.tensor_tensor(
                s_Cexp[:, :].rearrange("p (h d) -> p h d", h=8),
                rep(s_Cf[:, co:co + 128], 8, 128),
                s_G[0:64, :].rearrange("p (h d) -> p h d", h=8), OP.mult),
                waits=[("ss", se2)], inc="vv")
            vGs = []
            yield
            for gi in range(2):
                for hh in range(4):
                    h = gi * 4 + hh
                    P.op("vector", lambda v, h=h, hh=hh, gi=gi: v.scalar_tensor_tensor(
                        s_Epre[:, hh * 128:(hh + 1) * 128], PB(2 + gi, hh * 128, 128),
                        s_clT[:, h:h + 1], s_mbias[:, :], OP.subtract, OP.add),
                        waits=[("vv", vclT), ("mm", mb8)] if (gi == 0 and hh == 0) else [],
                        inc="vv")
                sE = P.op("scalar", lambda s: s.activation(s_Ehat[:, :], s_Epre[:, :],
                                                           AF.Exp),
                          waits=[("vv", P.count["vv"])], inc="ss")
                vG = P.op("vector", lambda v, gi=gi: v.tensor_tensor(
                    s_G[:, gi * 512:(gi + 1) * 512].rearrange("p (h d) -> p h d", h=4),
                    s_Ehat[:, :].rearrange("p (h d) -> p h d", h=4),
                    rep(s_SU[:, :], 4, 128), OP.mult),
                    waits=[("ss", sE), ("vv", vSU)], inc="vv")
                vGs.append(vG)
                yield
            ve2 = P.count["vv"]
            BK.consumed([2], "vv", ve2)
            BK.consumed([3], "vv", ve2)
            BK.consumed([2], "ss", se1)
            BK.consumed([3], "ss", se2)
            vpo = None
            if not first:
                P.op("vector", lambda v: v.tensor_tensor(
                    s_HPE[:, :].rearrange("p (h d) -> p h d", h=8),
                    s_SP[:, :].rearrange("p (h d) -> p h d", h=8),
                    col_rep(s_hmE[0:64, :], 8, 64), OP.mult), inc="vv")
                vpo = P.op("vector", lambda v: v.tensor_tensor(
                    s_HPO[:, :].rearrange("p (h d) -> p h d", h=8),
                    s_SP[:, :].rearrange("p (h d) -> p h d", h=8),
                    col_rep(s_hmO[0:64, :], 8, 64), OP.mult), inc="vv")
                P.op("gpsimd", lambda gp: gp.dma_start(out=s_cer[:, :],
                                                       in_=s_clT[127:128, :]),
                     waits=[("vv", vclT), ("ss", last_sce[0])], inc="gg", inc_n=16)
                sce = P.op("scalar", lambda s: s.activation(s_cer2[:, :], s_cer[:, :],
                                                            AF.Exp),
                           waits=[("gg", P.count["gg"])], inc="ss")
                last_sce[0] = sce
                mEH = P.op("tensor", lambda t_: t_.matmul(PB(1, 256, 8, 64),
                                                          s_onesr[0:1, 0:64], s_cer2[:, :],
                                                          start=True, stop=True),
                           waits=[("ss", sce)] + BK.writer_waits([1]), inc="mm")
                vsp = P.op("vector", lambda v: v.tensor_tensor(
                    s_SP2[:, :].rearrange("p (h d) -> p h d", h=8),
                    s_SP[:, :].rearrange("p (h d) -> p h d", h=8),
                    col_rep(PB(1, 256, 8, 64), 8, 64), OP.mult),
                    waits=[("mm", mEH), ("vv", vpo)], inc="vv")
                BK.consumed([1], "vv", vsp)
            out.update(first=first, co=co, vXO=vXO, vWx=vWx, vB=vB, vce=vce,
                       vGs=vGs, vpo=vpo)
            yield

        def z_half(h0, gen):
            def record_zg(zf):
                zgp0 = zgp_mm.get(zf, None)
                for j in range(4):
                    bank, off = 4 + j // 2, (j % 2) * 256
                    P.op("tensor", lambda t_, zf=zf, j=j, bank=bank, off=off: t_.matmul(
                        PB(bank, off, 256),
                        s_wzg[:, zf * DI + j * 128: zf * DI + j * 128 + 128],
                        A_zc()[:, (zf % 2) * 256:(zf % 2) * 256 + 256],
                        start=True, stop=True),
                        waits=([("vv", zc_vv[zf])] if j == 0 else []) +
                              (BK.writer_waits([4, 5]) if j == 0 else []),
                        inc="mm" if j == 3 else None)
                zgp = P.count["mm"]
                zgp_mm[zf] = zgp
                src, dst = (A_zga, A_zgb) if zf % 2 == 1 else (A_zgb, A_zga)
                if zf == 0:
                    vz = P.op("vector", lambda v: v.tensor_copy(
                        A_zga(), ps[:, 4 * 512: 4 * 512 + 1024]),
                        waits=[("mm", zgp)], inc="vv")
                else:
                    vz = P.op("vector", lambda v, src=src, dst=dst: v.tensor_tensor(
                        dst(), src(), ps[:, 4 * 512: 4 * 512 + 1024], OP.add),
                        waits=[("mm", zgp)], inc="vv")
                BK.consumed([4, 5], "vv", vz)

            for zf in range(h0, h0 + 8):
                gz = t * 16 + zf
                zb = 6 + zf % 2
                for k in range(8):
                    P.op("tensor", lambda t_, k=k, zb=zb, gz=gz: t_.matmul(
                        PB(zb, 0, 256),
                        s_wzb[:, (gz % 3) * 1024 + k * 128: (gz % 3) * 1024 + k * 128 + 128],
                        s_h[:, k * 256:(k + 1) * 256], start=(k == 0), stop=(k == 7)),
                        waits=(BK.writer_waits([zb]) + [("lz", 16 * (gz + 1))]) if k == 0
                        else [], inc="mm" if k == 7 else None)
                mz = P.count["mm"]
                z_ring[gz] = mz
                if zf + 3 < 16:
                    wz_dma(zf + 3, t * 16 + zf + 3)
                elif t + 1 < n_tiles:
                    nzf = zf + 3 - 16
                    wz_dma(nzf, (t + 1) * 16 + nzf)
                sth = P.op("scalar", lambda sc, zf=zf, zb=zb: sc.activation(
                    s_big[:, 1536 + (zf % 2) * 256: 1536 + (zf % 2) * 256 + 256], PB(zb, 0, 256), AF.Tanh,
                    scale=cc(CHALF)),
                    waits=[("mm", mz)] + ([("vv", zc_vv.get(zf - 2, None) or 0)]
                                          if zf >= 2 else []), inc="ss")
                P.op("vector", lambda v, zf=zf: v.tensor_scalar(
                    s_sg[:, (zf % 2) * 256:(zf % 2) * 256 + 256],
                    s_big[:, 1536 + (zf % 2) * 256: 1536 + (zf % 2) * 256 + 256],
                    cc(CHALF), cc(CHALF), OP.mult, OP.add),
                    waits=[("ss", sth)], inc="vv")
                vzc = P.op("vector", lambda v, zf=zf, zb=zb: v.tensor_tensor(
                    A_zc()[:, (zf % 2) * 256:(zf % 2) * 256 + 256],
                    s_sg[:, (zf % 2) * 256:(zf % 2) * 256 + 256],
                    PB(zb, 0, 256), OP.mult), inc="vv")
                zc_vv[zf] = vzc
                BK.consumed([zb], "vv", vzc)
                if zf > h0:
                    record_zg(zf - 1)
                next(gen, None)
                next(gen, None)
            record_zg(h0 + 7)
            for _ in gen:
                pass

        def y_state(ch):
            first, co = ch["first"], ch["co"]
            for fc in range(4):
                hE, hO = 2 * fc, 2 * fc + 1
                w0 = (BK.writer_waits([4]) + [("vv", ch["vXO"]), ("vv", ch["vGs"][1]),
                                              ("vv", ch["vce"])]
                      + ([("vv", ch["vpo"])] if not first else [])) if fc == 0 else []
                P.op("tensor", lambda t_, fc=fc, hE=hE: t_.matmul(
                    PB(4, fc * 128, 128), s_XdE[:, fc * 128:(fc + 1) * 128],
                    s_G[:, hE * 128:(hE + 1) * 128], start=True, stop=False),
                    waits=w0, inc=None)
                P.op("tensor", lambda t_, fc=fc, hO=hO, first=first: t_.matmul(
                    PB(4, fc * 128, 128), s_XdO[:, fc * 128:(fc + 1) * 128],
                    s_G[:, hO * 128:(hO + 1) * 128], start=False, stop=first),
                    inc="mm" if (first and fc == 3) else None)
                if not first:
                    P.op("tensor", lambda t_, fc=fc, hE=hE: t_.matmul(
                        PB(4, fc * 128, 128), s_HPE[:, fc * 128:(fc + 1) * 128],
                        s_Cexp[:, hE * 128:(hE + 1) * 128], start=False, stop=False),
                        inc=None)
                    P.op("tensor", lambda t_, fc=fc, hO=hO: t_.matmul(
                        PB(4, fc * 128, 128), s_HPO[:, fc * 128:(fc + 1) * 128],
                        s_Cexp[:, hO * 128:(hO + 1) * 128], start=False, stop=True),
                        inc="mm" if fc == 3 else None)
            ym = P.count["mm"]
            for fc in range(4):
                P.op("tensor", lambda t_, fc=fc: t_.matmul(
                    PB(5, fc * 128, 128, 64), s_Btm[:, :], s_Wx[:, fc * 128:(fc + 1) * 128],
                    start=True, stop=True),
                    waits=(BK.writer_waits([5]) + [("vv", ch["vWx"]), ("vv", ch["vB"])])
                    if fc == 0 else [], inc="mm" if fc == 3 else None)
            mst = P.count["mm"]
            if not first:
                vH = P.op("vector", lambda v: v.tensor_tensor(s_SP[:, :], s_SP2[:, :],
                                                              PB(5, 0, 512, 64), OP.add),
                          waits=[("mm", mst)], inc="vv")
            else:
                vH = P.op("vector", lambda v: v.tensor_copy(s_SP[:, :], PB(5, 0, 512, 64)),
                          waits=[("mm", mst)], inc="vv")
            BK.consumed([5], "vv", vH)
            for fc in range(4):
                v = P.op("vector", lambda v, fc=fc, co=co: v.scalar_tensor_tensor(
                    s_yT[:, fc * 256 + co: fc * 256 + co + 128],
                    s_XT[:, fc * 256 + co: fc * 256 + co + 128],
                    s_drep[:, fc:fc + 1], PB(4, fc * 128, 128), OP.mult, OP.add),
                    waits=[("mm", ym)], inc="vv")
                BK.consumed([4], "vv", v)

        if t == 0:
            for i in range(3):
                wz_dma(i, i)
        zgp_mm = {}
        zc_vv = {}
        ch0 = {}
        z_half(0, scan_chain(0, ch0))
        y_state(ch0)
        ch1 = {}
        z_half(8, scan_chain(1, ch1))
        y_state(ch1)
        tap("yT", lambda: bass.AP(s_yT, 0, [[1024, 128], [1, 1024]]), t)

        vu = P.op("vector", lambda v: v.tensor_tensor(A_u2(), A_zgb(), s_yT[:, :],
                                                      OP.mult), inc="vv")
        vx2 = None
        for cC in range(8):
            bank = 4 + (cC % 2)
            for k in range(4):
                P.op("tensor", lambda t_, cC=cC, k=k, bank=bank: t_.matmul(
                    PB(bank, 0, 256), s_wout[:, k * D + cC * 128: k * D + cC * 128 + 128],
                    A_u2()[:, k * 256:(k + 1) * 256], start=(k == 0), stop=(k == 3)),
                    waits=(BK.writer_waits([bank]) + ([("vv", vu)] if cC == 0 else []))
                    if k == 0 else [], inc="mm" if k == 3 else None)
            ma = P.count["mm"]
            vx2 = P.op("vector", lambda v, cC=cC, bank=bank, s_xa=s_xa: v.tensor_tensor(
                s_x2[:, cC * 256:(cC + 1) * 256], PB(bank, 0, 256),
                s_xa[:, cC * 256:(cC + 1) * 256], OP.add), waits=[("mm", ma)], inc="vv")
            BK.consumed([bank], "vv", vx2)
        tap("u", lambda: bass.AP(s_big, 0, [[4096, 128], [1, 1024]]), t)
        tap("x2", lambda: s_x2[:, :], t)

        ln_stats(s_x2, wv=vx2)
        vh2 = ln_apply(s_x2, s_h, s_xa)
        tap("h2", lambda: s_h[:, :], t)

        hcs = []
        for f in range(2):
            bank = 6 + f
            for k in range(8):
                P.op("tensor", lambda t_, f=f, k=k, bank=bank: t_.matmul(
                    PB(bank, 0, 256), s_cw1[:, k * 256 + f * 128: k * 256 + f * 128 + 128],
                    s_h[:, k * 256:(k + 1) * 256], start=(k == 0), stop=(k == 7)),
                    waits=BK.writer_waits([bank]) if k == 0 else [],
                    inc="mm" if k == 7 else None)
            mc = P.count["mm"]
            sg = P.op("scalar", lambda s, f=f, bank=bank: s.activation(
                A_hc()[:, f * 256:(f + 1) * 256], PB(bank, 0, 256), AF.Gelu,
                bias=s_cb1[:, f:f + 1]), waits=[("mm", mc)], inc="ss")
            hcs.append(sg)
            BK.consumed([bank], "ss", sg)
        for k in range(2):
            P.op("tensor", lambda t_, k=k: t_.matmul(
                PB(6, 0, 256, 2), s_cw2[:, k * 2:(k + 1) * 2],
                A_hc()[:, k * 256:(k + 1) * 256], start=(k == 0), stop=(k == 1)),
                waits=[("ss", hcs[k])] + (BK.writer_waits([6]) if k == 0 else []),
                inc="mm" if k == 1 else None)
        mcp = P.count["mm"]
        scp = P.op("scalar", lambda s: s.activation(A_comp()[:, :], PB(6, 0, 256, 2),
                                                    AF.Tanh, bias=s_cb2[:, :]),
                   waits=[("mm", mcp)], inc="ss")
        BK.consumed([6], "ss", scp)
        for k in range(8):
            P.op("tensor", lambda t_, k=k: t_.matmul(
                PB(2, 0, 256, 64), s_qT[:, k * 64:(k + 1) * 64],
                s_h[:, k * 256:(k + 1) * 256], start=(k == 0), stop=(k == 7)),
                waits=(BK.writer_waits([2]) + [("vv", vh2)]) if k == 0 else [], inc=None)
            P.op("tensor", lambda t_, k=k: t_.matmul(
                PB(3, 0, 256, 8), s_csT[:, k * 8:(k + 1) * 8],
                s_h[:, k * 256:(k + 1) * 256], start=(k == 0), stop=(k == 7)),
                waits=BK.writer_waits([3]) if k == 0 else [],
                inc="mm" if k == 7 else None)
        msc = P.count["mm"]
        for hf in range(2):
            ho = hf * 128
            so_ = hf * 80
            vts = P.op("vector", lambda v, ho=ho: v.tensor_copy(A_ga()[0:64, 0:128],
                                                                PB(2, ho, 128, 64)),
                       waits=[("mm", msc)], inc="vv")
            vcs3 = P.op("vector", lambda v, ho=ho: v.tensor_copy(s_sq[0:8, 0:128],
                                                                 PB(3, ho, 128, 8)),
                        inc="vv")
            if hf == 1:
                BK.consumed([2], "vv", vcs3)
                BK.consumed([3], "vv", vcs3)
            mt1 = P.op("tensor", lambda t_: t_.transpose(PB(4, 0, 64), A_ga()[0:64, 0:128],
                                                         s_ident[0:64, 0:64]),
                       waits=[("vv", vts)] + BK.writer_waits([4]), inc="mm")
            mt2 = P.op("tensor", lambda t_: t_.transpose(PB(4, 64, 8), s_sq[0:8, 0:128],
                                                         s_ident[0:8, 0:8]),
                       waits=[("vv", vcs3)], inc="mm")
            vev = P.op("vector", lambda v, so_=so_: v.tensor_copy(A_sct()[:, so_:so_ + 72],
                                                                  PB(4, 0, 72)),
                       waits=[("mm", mt2)], inc="vv")
            BK.consumed([4], "vv", vev)
            vcl2 = P.op("vector", lambda v, so_=so_: v.tensor_scalar(
                A_ga()[:, 0:72], A_sct()[:, so_:so_ + 72], s_itcn[:, :], cc(C30),
                OP.mult, OP.min), waits=[("vv", vev)], inc="vv")
            sx = P.op("scalar", lambda s, so_=so_: s.activation(
                A_gn()[:, so_:so_ + 72], A_ga()[:, 0:72], AF.Exp),
                waits=[("vv", vcl2)], inc="ss")
            P.op("vector", lambda v, so_=so_: v.tensor_scalar(
                A_ga()[:, 0:72], A_gn()[:, so_:so_ + 72], cc(CONE), None, OP.add),
                waits=[("ss", sx)], inc="vv")
            P.op("vector", lambda v, so_=so_: v.reciprocal(A_gn()[:, so_:so_ + 72],
                                                           A_ga()[:, 0:72]), inc="vv")
            P.op("vector", lambda v, so_=so_: v.tensor_reduce(
                A_gm()[:, 0:1], A_gn()[:, so_ + 64:so_ + 72], AXL.X, OP.max), inc="vv")
            P.op("vector", lambda v, so_=so_: v.tensor_scalar(
                A_ohc()[:, :], A_gn()[:, so_ + 64:so_ + 72], A_gm()[:, 0:1], None,
                OP.is_equal), inc="vv")
            P.op("vector", lambda v: v.tensor_tensor(A_ohl()[:, :], A_ohc()[:, :],
                                                     s_rev8[:, :], OP.mult), inc="vv")
            P.op("vector", lambda v: v.tensor_reduce(A_gm()[:, 1:2], A_ohl()[:, :], AXL.X,
                                                     OP.max), inc="vv")
            P.op("vector", lambda v: v.tensor_scalar(A_ohc()[:, :], A_ohl()[:, :],
                                                     A_gm()[:, 1:2], None, OP.is_equal),
                 inc="vv")
            ohc_ap = A_ohc()
            gb_ap = A_gb()
            gt_ap = A_gt()
            P.op("vector", lambda v, so_=so_: v.tensor_tensor(
                A_gb()[:, :].rearrange("p (c j) -> p c j", c=8),
                A_gn()[:, so_:so_ + 64].rearrange("p (c j) -> p c j", c=8),
                bass.AP(ohc_ap.tensor, ohc_ap.offset, [ohc_ap.ap[0], [1, 8], [0, 8]]),
                OP.mult), inc="vv")
            P.op("vector", lambda v: v.tensor_reduce(
                A_selg()[:, :].rearrange("p (j o) -> p j o", o=1),
                bass.AP(gb_ap.tensor, gb_ap.offset, [gb_ap.ap[0], [1, 8], [8, 8]]),
                AXL.X, OP.add), inc="vv")
            P.op("vector", lambda v: v.tensor_reduce(A_gm()[:, 2:3], A_selg()[:, :], AXL.X,
                                                     OP.max), inc="vv")
            P.op("vector", lambda v: v.tensor_scalar(A_gt()[:, :], A_selg()[:, :],
                                                     A_gm()[:, 2:3], None, OP.is_equal),
                 inc="vv")
            P.op("vector", lambda v: v.tensor_tensor(A_ohl()[:, :], A_gt()[:, :],
                                                     s_rev8[:, :], OP.mult), inc="vv")
            P.op("vector", lambda v: v.tensor_reduce(A_gm()[:, 3:4], A_ohl()[:, :], AXL.X,
                                                     OP.max), inc="vv")
            P.op("vector", lambda v: v.tensor_scalar(A_gt()[:, :], A_ohl()[:, :],
                                                     A_gm()[:, 3:4], None, OP.is_equal),
                 inc="vv")
            voh = P.op("vector", lambda v: v.tensor_tensor(
                A_oht()[:, :].rearrange("p (c j) -> p c j", c=8),
                bass.AP(ohc_ap.tensor, ohc_ap.offset, [ohc_ap.ap[0], [1, 8], [0, 8]]),
                bass.AP(gt_ap.tensor, gt_ap.offset, [gt_ap.ap[0], [0, 8], [1, 8]]),
                OP.mult), inc="vv")
            mo = P.op("tensor", lambda t_: t_.transpose(PB(4, 128, 128, 64), A_oht()[:, :],
                                                        s_ident[:, :]),
                      waits=[("vv", voh)] + BK.writer_waits([4]), inc="mm")
            vof = P.op("vector", lambda v, ho=ho: v.tensor_copy(A_ohf()[0:64, ho:ho + 128],
                                                                PB(4, 128, 128, 64)),
                       waits=[("mm", mo)], inc="vv")
            BK.consumed([4], "vv", vof)
        vohf = P.count["vv"]
        tap("sct0", lambda: bass.AP(s_big, 2560, [[4096, 128], [1, 72]]), t)
        tap("sct1", lambda: bass.AP(s_big, 2640, [[4096, 128], [1, 72]]), t)
        tap("gn0", lambda: bass.AP(s_big, 2720, [[4096, 128], [1, 72]]), t)
        tap("gn1", lambda: bass.AP(s_big, 2800, [[4096, 128], [1, 72]]), t)
        tap("ohf", lambda: bass.AP(s_big, 3176, [[4096, 64], [1, 256]]), t)

        vM = []
        for i in range(2):
            bank = 6 + i
            mi = P.op("tensor", lambda t_, i=i, bank=bank: t_.matmul(
                PB(bank, 0, 256, 64), s_hsel[0:2, i * 128:i * 128 + 64],
                A_comp()[0:2, :], start=True, stop=True),
                waits=[("ss", scp)] + BK.writer_waits([bank]), inc="mm")
            v = P.op("vector", lambda v, i=i, bank=bank: v.tensor_tensor(
                (A_M0() if i == 0 else A_M1())[:, :], A_ohf(), PB(bank, 0, 256, 64),
                OP.mult), waits=[("mm", mi), ("vv", vohf)], inc="vv")
            BK.consumed([bank], "vv", v)
            vM.append(v)
        P.op("tensor", lambda t_: t_.matmul(PB(7, 0, 256, 16), s_w1a[:, :], A_M0(),
                                            start=True, stop=False),
             waits=[("vv", vM[1])] + BK.writer_waits([7]), inc=None)
        P.op("tensor", lambda t_: t_.matmul(PB(7, 0, 256, 16), s_w1b[:, :], A_M1(),
                                            start=False, stop=False), inc=None)
        mhh = P.op("tensor", lambda t_: t_.matmul(PB(7, 0, 256, 16), s_b1t[:, :], A_ohf(),
                                                  start=False, stop=True), inc="mm")
        shh = P.op("scalar", lambda s: s.activation(A_hh()[:, :], PB(7, 0, 256, 16),
                                                    AF.Relu), waits=[("mm", mhh)], inc="ss")
        BK.consumed([7], "ss", shh)
        mw2 = P.op("tensor", lambda t_: t_.matmul(PB(6, 0, 256, 16), s_w2t[:, :], A_ohf(),
                                                  start=True, stop=True),
                   waits=BK.writer_waits([6]), inc="mm")
        vt1 = P.op("vector", lambda v: v.tensor_tensor(A_t1()[:, :], A_hh(),
                                                       PB(6, 0, 256, 16), OP.mult),
                   waits=[("mm", mw2), ("ss", shh)], inc="vv")
        BK.consumed([6], "vv", vt1)
        P.op("tensor", lambda t_: t_.matmul(PB(5, 0, 256, 1), s_ones16[:, :], A_t1(),
                                            start=True, stop=False),
             waits=[("vv", vt1)] + BK.writer_waits([5]), inc=None)
        mmg = P.op("tensor", lambda t_: t_.matmul(PB(5, 0, 256, 1), s_b2t[:, :], A_ohf(),
                                                  start=False, stop=True), inc="mm")
        vmg = P.op("vector", lambda v: v.tensor_copy(A_magr()[:, :], PB(5, 0, 256, 1)),
                   waits=[("mm", mmg)], inc="vv")
        BK.consumed([5], "vv", vmg)
        mgb = P.op("tensor", lambda t_: t_.matmul(PB(5, 0, 256, 64), s_onesr[0:1, 0:64],
                                                  A_magr(), start=True, stop=True),
                   waits=[("vv", vmg)] + BK.writer_waits([5]), inc="mm")
        vso = P.op("vector", lambda v: v.tensor_tensor(A_soh()[:, :], A_ohf(),
                                                       PB(5, 0, 256, 64), OP.mult),
                   waits=[("mm", mgb)], inc="vv")
        BK.consumed([5], "vv", vso)
        tap("soh", lambda: bass.AP(s_big, 3432, [[4096, 64], [1, 256]]), t)
        vyp = None
        for cC in range(8):
            bank = 4 + (cC % 2)
            mD = P.op("tensor", lambda t_, cC=cC, bank=bank: t_.matmul(
                PB(bank, 0, 256), s_dirss[:, cC * 128:(cC + 1) * 128], A_soh(),
                start=True, stop=True),
                waits=[("vv", vso)] + BK.writer_waits([bank]), inc="mm")
            vyp = P.op("vector", lambda v, cC=cC, bank=bank, s_xa=s_xa: v.tensor_tensor(
                s_xa[:, cC * 256:(cC + 1) * 256], PB(bank, 0, 256),
                s_h[:, cC * 256:(cC + 1) * 256], OP.add), waits=[("mm", mD)], inc="vv")
            BK.consumed([bank], "vv", vyp)
        tap("ypre", lambda: s_xa[:, :], t)

        ln_stats(s_xa, wv=vyp)
        vmr = P.op("vector", lambda v: v.tensor_tensor(s_mrb[:, :], s_mb[:, :], s_rb[:, :],
                                                       OP.mult), inc="vv")
        vt2 = ln_apply(s_xa, s_h, A_tmp3())
        P.op("vector", lambda v: v.nop(), waits=[("vv", vt2)], inc="xf")
        for cC in range(8):
            P.op("vector", lambda v, cC=cC: v.tensor_scalar(
                s_vb[:, :], s_mrb[:, :], s_gamc[:, cC:cC + 1], None, OP.mult),
                waits=[("vv", vmr)], inc="vv")
            P.op("vector", lambda v, cC=cC: v.scalar_tensor_tensor(
                s_wfin[:, :], s_x2[:, cC * 256:(cC + 1) * 256], s_betc[:, cC:cC + 1],
                s_vb[:, :], OP.add, OP.subtract), inc="vv")
            P.op("vector", lambda v, cC=cC: v.scalar_tensor_tensor(
                s_x2[:, cC * 256:(cC + 1) * 256], s_h[:, cC * 256:(cC + 1) * 256],
                s_gamc[:, cC:cC + 1], s_wfin[:, :], OP.mult, OP.add), inc="vv")
        vfin = P.count["vv"]
        tap("out", lambda: s_x2[:, :], t)

        st_count = P.op("gpsimd", lambda g, t=t: g.dma_start(
            out=bass.AP(d_out, t * TT, [[RTOK, 128], [128 * RTOK, 8], [1, TT]]),
            in_=s_x2[:, :].rearrange("p (c f) -> p c f", c=8)),
            waits=[("vv", vfin)], inc="st", inc_n=16)

    P.op("gpsimd", lambda g: None, waits=[("st", P.count["st"])])

    sems = {}
    for s in P.count:
        sems[s] = es.enter_context(nc.semaphore(s))
    with nc.Block() as block:
        @block.tensor
        def _(eng):
            P.emit("tensor", eng, sems)

        @block.vector
        def _(eng):
            P.emit("vector", eng, sems)

        @block.scalar
        def _(eng):
            P.emit("scalar", eng, sems)

        @block.gpsimd
        def _(eng):
            P.emit("gpsimd", eng, sems)

        @block.sync
        def _(eng):
            P.emit("sync", eng, sems)

    nc._keepalive = es
    return nc


def make_inmaps(inputs):
    f32 = np.float32
    x = np.asarray(inputs["L0_attn_x"], f32)
    wz = np.asarray(inputs["L0_attn_Wz"], f32)
    wzs = np.ascontiguousarray(
        wz.reshape(8, 128, 16, 128).transpose(2, 0, 1, 3).reshape(16 * 8 * 128, 128))
    win = np.asarray(inputs["L0_attn_Win"], f32)
    win_r = np.ascontiguousarray(np.concatenate(
        [win[:, 136:648], win[:, 0:64], win[:, 64:128], win[:, 128:136]], axis=1))
    sw1 = np.asarray(inputs["L0_ffn_spline_w1"], f32)
    hm = np.zeros((128, 8), f32)
    hm[:, 0::2] = 1.0
    common = {
        "wzs": wzs,
        "win": win_r,
        "wzg": np.ascontiguousarray(np.asarray(inputs["L0_attn_Wzg"], f32)),
        "wout": np.ascontiguousarray(np.asarray(inputs["L0_attn_Wout"], f32)),
        "cw1": np.ascontiguousarray(np.asarray(inputs["L0_ffn_compress_w1"], f32)),
        "cw2": np.ascontiguousarray(np.asarray(inputs["L0_ffn_compress_w2"], f32)),
        "w1a": np.ascontiguousarray(sw1[:, 0, :]),
        "w1b": np.ascontiguousarray(sw1[:, 1, :]),
        "b1t": np.ascontiguousarray(np.asarray(inputs["L0_ffn_spline_b1"], f32)),
        "w2t": np.ascontiguousarray(np.asarray(inputs["L0_ffn_spline_w2"], f32)[:, :, 0]),
        "b2t": np.ascontiguousarray(np.asarray(inputs["L0_ffn_spline_b2"], f32)),
        "dirs": np.ascontiguousarray(np.asarray(inputs["L0_ffn_directions"], f32)),
        "sigs": np.ascontiguousarray(np.asarray(inputs["L0_ffn_signatures_raw"], f32)),
        "csel": np.ascontiguousarray(
            (np.arange(64)[:, None] // 8 == np.arange(8)[None, :]).astype(f32)),
        "dtb": np.asarray(inputs["L0_attn_dt_bias"], f32).reshape(8, 1),
        "alog": np.asarray(inputs["L0_attn_A_log"], f32).reshape(8, 1),
        "drep": np.ascontiguousarray(
            np.repeat(np.asarray(inputs["L0_attn_D_param"], f32), 64).reshape(4, 128).T),
        "gamc": np.ascontiguousarray(
            np.asarray(inputs["L0_ffn_norm_gamma"], f32).reshape(8, 128).T),
        "betc": np.ascontiguousarray(
            np.asarray(inputs["L0_ffn_norm_beta"], f32).reshape(8, 128).T),
        "cb1": np.ascontiguousarray(
            np.asarray(inputs["L0_ffn_compress_b1"], f32).reshape(2, 128).T),
        "cb2": np.asarray(inputs["L0_ffn_compress_b2"], f32).reshape(2, 1),
        "temp": np.asarray(inputs["L0_ffn_temperature"], f32).reshape(1, 1),
        "oscale": np.asarray(inputs["L0_ffn_output_scale"], f32).reshape(1, 1),
        "ident": np.eye(128, dtype=f32),
        "rev8": np.tile(8.0 - np.arange(8, dtype=f32), (128, 1)),
        "u01": (np.arange(128)[None, :] >= np.arange(128)[:, None]).astype(f32),
        "mbias": np.where(np.arange(128)[None, :] >= np.arange(128)[:, None],
                          0.0, MASKNEG).astype(f32),
        "onesr": np.ones((1, 256), f32),
        "ones8r": np.ones((8, 128), f32),
        "onespad": np.concatenate([np.ones((128, 1), f32), np.zeros((128, 127), f32)], 1),
        "ones16": np.ones((16, 1), f32),
        "cc": np.tile(np.array([[1.0 / 1024, EPS, -1.0, 0.0, 1.0, 1e-6, 0.3, -0.3,
                                 30.0, -0.5, 0.5, 0.0]], f32), (128, 1)),
        "hmE": hm,
        "hmO": (1.0 - hm).astype(f32),
        "hsel": np.ascontiguousarray(np.repeat(np.eye(8, dtype=f32), 128, axis=1)),
    }
    in_maps = []
    for i in range(NCORES):
        m = dict(common)
        m["x"] = np.ascontiguousarray(x[BPC * i:BPC * (i + 1)].reshape(RTOK, D).T)
        in_maps.append(m)
    return in_maps


_NC_CACHE = {}


def kernel(**inputs):
    if "full" not in _NC_CACHE:
        _NC_CACHE["full"] = build_program()
    nc = _NC_CACHE["full"]
    in_maps = make_inmaps(inputs)
    res = run_bass_kernel_spmd(nc, in_maps, core_ids=list(range(NCORES)))
    outs = []
    for i in range(NCORES):
        o = np.asarray(res.results[i]["out"])
        outs.append(o.T.reshape(BPC, SEQ, D))
    return np.concatenate(outs, axis=0)
